# version 18
# baseline (speedup 1.0000x reference)
import numpy as np
from contextlib import ExitStack

import concourse.bacc as bacc
import concourse.mybir as mybir
from concourse.bass import AP

F32 = mybir.dt.float32
BF16 = mybir.dt.bfloat16
I16 = mybir.dt.int16
U32 = mybir.dt.uint32
AF = mybir.ActivationFunctionType
ALU = mybir.AluOpType
AXX = mybir.AxisListType.X

NCORE = 8
TPC = 1024
NH = 16
D = 128
E = 2048
INTER = 8192
NEXP = 4096
EPS = 1e-5
RG = [list(range(NCORE))]

WNAMES = [("wqkvT", E, 3 * E), ("woutT", E, E), ("gate", E, INTER),
          ("up", E, INTER), ("down", INTER, E), ("de", NEXP, E), ("ue", NEXP, E)]


class Sem:
    def __init__(self, h):
        self.h = h
        self.n = 0

    def inc(self, inst, k):
        inst.then_inc(self.h, k)
        self.n += k
        return inst


def build(cfg=None):
    cfg = cfg or {}
    upto = cfg.get("upto", "full")

    nc = bacc.Bacc("TRN2", target_bir_lowering=False, num_swdge_queues=2)

    xT = nc.dram_tensor("xT", [E, TPC], F32, kind="ExternalInput").ap()
    cs = nc.dram_tensor("cs", [128, 2048], F32, kind="ExternalInput").ap()
    cf = nc.dram_tensor("cf", [128, 1024], F32, kind="ExternalInput").ap()
    cbi = nc.dram_tensor("cbi", [128, 132], BF16, kind="ExternalInput").ap()
    wqf = nc.dram_tensor("wqf", [E, 128], F32, kind="ExternalInput").ap()
    keysf = nc.dram_tensor("keysf", [64, 64], F32, kind="ExternalInput").ap()
    pself = nc.dram_tensor("pself", [128, 8], F32, kind="ExternalInput").ap()
    rtself = nc.dram_tensor("rtself", [128, 16], F32, kind="ExternalInput").ap()
    sh = {}
    for nm, rows, cols in WNAMES:
        sh[nm] = nc.dram_tensor("sh_" + nm, [rows // 8, cols], F32,
                                kind="ExternalInput").ap()

    out = nc.dram_tensor("out", [TPC, E], F32, kind="ExternalOutput").ap()
    dbg = {}
    if upto in ("p1", "p2"):
        dbg["qT"] = nc.dram_tensor("dbg_qT", [128, NH * TPC], BF16, kind="ExternalOutput").ap()
        dbg["ktok"] = nc.dram_tensor("dbg_ktok", [128, 8 * NH * 128], BF16, kind="ExternalOutput").ap()
        dbg["v"] = nc.dram_tensor("dbg_v", [128, 8 * E], BF16, kind="ExternalOutput").ap()
        dbg["T"] = nc.dram_tensor("dbg_T", [8, 128, NH * 129], F32, kind="ExternalOutput").ap()
        dbg["st"] = nc.dram_tensor("dbg_st", [128, 64], F32, kind="ExternalOutput").ap()
    if upto == "p2":
        dbg["x2"] = nc.dram_tensor("dbg_x2", [E, TPC], F32, kind="ExternalOutput").ap()
        dbg["oh"] = nc.dram_tensor("dbg_oh", [128, 8192], BF16, kind="ExternalOutput").ap()
        dbg["h2"] = nc.dram_tensor("dbg_h2", [128, 16384], BF16, kind="ExternalOutput").ap()

    agi = {}
    ago = {}
    AGNAMES = [("wv", E, E), ("wqk", E, 2 * E)] + WNAMES[1:]
    for nm, rows, cols in AGNAMES:
        agi[nm] = nc.dram_tensor("agi_" + nm, [rows // 8, cols], BF16, kind="Internal").ap()
        ago[nm] = nc.dram_tensor("ago_" + nm, [rows, cols], BF16, kind="Internal",
                                 addr_space="Shared").ap()
    agi_T = [nc.dram_tensor("agi_T%d" % i, [128, 8 * 129], F32, kind="Internal").ap()
             for i in range(2)]
    ago_T = [nc.dram_tensor("ago_T%d" % i, [8, 128, 8 * 129], F32, kind="Internal",
                            addr_space="Shared").ap() for i in range(2)]
    agi_rt = nc.dram_tensor("agi_rt", [128, 256], F32, kind="Internal").ap()
    ago_rt = nc.dram_tensor("ago_rt", [8, 128, 256], F32, kind="Internal",
                            addr_space="Shared").ap()
    x2d = nc.dram_tensor("x2d", [E, TPC], F32, kind="Internal").ap()
    eidxd = nc.dram_tensor("eidxd", [8, 1024], I16, kind="Internal").ap()

    ctx = ExitStack()

    def sb(name, shape, dt):
        return ctx.enter_context(nc.sbuf_tensor(name, shape, dt))

    AR1 = sb("AR1", [128, 32768], BF16)
    AR2 = sb("AR2", [128, 16384], BF16)
    AR4 = sb("AR4", [128, 8192], BF16)
    AR5 = sb("AR5", [128, 16384], BF16)
    WS = sb("WS", [128, 12288], BF16)
    CSB = sb("CSB", [128, 2048], F32)
    CF = sb("CF", [128, 1024], F32)
    CB = sb("CB", [128, 132], BF16)
    STT = sb("STT", [128, 258], F32)
    SST = sb("SST", [128, 2064], F32)
    SBF = sb("SBF", [128, 258], BF16)
    STAT = sb("STAT", [128, 64], F32)
    RBC = sb("RBC", [128, 1024], F32)
    QRAW = sb("QRAW", [128, 1024], F32)
    RTMP = sb("RTMP", [128, 1280], F32)
    KTS = sb("KTS", [128, 1024], BF16)
    ATT = sb("ATT", [128, 512], BF16)
    RBS = sb("RBS", [128, 260], F32)
    RTI = sb("RTI", [128, 8], U32)
    IDX = sb("IDX", [128, 128], I16)
    PSEL = sb("PSEL", [128, 24], F32)
    OST = CSB[:, 0:1024]
    XST = CSB[:, 1024:2048]
    RTB = QRAW[:, 0:512]
    RT2 = QRAW[:, 512:832]
    EWS = QRAW[:, 832:928]
    PRJ = None

    PB = [ctx.enter_context(nc.psum_tensor("pb%d" % i, [128, 512], F32))
          for i in range(8)]

    semnames = ["vself", "sself", "gx", "gld", "gst", "idxz", "cc1",
                "amm", "aev", "ascl", "arow", "abc", "actr",
                "bw0", "bw1", "bmm", "bev",
                "cw0", "cw1", "cmm", "crot", "cse", "csx", "cvm0", "cvf",
                "ckt", "ckte", "cTg", "cTe", "cTd0", "cTd1",
                "misc"]
    S = {nm: Sem(ctx.enter_context(nc.semaphore(nm))) for nm in semnames}
    if cfg.get("upto") != "full":
        S["dbgs"] = Sem(ctx.enter_context(nc.semaphore("dbgs")))
    VS, SS = S["vself"], S["sself"]

    class Chain:
        def __init__(self):
            self.last = None

        def pre(self, e):
            if self.last:
                e.wait_ge(self.last[0], self.last[1])

        def tag(self, sem, inst, k=1):
            sem.inc(inst, k)
            self.last = (sem.h, sem.n)
            return inst

    VC = Chain()
    SC = Chain()

    IDENT_F = CF[:, 0:128]
    PMAT = CF[:, 128:256]
    TRIU = [CF[:, 256:512], CF[:, 512:768]]
    ONES_F = CF[:, 768:896]
    C8 = CF[:, 896:904]
    C1 = CF[:, 904:912]
    IDENT_B = CB[:, 0:128]
    ONESB = CB[:, 128:129]

    qTs = AR1[:, 0:16384].rearrange("p (h t) -> p h t", h=NH)
    ktok = AR1[:, 16384:32768].rearrange("p (m h d) -> p m h d", m=8, h=NH)
    xc = AR2[:, :].rearrange("p (tt c t) -> p tt c t", tt=2, c=16)
    vtok = AR5[:, :].rearrange("p (m e) -> p m e", m=8)
    outh = AR4[:, :].rearrange("p (h l) -> p h l", h=NH)

    with nc.Block() as block:
        def p0_sync(sync):
            S["misc"].inc(sync.dma_start(out=CSB[:, :], in_=cs), 16)
            S["misc"].inc(sync.dma_start(out=CF[:, :], in_=cf), 16)
            S["misc"].inc(sync.dma_start(out=CB[:, :], in_=cbi), 16)
            S["misc"].inc(sync.dma_start(out=PSEL[:, 0:8], in_=pself), 16)
            S["misc"].inc(sync.dma_start(out=PSEL[:, 8:24], in_=rtself), 16)
        with nc.named_scope("p0"):
            block.sync(p0_sync)

        def p0_gpsimd(g):
            S["idxz"].inc(g.memset(IDX[:, :], 0), 1)
            for tt in range(2):
                xv = xT.rearrange("(c p) t -> p c t", p=128)[:, :, tt * 512:(tt + 1) * 512]
                S["gx"].inc(g.dma_start(out=xc[:, tt], in_=xv), 16)
            par = 0
            CONV = [("wv", "wqkvT", 2 * E, E, E), ("wqk", "wqkvT", 0, E, 2 * E),
                    ("woutT", "woutT", 0, E, E), ("gate", "gate", 0, E, INTER),
                    ("up", "up", 0, E, INTER), ("down", "down", 0, INTER, E),
                    ("de", "de", 0, NEXP, E), ("ue", "ue", 0, NEXP, E)]
            for nm, snm, c0, rows, cols in CONV:
                nrp = rows // 8 // 128
                src_v = sh[snm].rearrange("(a p) c -> a p c", p=128)
                dst_v = agi[nm].rearrange("(a p) c -> a p c", p=128)
                for a in range(nrp):
                    pos = 0
                    while pos < cols:
                        take = min(4096, cols - pos)
                        if par >= 2:
                            g.wait_ge(S["gst"].h, par * 16)
                        S["gld"].inc(g.dma_start(
                            out=AR4[:, (par % 2) * 4096:(par % 2) * 4096 + take],
                            in_=src_v[a, :, c0 + pos:c0 + pos + take]), 16)
                        g.wait_ge(S["gld"].h, (par + 1) * 16)
                        S["gst"].inc(g.dma_start(
                            out=dst_v[a, :, pos:pos + take],
                            in_=AR4[:, (par % 2) * 4096:(par % 2) * 4096 + take]), 16)
                        par += 1
                        pos += take
                if nm in ("wv", "wqk", "woutT"):
                    g.wait_ge(S["gst"].h, S["gst"].n)
                    S["cc1"].inc(g.collective_compute(
                        "AllGather", ALU.bypass, replica_groups=RG,
                        ins=[agi[nm].opt()], outs=[ago[nm].opt()]), 1)
            g.wait_ge(S["gst"].h, S["gst"].n)
            for nm in ["gate", "up", "down", "de", "ue"]:
                S["cc1"].inc(g.collective_compute(
                    "AllGather", ALU.bypass, replica_groups=RG,
                    ins=[agi[nm].opt()], outs=[ago[nm].opt()]), 1)
        with nc.named_scope("p0conv"):
            block.gpsimd(p0_gpsimd)

        def p1a_tensor(t):
            t.wait_ge(S["gx"].h, 32)
            t.wait_ge(S["misc"].h, 80)
            for blk in range(8):
                tt, tb = blk // 4, blk % 4
                if blk >= 1:
                    t.wait_ge(S["aev"].h, blk)
                for c in range(16):
                    lhs = xc[:, tt, c, tb * 128:tb * 128 + 128]
                    t.matmul(PB[6][:, 0:128], lhs, lhs,
                             start=(c == 0), stop=(c == 15))
                for c in range(16):
                    mm = t.matmul(PB[6][:, 128:129],
                                  xc[:, tt, c, tb * 128:tb * 128 + 128],
                                  ONESB, start=False, stop=(c == 15),
                                  skip_group_check=True)
                S["amm"].inc(mm, 1)
        with nc.named_scope("p1a"):
            block.tensor(p1a_tensor)

        def p1a_vector(v):
            for blk in range(8):
                v.wait_ge(S["amm"].h, blk + 1)
                VC.pre(v)
                VC.tag(VS, v.tensor_mul(QRAW[:, 0:128], PB[6][:, 0:128], IDENT_F))
                VC.pre(v)
                VC.tag(VS, v.reduce_sum(STAT[:, blk:blk + 1], QRAW[:, 0:128], axis=AXX))
                VC.pre(v)
                VC.tag(S["aev"], v.tensor_copy(STAT[:, 8 + blk:9 + blk],
                                               PB[6][:, 128:129]))
            VC.pre(v)
            VC.tag(VS, v.tensor_scalar_mul(STAT[:, 16:24], STAT[:, 8:16], 1.0 / E))
            VC.pre(v)
            VC.tag(VS, v.tensor_mul(STAT[:, 40:48], STAT[:, 16:24], STAT[:, 16:24]))
            VC.pre(v)
            VC.tag(VS, v.scalar_tensor_tensor(STAT[:, 48:56], STAT[:, 0:8], 1.0 / E,
                                              STAT[:, 40:48], op0=ALU.mult,
                                              op1=ALU.subtract))
            VC.pre(v)
            VC.tag(S["ascl"], v.tensor_scalar_add(STAT[:, 48:56], STAT[:, 48:56], EPS))
        block.vector(p1a_vector)

        def p1a_scalar(s):
            s.wait_ge(S["ascl"].h, 1)
            SC.pre(s)
            SC.tag(S["ascl"], s.activation(STAT[:, 56:64], STAT[:, 48:56], AF.Sqrt))
        block.scalar(p1a_scalar)

        def p1a_vector2(v):
            v.wait_ge(S["ascl"].h, 2)
            VC.pre(v)
            VC.tag(VS, v.reciprocal(STAT[:, 40:48], STAT[:, 56:64]))
            VC.pre(v)
            VC.tag(VS, v.tensor_mul(STAT[:, 40:48], STAT[:, 40:48], STAT[:, 48:56]))
            VC.pre(v)
            VC.tag(VS, v.tensor_add(STAT[:, 40:48], STAT[:, 40:48], STAT[:, 56:64]))
            VC.pre(v)
            VC.tag(VS, v.tensor_scalar_mul(STAT[:, 40:48], STAT[:, 40:48], 0.5))
            VC.pre(v)
            VC.tag(VS, v.reciprocal(STAT[:, 24:32], STAT[:, 40:48]))
            VC.pre(v)
            VC.tag(S["ascl"], v.tensor_scalar_mul(STAT[:, 32:40], STAT[:, 16:24],
                                                  -1.0))
        block.vector(p1a_vector2)

        def p1a_sync2(sync):
            sync.wait_ge(S["ascl"].h, 3)
            for b in range(8):
                S["arow"].inc(sync.dma_start(
                    out=AP(QRAW, b * 128, [[1024, 1], [1, 128]]),
                    in_=STAT[:, 32 + b:33 + b]), 16)
                S["arow"].inc(sync.dma_start(
                    out=AP(RTMP, b * 128, [[1280, 1], [1, 128]]),
                    in_=STAT[:, 24 + b:25 + b]), 16)
        block.sync(p1a_sync2)

        def p1a_tensor2(t):
            t.wait_ge(S["arow"].h, 256)
            t.wait_ge(S["aev"].h, 8)
            for tt in range(2):
                t.matmul(PB[7 - tt][:, 0:512], ONES_F[0:1, 0:128],
                         AP(QRAW, tt * 512, [[1024, 1], [1, 512]]),
                         start=True, stop=True)
                S["abc"].inc(t.matmul(PB[5 - tt][:, 0:512], ONES_F[0:1, 0:128],
                                      AP(RTMP, tt * 512, [[1280, 1], [1, 512]]),
                                      start=True, stop=True), 1)
        block.tensor(p1a_tensor2)

        def p1a_scalar2(s):
            for tt in range(2):
                s.wait_ge(S["abc"].h, tt + 1)
                SC.pre(s)
                SC.tag(SS, s.activation(KTS[:, tt * 512:tt * 512 + 512],
                                        PB[7 - tt][:, 0:512], AF.Copy))
                SC.pre(s)
                SC.tag(S["ascl"], s.activation(RBC[:, tt * 512:tt * 512 + 512],
                                               PB[5 - tt][:, 0:512], AF.Copy))
        block.scalar(p1a_scalar2)

        def p1a_vector3(v):
            v.wait_ge(S["ascl"].h, 5)
            for tt in range(2):
                for c in range(16):
                    VC.pre(v)
                    last = (tt == 1 and c == 15)
                    VC.tag(S["actr"] if last else VS,
                           v.tensor_add(xc[:, tt, c], xc[:, tt, c],
                                        KTS[:, tt * 512:tt * 512 + 512]))
        block.vector(p1a_vector3)

        wv16v = ago["wv"].rearrange("(c p) j -> p c j", p=128)
        wqk16v = ago["wqk"].rearrange("(c p) j -> p c j", p=128)

        def p1b_sync(sync):
            sync.wait_ge(S["cc1"].h, 1)
            for eg in range(8):
                if eg >= 2:
                    sync.wait_ge(S["bev"].h, (eg - 1) * 8)
                wv = wv16v[:, :, eg * 256:(eg + 1) * 256]
                S["bw%d" % (eg % 2)].inc(sync.dma_start(
                    out=WS[:, (eg % 2) * 4096:(eg % 2) * 4096 + 4096], in_=wv), 16)
        with nc.named_scope("p1b"):
            block.sync(p1b_sync)

        def p1b_tensor(t):
            t.wait_ge(S["actr"].h, 1)
            for eg in range(8):
                t.wait_ge(S["bw%d" % (eg % 2)].h, (eg // 2 + 1) * 16)
                for mb in range(8):
                    it = eg * 8 + mb
                    if it >= 2:
                        t.wait_ge(S["bev"].h, it - 1)
                    tt, tb = mb // 4, mb % 4
                    for c in range(16):
                        mm = t.matmul(PB[it % 2][:, 0:256],
                                      xc[:, tt, c, tb * 128:tb * 128 + 128],
                                      WS[:, (eg % 2) * 4096 + c * 256:
                                         (eg % 2) * 4096 + (c + 1) * 256],
                                      start=(c == 0), stop=(c == 15))
                    S["bmm"].inc(mm, 1)
        with nc.named_scope("p1b"):
            block.tensor(p1b_tensor)

        def p1b_scalar(s):
            for eg in range(8):
                for mb in range(8):
                    it = eg * 8 + mb
                    s.wait_ge(S["bmm"].h, it + 1)
                    SC.pre(s)
                    SC.tag(S["bev"], s.activation(
                        vtok[:, mb, eg * 256:(eg + 1) * 256], PB[it % 2][:, 0:256],
                        AF.Copy, scale=STAT[:, 24 + mb:25 + mb]))
        with nc.named_scope("p1b"):
            block.scalar(p1b_scalar)

        def dec(u):
            return u // 4, (u // 2) % 2, u % 2

        def p1c_sync(sync):
            sync.wait_ge(S["cc1"].h, 2)
            for h in range(NH):
                if h < 2:
                    sync.wait_ge(S["bmm"].h, 64)
                else:
                    sync.wait_ge(S["cmm"].h, (h - 1) * 4)
                for kk in range(2):
                    wv = wqk16v[:, :, kk * E + h * 128:kk * E + (h + 1) * 128]
                    S["cw%d" % (h % 2)].inc(sync.dma_start(
                        out=WS[:, (h % 2) * 4096 + kk * 2048:
                               (h % 2) * 4096 + (kk + 1) * 2048], in_=wv), 16)
        with nc.named_scope("p1c"):
            block.sync(p1c_sync)

        def rslot(u):
            tn = QRAW if u % 4 < 2 else RBC
            return tn[:, (u % 2) * 512:(u % 2) * 512 + 512]
        CSRB = S["arow"].n

        def p1c_tensor(t):
            t.wait_ge(S["actr"].h, 1)
            t.wait_ge(S["bev"].h, 64)
            for u in range(64 + 4):
                if 0 <= u - 3 < 64:
                    uT = u - 3
                    hT, kkT, ttT = dec(uT)
                    if kkT == 1:
                        ksT = 2 * hT + ttT
                        t.wait_ge(S["ckte"].h, ksT + 1)
                        if hT == 0 and ttT == 0:
                            t.wait_ge(S["bev"].h, 64)
                        if ttT == 0 and hT >= 1:
                            t.wait_ge(S["cTe"].h, hT)
                        for i in range(4):
                            mb = ttT * 4 + i
                            first = (ttT == 0 and i == 0)
                            t.matmul(PB[6][:, 0:128], ktok[:, mb, hT],
                                     vtok[:, mb, hT * 128:(hT + 1) * 128],
                                     start=first, stop=False,
                                     skip_group_check=True)
                            mm = t.matmul(PB[6][:, 128:129], ktok[:, mb, hT],
                                          ONESB, start=False,
                                          stop=(ttT == 1 and i == 3),
                                          skip_group_check=True)
                        S["cTg"].inc(mm, 1)
                if 0 <= u - 2 < 64:
                    uk = u - 2
                    h2_, kk2, tt2 = dec(uk)
                    if kk2 == 1:
                        ks = 2 * h2_ + tt2
                        t.wait_ge(S["cvf"].h, uk + 1)
                        if ks >= 1:
                            t.wait_ge(S["ckte"].h, ks)
                        for i in range(4):
                            mm = t.matmul(PB[7][:, i * 128:(i + 1) * 128],
                                          KTS[:, (ks % 2) * 512 + i * 128:
                                              (ks % 2) * 512 + (i + 1) * 128],
                                          IDENT_B, start=(i == 0), stop=(i == 3))
                        S["ckt"].inc(mm, 1)
                if u < 64:
                    h, kk, tt = dec(u)
                    if kk == 0 and tt == 0:
                        t.wait_ge(S["cw%d" % (h % 2)].h, (h // 2 + 1) * 32)
                    if u >= 4:
                        t.wait_ge(S["cse"].h, u - 3)
                    base = (h % 2) * 4096 + kk * 2048
                    for c in range(16):
                        mm = t.matmul(PB[u % 4][:, 0:512],
                                      WS[:, base + c * 128:base + (c + 1) * 128],
                                      xc[:, tt, c],
                                      start=(c == 0), stop=(c == 15))
                    S["cmm"].inc(mm, 1)
                if 0 <= u - 1 < 64:
                    up = u - 1
                    t.wait_ge(S["cse"].h, up + 1)
                    if up >= 2:
                        t.wait_ge(S["cvm0"].h, up - 1)
                    S["crot"].inc(t.matmul(
                        PB[4 + up % 2][:, 0:512], PMAT,
                        rslot(up),
                        start=True, stop=True), 1)
        with nc.named_scope("p1c"):
            block.tensor(p1c_tensor)

        def p1c_scalar(s):
            for u in range(64 + 1):
                if u < 64:
                    h, kk, tt = dec(u)
                    s.wait_ge(S["cmm"].h, u + 1)
                    if u in (2, 3):
                        s.wait_ge(S["arow"].h, CSRB + 1)
                    if u >= 4:
                        s.wait_ge(S["cvm0"].h, u - 3)
                    SC.pre(s)
                    SC.tag(S["cse"], s.activation(
                        rslot(u), PB[u % 4][:, 0:512], AF.Copy))
                if u >= 1:
                    up = u - 1
                    s.wait_ge(S["cvm0"].h, up + 1)
                    SC.pre(s)
                    SC.tag(S["csx"], s.activation(
                        RTMP[:, 1024:1280].bitcast(BF16),
                        rslot(up), AF.Exp))
                    hp, kkp, ttp = dec(up)
                    if kkp == 1:
                        ks = 2 * hp + ttp
                        s.wait_ge(S["ckt"].h, ks + 1)
                        for i in range(4):
                            SC.pre(s)
                            SC.tag(S["ckte"] if i == 3 else SS,
                                   s.activation(ktok[:, ttp * 4 + i, hp],
                                                PB[7][:, i * 128:(i + 1) * 128],
                                                AF.Copy))
                    if kkp == 1 and ttp == 1:
                        s.wait_ge(S["cTg"].h, 2 * hp + 2)
                        if hp >= 2:
                            s.wait_ge(S["cTd%d" % (hp % 2)].h, (hp // 2) * 16)
                        SC.pre(s)
                        SC.tag(S["cTe"], s.activation(
                            STT[:, (hp % 2) * 129:(hp % 2) * 129 + 129],
                            PB[6][:, 0:129], AF.Copy))
                        s.wait_ge(S["cTe"].h, S["cTe"].n)
                        S["cTd%d" % (hp % 2)].inc(s.dma_start(
                            out=agi_T[hp // 8][:, (hp % 8) * 129:(hp % 8 + 1) * 129],
                            in_=STT[:, (hp % 2) * 129:(hp % 2) * 129 + 129]), 16)
        with nc.named_scope("p1c"):
            block.scalar(p1c_scalar)

        def p1c_vector(v):
            v.wait_ge(S["misc"].h, 80)
            v.wait_ge(S["ascl"].h, 5)
            for i in range(4):
                VC.pre(v)
                VC.tag(S["arow"] if i == 3 else VS,
                       v.tensor_mul(CSB[:, i * 512:(i + 1) * 512],
                                    CSB[:, i * 512:(i + 1) * 512],
                                    RBC[:, (i % 2) * 512:(i % 2) * 512 + 512]))
            for u in range(64):
                h, kk, tt = dec(u)
                v.wait_ge(S["crot"].h, u + 1)
                A = RTMP[:, 0:512]
                B = RTMP[:, 512:1024]
                VC.pre(v)
                VC.tag(VS, v.tensor_mul(A, rslot(u),
                                        CSB[:, tt * 512:(tt + 1) * 512]))
                VC.pre(v)
                VC.tag(VS, v.tensor_mul(B, PB[4 + u % 2][:, 0:512],
                                        CSB[:, 1024 + tt * 512:1024 + (tt + 1) * 512]))
                VC.pre(v)
                VC.tag(VS, v.tensor_add(B, A, B))
                VC.pre(v)
                VC.tag(S["cvm0"], v.tensor_scalar_min(rslot(u), B, 0.0))
                VC.pre(v)
                VC.tag(VS, v.tensor_scalar_max(A, B, 0.0))
                v.wait_ge(S["csx"].h, u + 1)
                VC.pre(v)
                if kk == 0:
                    dst = qTs[:, h, tt * 512:(tt + 1) * 512]
                else:
                    ks = 2 * h + tt
                    dst = KTS[:, (ks % 2) * 512:(ks % 2 + 1) * 512]
                VC.tag(S["cvf"], v.tensor_add(dst, A, RTMP[:, 1024:1280].bitcast(BF16)))
        with nc.named_scope("p1c"):
            block.vector(p1c_vector)

        def p1d_gpsimd(g):
            for i in range(2):
                g.wait_ge(S["cTd0"].h, (i + 1) * 64)
                g.wait_ge(S["cTd1"].h, (i + 1) * 64)
                S["cc1"].inc(g.collective_compute(
                    "AllGather", ALU.bypass, replica_groups=RG,
                    ins=[agi_T[i].opt()], outs=[ago_T[i].opt()]), 1)
        with nc.named_scope("p1d"):
            block.gpsimd(p1d_gpsimd)

        if upto == "p1":
            def dbg_sync(sync):
                sync.wait_ge(S["cc1"].h, 10)
                sync.wait_ge(S["cvf"].h, 64)
                S["dbgs"].inc(sync.dma_start(out=dbg["qT"], in_=AR1[:, 0:16384]), 16)
                S["dbgs"].inc(sync.dma_start(out=dbg["ktok"], in_=AR1[:, 16384:32768]), 16)
                S["dbgs"].inc(sync.dma_start(out=dbg["v"], in_=AR5[:, :]), 16)
                for i in range(2):
                    S["dbgs"].inc(sync.dma_start(
                        out=dbg["T"].rearrange("r p (i c) -> i r p c", i=2)[i],
                        in_=ago_T[i]), 16)
                S["dbgs"].inc(sync.dma_start(out=dbg["st"], in_=STAT[:, :]), 16)
                S["dbgs"].inc(sync.dma_start(
                    out=out[0:128, 0:1024], in_=QRAW[:, 0:1024]), 16)
                sync.wait_ge(S["dbgs"].h, 96)
            block.sync(dbg_sync)
            ctx.close()
            return nc

        for nm in ["e2w", "e2s", "wo0", "wo1", "gau0", "gau1",
                   "d2kt", "d2at", "d2dn", "d2rb", "d2nm", "d2tu", "d2sb",
                   "x2s0", "x2s1",
                   "d2ke", "d2am", "d2dr", "d2re", "d2oh", "d2su",
                   "womm", "woev",
                   "l2mm", "l2ev", "l2sc", "l2row", "l2bc", "l2h2",
                   "rw", "rmm", "rev", "rpk", "rcl", "rsh",
                   "gat0", "gat1", "emm",
                   "mg0", "mg1", "mu0", "mu1", "md0", "md1",
                   "mmm", "msil", "mev", "x2l0", "x2l1"]:
            S[nm] = Sem(ctx.enter_context(nc.semaphore(nm)))

        pselsb = PSEL[:, 0:8]
        rtselsb = PSEL[:, 8:24]
        SGS = SST[:, 0:2048]
        SGZ = SST[:, 2048:2064]
        x2bf = AR5[:, 0:8192]
        exp_tok = AR5[:, 8192:16384]
        proj = AR5[:, 0:2048].bitcast(F32)
        deT = [AR4[:, 0:2048], AR4[:, 2048:4096]]
        ueg = [AR4[:, 4096:6144], AR4[:, 6144:8192]]
        h2v = AR2[:, :].rearrange("p (c t) -> p c t", c=16)

        ar4f = AR4[:, :].bitcast(F32)
        wsf = WS[:, 8192:12288].bitcast(F32)

        def p2i_bufs(half, sl):
            if half == 0:
                return (ar4f[:, sl * 1024:sl * 1024 + 1024],
                        ar4f[:, 2048 + sl * 8:2048 + sl * 8 + 8])
            return (wsf[:, sl * 1024:sl * 1024 + 1024],
                    STT[:, sl * 8:sl * 8 + 8])

        S["e2w0"], S["e2w1"] = S["gx"], S["gld"]
        E2WB = [S["e2w0"].n, S["e2w1"].n]

        def p2i_gpsimd(g):
            for half in range(2):
                g.wait_ge(S["cc1"].h, 9 + half)
                for r in range(8):
                    idx = half * 8 + r
                    p = idx % 2
                    if idx >= 2:
                        g.wait_ge(S["e2s"].h, idx - 1)
                        g.wait_ge(S["e2w%d" % p].h, E2WB[p] + (idx // 2) * 32)
                    sbuf, zbuf = p2i_bufs(half, p)
                    S["e2w%d" % p].inc(g.dma_start(
                        out=sbuf,
                        in_=AP(ago_T[half].tensor, r * 128 * 1032,
                               [[1032, 128], [129, 8], [1, 128]])), 16)
                    S["e2w%d" % p].inc(g.dma_start(
                        out=zbuf,
                        in_=AP(ago_T[half].tensor, r * 128 * 1032 + 128,
                               [[1032, 128], [129, 8]]),
                        allow_slow_non_contiguous=True), 16)
        with nc.named_scope("p2i"):
            block.gpsimd(p2i_gpsimd)

        def p2i_vec_half(v, half):
            if True:
                for r in range(8):
                    idx = half * 8 + r
                    p = idx % 2
                    sbuf, zbuf = p2i_bufs(half, p)
                    v.wait_ge(S["e2w%d" % p].h, E2WB[p] + (idx // 2 + 1) * 32)
                    sgs_h = SGS[:, half * 1024:half * 1024 + 1024]
                    sgz_h = SGZ[:, half * 8:half * 8 + 8]
                    VC.pre(v)
                    if r == 0:
                        VC.tag(VS, v.tensor_scalar_mul(sgs_h, sbuf,
                                                       pselsb[:, 0:1]))
                        VC.pre(v)
                        VC.tag(S["e2s"], v.tensor_scalar_mul(sgz_h, zbuf,
                                                             pselsb[:, 0:1]))
                    else:
                        VC.tag(VS, v.scalar_tensor_tensor(
                            sgs_h, sbuf, pselsb[:, r:r + 1], sgs_h,
                            op0=ALU.mult, op1=ALU.add))
                        VC.pre(v)
                        VC.tag(S["e2s"], v.scalar_tensor_tensor(
                            sgz_h, zbuf, pselsb[:, r:r + 1], sgz_h,
                            op0=ALU.mult, op1=ALU.add))
        def p2i_vector(v):
            p2i_vec_half(v, 0)
        with nc.named_scope("p2i"):
            block.vector(p2i_vector)

        TRIUB = TRIU

        def emit_p2():
            nsc = [0]

            def scan_tensor(t, c, h):
                u2 = nsc[0]
                if u2 == 0:
                    t.wait_ge(S["e2s"].h, 8)
                    t.wait_ge(S["ckte"].h, 32)
                    t.wait_ge(S["cTe"].h, 16)
                    t.wait_ge(S["cvf"].h, 64)
                def emit_kts(u2k):
                    ck, hk = u2k // 16, u2k % 16
                    if u2k >= 1:
                        t.wait_ge(S["d2ke"].h, u2k)
                    if u2k == 32:
                        t.wait_ge(S["l2ev"].h, 4)
                    for i in range(2):
                        mm = t.matmul(PB[6][:, i * 128:(i + 1) * 128],
                                      ktok[:, 2 * ck + i, hk], IDENT_B,
                                      start=(i == 0), stop=(i == 1))
                    S["d2kt"].inc(mm, 1)
                if u2 % 32 == 0:
                    emit_kts(u2)
                t.wait_ge(S["d2ke"].h, u2 + 1)
                if u2 >= 1:
                    t.wait_ge(S["d2am"].h, 2 * u2)
                for mb in range(2):
                    mm = t.matmul(PB[mb][:, 0:256], KTS[:, mb * 128:(mb + 1) * 128],
                                  qTs[:, h, c * 256:(c + 1) * 256],
                                  start=True, stop=True)
                S["d2at"].inc(mm, 1)
                if u2 + 1 < 64 and (u2 + 1) % 32 != 0:
                    emit_kts(u2 + 1)
                t.wait_ge(S["d2am"].h, 2 * u2 + 2)
                if u2 >= 1:
                    t.wait_ge(S["d2dr"].h, u2)
                t.wait_ge(S["d2sb"].h, u2 + 1)
                first = True
                for j in range(2):
                    for mb in range(2):
                        mm = t.matmul(PB[2][:, j:j + 1],
                                      ATT[:, mb * 256 + j * 128:mb * 256 + (j + 1) * 128],
                                      ONESB, start=first, stop=False,
                                      skip_group_check=True)
                        first = False
                    mm = t.matmul(PB[2][:, j:j + 1],
                                  qTs[:, h, c * 256 + j * 128:c * 256 + (j + 1) * 128],
                                  SBF[:, 256 + (u2 % 2):257 + (u2 % 2)],
                                  start=False, stop=(j == 1), skip_group_check=True)
                S["d2dn"].inc(mm, 1)
                if u2 >= 1:
                    t.wait_ge(S["d2oh"].h, u2)
                for mb in range(2):
                    t.matmul(PB[4][:, 0:256], vtok[:, 2 * c + mb, h * 128:(h + 1) * 128],
                             ATT[:, mb * 256:(mb + 1) * 256],
                             start=(mb == 0), stop=False, skip_group_check=True)
                mm = t.matmul(PB[4][:, 0:256], SBF[:, (u2 % 2) * 128:(u2 % 2 + 1) * 128],
                              qTs[:, h, c * 256:(c + 1) * 256],
                              start=False, stop=True, skip_group_check=True)
                S["d2nm"].inc(mm, 1)
                t.wait_ge(S["d2dr"].h, u2 + 1)
                if u2 >= 1:
                    t.wait_ge(S["d2re"].h, u2)
                for j in range(2):
                    mm = t.matmul(PB[3][:, j * 128:(j + 1) * 128],
                                  RBS[:, 258 + j:259 + j].to_broadcast([128, 128]),
                                  IDENT_F, start=(j == 0), stop=(j == 1))
                S["d2rb"].inc(mm, 1)
                if c < 3:
                    if u2 == 32:
                        t.wait_ge(S["woev"].h, 16)
                    if u2 >= 1:
                        t.wait_ge(S["d2su"].h, u2)
                    first = True
                    for mb in range(2):
                        t.matmul(PB[5][:, 0:128], ktok[:, 2 * c + mb, h],
                                 vtok[:, 2 * c + mb, h * 128:(h + 1) * 128],
                                 start=first, stop=False, skip_group_check=True)
                        first = False
                        mm = t.matmul(PB[5][:, 128:129], ktok[:, 2 * c + mb, h],
                                      ONESB, start=False, stop=(mb == 1),
                                      skip_group_check=True)
                    S["d2tu"].inc(mm, 1)
                nsc[0] += 1

            def scan_scalar(s, c, h):
                u2 = c * 16 + h
                s.wait_ge(S["d2kt"].h, u2 + 1)
                SC.pre(s)
                SC.tag(S["d2ke"], s.activation(KTS[:, 0:256], PB[6][:, 0:256], AF.Copy))

            def emit_sbf(v, u2s):
                hs = u2s % 16
                VC.pre(v)
                VC.tag(VS, v.tensor_copy(SBF[:, (u2s % 2) * 128:(u2s % 2 + 1) * 128],
                                         SGS[:, hs * 128:(hs + 1) * 128]))
                VC.pre(v)
                VC.tag(S["d2sb"], v.tensor_copy(
                    SBF[:, 256 + (u2s % 2):257 + (u2s % 2)], SGZ[:, hs:hs + 1]))

            def scan_vector(v, c, h):
                u2 = c * 16 + h
                if u2 == 0:
                    emit_sbf(v, 0)
                v.wait_ge(S["d2at"].h, u2 + 1)
                for mb in range(2):
                    VC.pre(v)
                    VC.tag(S["d2am"] if True else VS,
                           v.tensor_mul(ATT[:, mb * 256:(mb + 1) * 256],
                                        PB[mb][:, 0:256], TRIUB[mb]))
                v.wait_ge(S["d2dn"].h, u2 + 1)
                VC.pre(v)
                VC.tag(VS, v.tensor_scalar_max(RBS[:, 256:258], PB[2][:, 0:2], 1e-6))
                VC.pre(v)
                VC.tag(S["d2dr"], v.reciprocal(RBS[:, 258:260], RBS[:, 256:258]))
                if u2 + 1 < 64:
                    emit_sbf(v, u2 + 1)
                v.wait_ge(S["d2rb"].h, u2 + 1)
                VC.pre(v)
                VC.tag(S["d2re"], v.tensor_copy(RBS[:, 0:256], PB[3][:, 0:256]))
                v.wait_ge(S["d2nm"].h, u2 + 1)
                VC.pre(v)
                VC.tag(S["d2oh"], v.tensor_mul(
                    outh[:, h, (c % 2) * 256:(c % 2) * 256 + 256],
                    PB[4][:, 0:256], RBS[:, 0:256]))
                if c < 3:
                    v.wait_ge(S["d2tu"].h, u2 + 1)
                    VC.pre(v)
                    VC.tag(VS, v.tensor_add(SGS[:, h * 128:(h + 1) * 128],
                                            SGS[:, h * 128:(h + 1) * 128],
                                            PB[5][:, 0:128]))
                    VC.pre(v)
                    VC.tag(S["d2su"], v.tensor_add(SGZ[:, h:h + 1], SGZ[:, h:h + 1],
                                                   PB[5][:, 128:129]))

            S["xo0"], S["xo1"] = S["bw0"], S["bw1"]
            XOB = [S["xo0"].n, S["xo1"].n]

            def wout_sync(sync, tt):
                if tt == 0:
                    sync.wait_ge(S["cc1"].h, 3)
                    sync.wait_ge(S["cvf"].h, 64)
                    sync.wait_ge(S["cmm"].h, 64)
                for f in range(16):
                    g = tt * 16 + f
                    if g >= 2:
                        sync.wait_ge(S["womm"].h, g - 1)
                        sync.wait_ge(S["woev"].h, g - 1)
                    wv = ago["woutT"].rearrange("(c p) j -> p c j", p=128)[
                        :, :, f * 128:(f + 1) * 128]
                    S["wo%d" % (g % 2)].inc(sync.dma_start(
                        out=WS[:, (g % 2) * 2048:(g % 2) * 2048 + 2048], in_=wv), 16)
                    xv = xT.rearrange("(c p) t -> c p t", p=128)[
                        f, :, tt * 512:(tt + 1) * 512]
                    S["xo%d" % (g % 2)].inc(sync.dma_start(
                        out=XST[:, (g % 2) * 512:(g % 2) * 512 + 512], in_=xv), 16)

            def wout_tensor(t, tt):
                t.wait_ge(S["d2oh"].h, (tt + 1) * 32)
                for f in range(16):
                    g = tt * 16 + f
                    t.wait_ge(S["wo%d" % (g % 2)].h, (g // 2 + 1) * 16)
                    if g % 2 == 1:
                        t.wait_ge(S["d2su"].h, 32 + tt * 16)
                    if g >= 2:
                        t.wait_ge(S["woev"].h, g - 1)
                    for ec in range(16):
                        mm = t.matmul(PB[7 - 2 * (g % 2)][:, 0:512],
                                      WS[:, (g % 2) * 2048 + ec * 128:
                                         (g % 2) * 2048 + (ec + 1) * 128],
                                      outh[:, ec, :],
                                      start=(ec == 0), stop=(ec == 15))
                    S["womm"].inc(mm, 1)

            def wout_vector(v, tt):
                for f in range(16):
                    g = tt * 16 + f
                    v.wait_ge(S["womm"].h, g + 1)
                    v.wait_ge(S["xo%d" % (g % 2)].h, XOB[g % 2] + (g // 2 + 1) * 16)
                    if g >= 2:
                        v.wait_ge(S["x2s%d" % (g % 2)].h, (g // 2) * 16)
                    VC.pre(v)
                    VC.tag(VS, v.tensor_add(OST[:, (g % 2) * 512:(g % 2) * 512 + 512],
                                            PB[7 - 2 * (g % 2)][:, 0:512],
                                            XST[:, (g % 2) * 512:(g % 2) * 512 + 512]))
                    VC.pre(v)
                    VC.tag(S["woev"], v.tensor_copy(
                        x2bf[:, f * 512:(f + 1) * 512],
                        OST[:, (g % 2) * 512:(g % 2) * 512 + 512]))

            def wout_scalar(s, tt):
                for f in range(16):
                    g = tt * 16 + f
                    s.wait_ge(S["woev"].h, g + 1)
                    S["x2s%d" % (g % 2)].inc(s.dma_start(
                        out=AP(x2d.tensor, f * 128 * TPC + tt * 512,
                               [[TPC, 128], [1, 512]]),
                        in_=OST[:, (g % 2) * 512:(g % 2) * 512 + 512]), 16)

            for tt in range(2):
                for c in (2 * tt, 2 * tt + 1):
                    def mk_t(c=c):
                        def f(t):
                            for h in range(NH):
                                scan_tensor(t, c, h)
                        return f

                    def mk_s(c=c):
                        def f(s):
                            for h in range(NH):
                                scan_scalar(s, c, h)
                        return f

                    def mk_v(c=c):
                        def f(v):
                            for h in range(NH):
                                if c == 0 and h == 7:
                                    p2i_vec_half(v, 1)
                                scan_vector(v, c, h)
                        return f
                    with nc.named_scope("scan"):
                        block.tensor(mk_t())
                        block.scalar(mk_s())
                        block.vector(mk_v())
                with nc.named_scope("wout"):
                    block.sync(lambda sync, tt=tt: wout_sync(sync, tt))
                    block.tensor(lambda t, tt=tt: wout_tensor(t, tt))
                    block.vector(lambda v, tt=tt: wout_vector(v, tt))
                    block.scalar(lambda s, tt=tt: wout_scalar(s, tt))
                with nc.named_scope("ln2"):
                    emit_ln2(tt)

        def emit_ln2(tt):
            def l2_tensor(t):
                base_ev = S["l2ev"].n
                t.wait_ge(S["woev"].h, (tt + 1) * 16)
                for blk in range(4):
                    if S["l2mm"].n >= 1:
                        t.wait_ge(S["l2ev"].h, S["l2mm"].n)
                    for c in range(16):
                        lhs = x2bf[:, c * 512 + blk * 128:c * 512 + blk * 128 + 128]
                        t.matmul(PB[6][:, 0:128], lhs, lhs,
                                 start=(c == 0), stop=(c == 15))
                    for c in range(16):
                        mm = t.matmul(PB[6][:, 128:129],
                                      x2bf[:, c * 512 + blk * 128:c * 512 + blk * 128 + 128],
                                      ONESB, start=False, stop=(c == 15),
                                      skip_group_check=True)
                    S["l2mm"].inc(mm, 1)
            block.tensor(l2_tensor)

            def l2_vector(v):
                base = S["l2mm"].n - 4
                for blk in range(4):
                    v.wait_ge(S["l2mm"].h, base + blk + 1)
                    VC.pre(v)
                    VC.tag(VS, v.tensor_mul(RT2[:, 192:320], PB[6][:, 0:128], IDENT_F))
                    VC.pre(v)
                    VC.tag(VS, v.reduce_sum(STAT[:, blk:blk + 1], RT2[:, 192:320],
                                            axis=AXX))
                    VC.pre(v)
                    VC.tag(S["l2ev"], v.tensor_copy(STAT[:, 8 + blk:9 + blk],
                                                    PB[6][:, 128:129]))
                VC.pre(v)
                VC.tag(VS, v.tensor_scalar_mul(STAT[:, 16:20], STAT[:, 8:12], 1.0 / E))
                VC.pre(v)
                VC.tag(VS, v.tensor_mul(STAT[:, 40:44], STAT[:, 16:20], STAT[:, 16:20]))
                VC.pre(v)
                VC.tag(VS, v.scalar_tensor_tensor(STAT[:, 48:52], STAT[:, 0:4], 1.0 / E,
                                                  STAT[:, 40:44], op0=ALU.mult,
                                                  op1=ALU.subtract))
                VC.pre(v)
                VC.tag(S["l2sc"], v.tensor_scalar_add(STAT[:, 48:52], STAT[:, 48:52], EPS))
            block.vector(l2_vector)

            def l2_scalar(s):
                s.wait_ge(S["l2sc"].h, 3 * tt + 1)
                SC.pre(s)
                SC.tag(S["l2sc"], s.activation(STAT[:, 56:60], STAT[:, 48:52], AF.Sqrt))
            block.scalar(l2_scalar)

            def l2_vector2(v):
                v.wait_ge(S["l2sc"].h, 3 * tt + 2)
                VC.pre(v)
                VC.tag(VS, v.reciprocal(STAT[:, 40:44], STAT[:, 56:60]))
                VC.pre(v)
                VC.tag(VS, v.tensor_mul(STAT[:, 40:44], STAT[:, 40:44], STAT[:, 48:52]))
                VC.pre(v)
                VC.tag(VS, v.tensor_add(STAT[:, 40:44], STAT[:, 40:44], STAT[:, 56:60]))
                VC.pre(v)
                VC.tag(VS, v.tensor_scalar_mul(STAT[:, 40:44], STAT[:, 40:44], 0.5))
                VC.pre(v)
                VC.tag(VS, v.reciprocal(STAT[:, 24:28], STAT[:, 40:44]))
                VC.pre(v)
                VC.tag(S["l2sc"], v.tensor_scalar_mul(STAT[:, 32:36], STAT[:, 16:20],
                                                      -1.0))
            block.vector(l2_vector2)

            def l2_sync(sync):
                sync.wait_ge(S["l2sc"].h, 3 * tt + 3)
                for b in range(4):
                    S["l2row"].inc(sync.dma_start(
                        out=AP(QRAW, b * 128, [[1024, 1], [1, 128]]),
                        in_=STAT[:, 32 + b:33 + b]), 16)
                    S["l2row"].inc(sync.dma_start(
                        out=AP(QRAW, 32 * 1024 + b * 128, [[1024, 1], [1, 128]]),
                        in_=STAT[:, 24 + b:25 + b]), 16)
            block.sync(l2_sync)

            def l2_tensor2(t):
                t.wait_ge(S["l2row"].h, (tt + 1) * 128)
                t.matmul(PB[4][:, 0:512], ONES_F[0:1, 0:128],
                         AP(QRAW, 0, [[1024, 1], [1, 512]]), start=True, stop=True)
                S["l2bc"].inc(t.matmul(PB[5][:, 0:512], ONES_F[32:33, 0:128],
                                       AP(QRAW, 32 * 1024, [[1024, 1], [1, 512]]),
                                       start=True, stop=True), 1)
            block.tensor(l2_tensor2)

            def l2_vector3(v):
                v.wait_ge(S["l2bc"].h, tt + 1)
                for c in range(16):
                    VC.pre(v)
                    VC.tag(VS, v.tensor_add(KTS[:, 512:1024],
                                            x2bf[:, c * 512:(c + 1) * 512],
                                            PB[4][:, 0:512]))
                    VC.pre(v)
                    last = (c == 15)
                    VC.tag(S["l2h2"] if last else VS,
                           v.tensor_mul(h2v[:, c, tt * 512:(tt + 1) * 512],
                                        KTS[:, 512:1024], PB[5][:, 0:512]))
            block.vector(l2_vector3)

        emit_p2()

        if upto == "p2":
            def dbg2_sync(sync):
                sync.wait_ge(S["l2h2"].h, 2)
                sync.wait_ge(S["x2s0"].h, 256)
                sync.wait_ge(S["x2s1"].h, 256)
                S["dbgs"].inc(sync.dma_start(out=dbg["x2"], in_=x2d), 16)
                S["dbgs"].inc(sync.dma_start(out=dbg["h2"], in_=AR2[:, :]), 16)
                S["dbgs"].inc(sync.dma_start(out=dbg["oh"], in_=AR4[:, :]), 16)
                S["dbgs"].inc(sync.dma_start(out=dbg["qT"], in_=AR1[:, 0:16384]), 16)
                S["dbgs"].inc(sync.dma_start(out=dbg["ktok"], in_=AR1[:, 16384:32768]), 16)
                S["dbgs"].inc(sync.dma_start(out=dbg["v"], in_=AR5[:, :]), 16)
                for i in range(2):
                    S["dbgs"].inc(sync.dma_start(
                        out=dbg["T"].rearrange("r p (i c) -> i r p c", i=2)[i],
                        in_=ago_T[i]), 16)
                S["dbgs"].inc(sync.dma_start(out=dbg["st"], in_=STAT[:, :]), 16)
                S["dbgs"].inc(sync.dma_start(
                    out=out[0:128, 0:1024], in_=QRAW[:, 0:1024]), 16)
                sync.wait_ge(S["dbgs"].h, 144)
            block.sync(dbg2_sync)
            ctx.close()
            return nc

        for nm in ["rtp", "rsd", "eix", "eexp", "enm", "gid", "gbk", "edg",
                   "esl", "eac", "mdq", "msum", "so0", "so1",
                   "mdn", "mtr", "mte"]:
            S[nm] = Sem(ctx.enter_context(nc.semaphore(nm)))

        def r2(a, b):
            return QRAW[:, 512 + a:512 + b]

        smw_all = QRAW[:, 832:896]
        ewd = QRAW[:, 896:904]
        ews = QRAW[:, 904:912]
        ewp = QRAW[:, 912:920]

        def p3r_gpsimd(g):
            g.wait_ge(S["womm"].h, 32)
            wv = wqf.rearrange("(c p) j -> p c j", p=128)
            S["rw"].inc(g.dma_start(out=WS[:, 0:2048], in_=wv), 16)
            keys32a = ATT[0:64, 0:128].bitcast(F32)
            keys32b = ATT[64:128, 0:128].bitcast(F32)
            S["rw"].inc(g.dma_start(out=keys32a, in_=keysf), 16)
            S["rw"].inc(g.dma_start(out=keys32b, in_=keysf), 16)
        with nc.named_scope("p3r"):
            block.gpsimd(p3r_gpsimd)

        def p3r_tensor(t):
            t.wait_ge(S["l2h2"].h, 2)
            t.wait_ge(S["rw"].h, 48)
            for tt in range(2):
                if tt == 1:
                    t.wait_ge(S["rev"].h, 1)
                for c in range(16):
                    mm = t.matmul(PB[0][:, 0:512], WS[:, c * 128:(c + 1) * 128],
                                  h2v[:, c, tt * 512:(tt + 1) * 512],
                                  start=(c == 0), stop=(c == 15))
                S["rmm"].inc(mm, 1)
            t.wait_ge(S["rev"].h, 2)
            for sb in range(8):
                for j in range(2):
                    gidx = sb * 2 + j
                    if gidx >= 1:
                        t.wait_ge(S["rtp"].h, gidx)
                    keys32 = (ATT[0:64, 0:128] if j == 0
                              else ATT[64:128, 0:128]).bitcast(F32)
                    mm = t.matmul(PB[1][:, 0:64],
                                  proj[j * 64:(j + 1) * 64, sb * 128:(sb + 1) * 128],
                                  keys32,
                                  start=True, stop=True)
                    S["rmm"].inc(mm, 1)
        with nc.named_scope("p3r"):
            block.tensor(p3r_tensor)

        def p3r_scalar(s):
            for tt in range(2):
                s.wait_ge(S["rmm"].h, tt + 1)
                SC.pre(s)
                SC.tag(S["rev"], s.activation(proj[:, tt * 512:(tt + 1) * 512],
                                              PB[0][:, 0:512], AF.Copy))
        with nc.named_scope("p3r"):
            block.scalar(p3r_scalar)

        def p3r_vector(v):
            for sb in range(8):
                for j in range(2):
                    gidx = sb * 2 + j
                    v.wait_ge(S["rmm"].h, 2 + gidx + 1)
                    VC.pre(v)
                    VC.tag(VS, v.tensor_copy(r2(0, 64), PB[1][:, 0:64]))
                    VC.pre(v)
                    VC.tag(VS, v.max(r2(64, 72), r2(0, 64)))
                    VC.pre(v)
                    VC.tag(VS, v.max_index(RTI[:, 0:8], r2(64, 72), r2(0, 64)))
                    VC.pre(v)
                    VC.tag(VS, v.tensor_copy(
                        QRAW[:, j * 128 + sb * 16:j * 128 + sb * 16 + 8],
                        r2(64, 72)))
                    VC.pre(v)
                    VC.tag(S["rtp"], v.tensor_copy(
                        QRAW[:, j * 128 + sb * 16 + 8:j * 128 + sb * 16 + 16],
                        RTI[:, 0:8]))
        with nc.named_scope("p3r"):
            block.vector(p3r_vector)

        def p3r_gpsimd2(g):
            g.wait_ge(S["rtp"].h, 16)
            S["rpk"].inc(g.dma_start(out=agi_rt, in_=QRAW[:, 0:256]), 16)
            g.wait_ge(S["rpk"].h, 16)
            S["cc1"].inc(g.collective_compute(
                "AllGather", ALU.bypass, replica_groups=RG,
                ins=[agi_rt.opt()], outs=[ago_rt.opt()]), 1)
            g.wait_ge(S["cc1"].h, 11)
            g.wait_ge(S["x2s0"].h, 256)
            g.wait_ge(S["x2s1"].h, 256)
            for i in range(16):
                r = (i % 8) // 2 + (4 if i >= 8 else 0)
                hh = i % 2
                src = AP(ago_rt.tensor, r * 128 * 256 + hh * 64,
                         [[256, 128], [128, 2], [1, 64]])
                S["rcl"].inc(g.dma_start(out=CSB[:, i * 128:(i + 1) * 128],
                                         in_=src), 16)
        with nc.named_scope("p3r"):
            block.gpsimd(p3r_gpsimd2)

        def p3c_vec_half(v, half):
            v.wait_ge(S["rcl"].h, 256)
            for i in range(half * 8, half * 8 + 8):
                dst = QRAW[:, 0:128] if i < 8 else QRAW[:, 128:256]
                sel = rtselsb[:, i:i + 1]
                VC.pre(v)
                if i % 8 == 0:
                    VC.tag(VS, v.tensor_scalar_mul(dst, CSB[:, i * 128:(i + 1) * 128],
                                                   sel))
                else:
                    tag = S["rsh"] if (i % 8 == 7) else VS
                    VC.tag(tag, v.scalar_tensor_tensor(
                        dst, CSB[:, i * 128:(i + 1) * 128], sel, dst,
                        op0=ALU.mult, op1=ALU.add))

        def p3c_gpsimd2(g):
            g.wait_ge(S["rsh"].h, 2)
            for qy in range(2):
                for j in range(2):
                    for tbp in range(2):
                        src = AP(QRAW, (64 * tbp) * 1024 + qy * 128 + j * 64,
                                 [[1024, 64], [16, 4], [1, 16]])
                        dst = AP(QRAW, j * 1024 + 256 + qy * 128 + 16 * tbp,
                                 [[2048, 64], [32, 4], [1, 16]])
                        S["rsd"].inc(g.dma_start(out=dst, in_=src), 16)
        with nc.named_scope("p3c"):
            block.gpsimd(p3c_gpsimd2)

        def p3a_vec_tb(v, tb):
            if tb == 0:
                v.wait_ge(S["rsd"].h, 128)
            m2 = AP(QRAW, 512 + 192, [[1024, 128], [8, 8], [1, 8]])
            if True:
                sxs = AP(QRAW, 256 + tb * 16, [[1024, 128], [1, 8], [0, 8]])
                sys_ = AP(QRAW, 384 + tb * 16, [[1024, 128], [0, 8], [1, 8]])
                VC.pre(v)
                VC.tag(VS, v.tensor_tensor(r2(64, 128), sxs, sys_, op=ALU.add))
                VC.pre(v)
                VC.tag(VS, v.max(r2(128, 136), r2(64, 128)))
                VC.pre(v)
                VC.tag(VS, v.max_index(RTI[:, 0:8], r2(128, 136), r2(64, 128)))
                VC.pre(v)
                VC.tag(VS, v.tensor_copy(r2(136, 144), RTI[:, 0:8]))
                VC.pre(v)
                VC.tag(VS, v.reduce_max(r2(144, 145), r2(128, 136), axis=AXX))
                VC.pre(v)
                VC.tag(S["enm"], v.tensor_scalar_mul(r2(145, 146), r2(144, 145),
                                                     -1.0))
                v.wait_ge(S["eexp"].h, tb + 1)
                VC.pre(v)
                VC.tag(VS, v.reciprocal(r2(161, 162), r2(160, 161)))
                VC.pre(v)
                VC.tag(VS, v.tensor_scalar_mul(smw_all[:, tb * 8:(tb + 1) * 8],
                                               r2(152, 160), r2(161, 162)))
                posv = AP(QRAW, 512 + 136, [[1024, 128], [1, 8], [0, 8]])
                c8pv = AP(CF, 912, [[1024, 128], [0, 8], [1, 8]])
                VC.pre(v)
                VC.tag(VS, v.tensor_tensor(r2(192, 256), posv, c8pv, op=ALU.is_ge))
                VC.pre(v)
                VC.tag(VS, v.reduce_sum(r2(170, 178), m2, axis=AXX))
                VC.pre(v)
                VC.tag(VS, v.scalar_tensor_tensor(r2(162, 170), r2(170, 178),
                                                  -8.0, r2(136, 144),
                                                  op0=ALU.mult, op1=ALU.add))
                pdv = AP(QRAW, 512 + 170, [[1024, 128], [1, 8], [0, 8]])
                c1v = AP(CF, 904, [[1024, 128], [0, 8], [1, 8]])
                ixv = AP(QRAW, 256 + tb * 16 + 8, [[1024, 128], [0, 8], [1, 8]])
                VC.pre(v)
                VC.tag(VS, v.tensor_tensor(r2(192, 256), pdv, c1v, op=ALU.is_equal))
                VC.pre(v)
                VC.tag(VS, v.tensor_tensor(r2(192, 256), m2, ixv, op=ALU.mult))
                VC.pre(v)
                VC.tag(VS, v.reduce_sum(r2(178, 186), m2, axis=AXX))
                pmv = AP(QRAW, 512 + 162, [[1024, 128], [1, 8], [0, 8]])
                iyv = AP(QRAW, 384 + tb * 16 + 8, [[1024, 128], [0, 8], [1, 8]])
                VC.pre(v)
                VC.tag(VS, v.tensor_tensor(r2(192, 256), pmv, c1v, op=ALU.is_equal))
                VC.pre(v)
                VC.tag(VS, v.tensor_tensor(r2(192, 256), m2, iyv, op=ALU.mult))
                VC.pre(v)
                VC.tag(VS, v.reduce_sum(r2(186, 194), m2, axis=AXX))
                VC.pre(v)
                VC.tag(S["eix"], v.scalar_tensor_tensor(
                    r2(208, 216), r2(178, 186), 64.0, r2(186, 194),
                    op0=ALU.mult, op1=ALU.add))
                v.wait_ge(S["gid"].h, (tb + 1) * 16)

        def p3a_scl_tb(s, tb):
            s.wait_ge(S["enm"].h, tb + 1)
            SC.pre(s)
            SC.tag(S["eexp"], s.activation(r2(152, 160), r2(128, 136), AF.Exp,
                                           bias=r2(145, 146),
                                           accum_out=r2(160, 161)))

        def p3a_gpsimd(g):
            for tb in range(8):
                g.wait_ge(S["eix"].h, tb + 1)
                S["gid"].inc(g.dma_start(
                    out=AP(eidxd.tensor, tb * 1024, [[8, 128], [1, 8]]),
                    in_=r2(208, 216)), 16)
                g.wait_ge(S["gid"].h, (tb + 1) * 16)
        with nc.named_scope("p3a"):
            block.gpsimd(p3a_gpsimd)

        gate16v = ago["gate"].rearrange("(a p) j -> a p j", p=128)
        up16v = ago["up"].rearrange("(a p) j -> a p j", p=128)
        down16v = ago["down"].rearrange("(a p) j -> a p j", p=128)

        S["mgt"] = S["womm"]
        MGTB = S["mgt"].n
        GAUB = [S["gau0"].n, S["gau1"].n]
        S["md2"], S["md3"] = S["wo0"], S["x2s0"]
        MDB = [S["md0"].n, S["md1"].n, S["md2"].n, S["md3"].n]

        def exp_gpsimd(g, tb):
            if tb >= 2:
                g.wait_ge(S["gat0"].h, (tb - 1) * 4 * 16)
                g.wait_ge(S["gat1"].h, (tb - 1) * 4 * 16)
                g.wait_ge(S["gau0"].h, GAUB[0] + (tb - 1) * 4 * 16)
                g.wait_ge(S["gau1"].h, GAUB[1] + (tb - 1) * 4 * 16)
            if tb == 0:
                g.wait_ge(S["idxz"].h, 1)
            src = AP(eidxd.tensor, tb * 1024, [[8, 16], [1, 8], [128, 8]])
            for pr in range(4):
                S["gbk"].inc(g.dma_start(
                    out=IDX[pr * 16:(pr + 1) * 16,
                            (tb % 2) * 64:(tb % 2) * 64 + 64], in_=src,
                    allow_slow_non_contiguous=True), 16)
            g.wait_ge(S["gbk"].h, S["gbk"].n)
            for k in range(8):
                gs = tb * 8 + k
                if gs >= 2:
                    g.wait_ge(S["emm"].h, gs - 1)
                    g.wait_ge(S["gat%d" % (gs % 2)].h, (gs // 2) * 16)
                S["gat%d" % (gs % 2)].inc(g.dma_gather(
                    deT[gs % 2].rearrange("p (a c) -> p a c", a=16),
                    ago["de"],
                    IDX[:, (tb % 2) * 64 + k * 8:(tb % 2) * 64 + k * 8 + 8],
                    128, 128, E, elem_step=E, transpose=True, queue_num=0), 16)
                if gs >= 2:
                    g.wait_ge(S["eac"].h, gs - 1)
                    g.wait_ge(S["gau%d" % (gs % 2)].h, GAUB[gs % 2] + (gs // 2) * 16)
                S["gau%d" % (gs % 2)].inc(g.dma_gather(
                    ueg[gs % 2].rearrange("p (a c) -> p a c", a=1),
                    ago["ue"],
                    IDX[:, (tb % 2) * 64 + k * 8:(tb % 2) * 64 + k * 8 + 8],
                    128, 128, E, elem_step=E, queue_num=1), 16)

        def mlp_sync(sync, tt):
            for ib in range(4):
                for f2 in range(16):
                    g2 = tt * 64 + ib * 16 + f2
                    if g2 == 0:
                        sync.wait_ge(S["rmm"].h, 2)
                    if g2 >= 2:
                        sync.wait_ge(S["mmm"].h, g2 - 1)
                    io = ib * 16 + f2
                    S["mg%d" % (g2 % 2)].inc(sync.dma_start(
                        out=WS[:, (g2 % 2) * 2048:(g2 % 2) * 2048 + 2048],
                        in_=gate16v[:, :, io * 128:(io + 1) * 128].rearrange(
                            "a p j -> p a j")), 16)
                    if g2 in (64, 65):
                        sync.wait_ge(S["mdq"].h, 64)
                    S["mu%d" % (g2 % 2)].inc(sync.dma_start(
                        out=WS[:, 4096 + (g2 % 2) * 2048:4096 + (g2 % 2) * 2048 + 2048],
                        in_=up16v[:, :, io * 128:(io + 1) * 128].rearrange(
                            "a p j -> p a j")), 16)
            for f in range(16):
                fd = tt * 16 + f
                for qq in (0, 2):
                    qg = fd * 4 + qq
                    sl = qg % 4
                    if qg < tt * 64 + 4:
                        sync.wait_ge(S["mmm"].h, (tt + 1) * 64)
                    if qg >= 4:
                        sync.wait_ge(S["mdq"].h, qg - 3)
                        sync.wait_ge(S["md%d" % sl].h, MDB[sl] + (qg // 4) * 16)
                    S["md%d" % sl].inc(sync.dma_start(
                        out=WS[:, 4096 + sl * 2048:4096 + sl * 2048 + 2048],
                        in_=down16v[qq * 16:(qq + 1) * 16, :,
                                    f * 128:(f + 1) * 128].rearrange(
                            "a p j -> p a j")), 16)

        EXPB = [25, 2]

        def mlp_tensor(t, tt):
            if tt == 0:
                t.wait_ge(S["rmm"].h, 18)
                t.wait_ge(S["rtp"].h, 16)
            for ib in range(4):
                for f2 in range(16):
                    g2 = tt * 64 + ib * 16 + f2
                    io = ib * 16 + f2
                    t.wait_ge(S["mg%d" % (g2 % 2)].h, (g2 // 2 + 1) * 16)
                    if g2 >= 1:
                        t.wait_ge(S["msil"].h, g2)
                    for c in range(16):
                        mmg = t.matmul(PB[0][:, 0:512],
                                       WS[:, (g2 % 2) * 2048 + c * 128:
                                          (g2 % 2) * 2048 + (c + 1) * 128],
                                       h2v[:, c, tt * 512:(tt + 1) * 512],
                                       start=(c == 0), stop=(c == 15))
                    S["mgt"].inc(mmg, 1)
                    if EXPB[tt] <= io < EXPB[tt] + 32:
                        e = io - EXPB[tt]
                        tb = 4 * tt + e // 8
                        k = e % 8
                        gs = tb * 8 + k
                        t.wait_ge(S["gat%d" % (gs % 2)].h, (gs // 2 + 1) * 16)
                        if gs >= 2:
                            t.wait_ge(S["edg"].h, gs - 1)
                        for f in range(16):
                            mm = t.matmul(PB[2 + gs % 2][:, 0:128],
                                          h2v[:, f, tb * 128:(tb + 1) * 128],
                                          deT[gs % 2][:, f * 128:(f + 1) * 128],
                                          start=(f == 0), stop=(f == 15))
                        S["emm"].inc(mm, 1)
                    t.wait_ge(S["mu%d" % (g2 % 2)].h, (g2 // 2 + 1) * 16)
                    if g2 >= 1:
                        t.wait_ge(S["mev"].h, g2)
                    for c in range(16):
                        mm = t.matmul(PB[1][:, 0:512],
                                      WS[:, 4096 + (g2 % 2) * 2048 + c * 128:
                                         4096 + (g2 % 2) * 2048 + (c + 1) * 128],
                                      h2v[:, c, tt * 512:(tt + 1) * 512],
                                      start=(c == 0), stop=(c == 15))
                    S["mmm"].inc(mm, 1)
            t.wait_ge(S["eac"].h, (tt * 4 + 4) * 8)
            for f in range(16):
                fd = tt * 16 + f
                t.wait_ge(S["mev"].h, tt * 64 + 64)
                if fd >= 1:
                    t.wait_ge(S["msum"].h, fd)
                first = True
                for qq in range(4):
                    qg = fd * 4 + qq
                    t.wait_ge(S["md%d" % (qg % 4)].h,
                              MDB[qg % 4] + (qg // 4 + 1) * 16)
                    for c in range(16):
                        io = qq * 16 + c
                        mm = t.matmul(PB[4 + fd % 2][:, 0:512],
                                      WS[:, 4096 + (qg % 4) * 2048 + c * 128:
                                         4096 + (qg % 4) * 2048 + (c + 1) * 128],
                                      AR1[:, io * 512:(io + 1) * 512],
                                      start=first, stop=False,
                                      skip_group_check=True)
                        first = False
                    S["mdq"].inc(mm, 1)
                for mt in range(4):
                    mm = t.matmul(PB[4 + fd % 2][:, mt * 128:(mt + 1) * 128],
                                  exp_tok[:, mt * 2048 + f * 128:mt * 2048 + (f + 1) * 128],
                                  IDENT_B, start=False, stop=(mt == 3),
                                  skip_group_check=True)
                S["mdn"].inc(mm, 1)
                t.wait_ge(S["msum"].h, fd + 1)
                if fd >= 1:
                    t.wait_ge(S["mte"].h, fd)
                for mt in range(4):
                    mm = t.matmul(PB[6][:, mt * 128:(mt + 1) * 128],
                                  OST[:, (fd % 2) * 512 + mt * 128:
                                      (fd % 2) * 512 + (mt + 1) * 128],
                                  IDENT_F, start=(mt == 0), stop=(mt == 3))
                S["mtr"].inc(mm, 1)

        def mlp_scalar(s, tt):
            for ib in range(4):
                for f2 in range(16):
                    g2 = tt * 64 + ib * 16 + f2
                    io = ib * 16 + f2
                    s.wait_ge(S["mgt"].h, MGTB + g2 + 1)
                    if g2 >= 1:
                        s.wait_ge(S["mev"].h, g2)
                    SC.pre(s)
                    SC.tag(S["msil"], s.activation(KTS[:, 512:1024], PB[0][:, 0:512],
                                                   AF.Silu))
                    if tt == 0 and 14 <= io < 22:
                        p3a_scl_tb(s, io - 14)
                    if EXPB[tt] <= io < EXPB[tt] + 32:
                        e = io - EXPB[tt]
                        gs = 32 * tt + e
                        k = e % 8
                        s.wait_ge(S["edg"].h, gs + 1)
                        SC.pre(s)
                        SC.tag(S["esl"], s.activation(ews[:, k:k + 1],
                                                      ewd[:, k:k + 1], AF.Silu))
            for f in range(16):
                fd = tt * 16 + f
                for qq in (1, 3):
                    qg = fd * 4 + qq
                    sl = qg % 4
                    if qg < tt * 64 + 4:
                        s.wait_ge(S["mmm"].h, (tt + 1) * 64)
                    if qg >= 4:
                        s.wait_ge(S["mdq"].h, qg - 3)
                        s.wait_ge(S["md%d" % sl].h, MDB[sl] + (qg // 4) * 16)
                    S["md%d" % sl].inc(s.dma_start(
                        out=WS[:, 4096 + sl * 2048:4096 + sl * 2048 + 2048],
                        in_=down16v[qq * 16:(qq + 1) * 16, :,
                                    f * 128:(f + 1) * 128].rearrange(
                            "a p j -> p a j")), 16)
                if fd >= 2:
                    s.wait_ge(S["msum"].h, fd - 1)
                S["x2l%d" % (fd % 2)].inc(s.dma_start(
                    out=XST[:, (fd % 2) * 512:(fd % 2) * 512 + 512],
                    in_=AP(x2d.tensor, f * 128 * TPC + tt * 512,
                           [[TPC, 128], [1, 512]])), 16)
                s.wait_ge(S["mte"].h, fd + 1)
                S["so%d" % (fd % 2)].inc(s.dma_start(
                    out=AP(out.tensor, (tt * 512) * E + f * 128,
                           [[E, 128], [128 * E, 4], [1, 128]]),
                    in_=RBC[:, (fd % 2) * 512:(fd % 2) * 512 + 512]), 16)

        def _exp_acc(v, tt, d):
            tb = 4 * tt + d // 8
            kd = d % 8
            gd = tb * 8 + kd
            v.wait_ge(S["esl"].h, gd + 1)
            VC.pre(v)
            VC.tag(VS, v.tensor_mul(ewp[:, kd:kd + 1], ews[:, kd:kd + 1],
                                    smw_all[:, tb * 8 + kd:tb * 8 + kd + 1]))
            v.wait_ge(S["gau%d" % (gd % 2)].h, GAUB[gd % 2] + (gd // 2 + 1) * 16)
            slot = exp_tok[:, (tb % 4) * 2048:(tb % 4 + 1) * 2048]
            VC.pre(v)
            if kd == 0:
                VC.tag(S["eac"], v.tensor_scalar_mul(
                    slot, ueg[gd % 2][:, 0:2048], ewp[:, 0:1]))
            else:
                VC.tag(S["eac"], v.scalar_tensor_tensor(
                    slot, ueg[gd % 2][:, 0:2048], ewp[:, kd:kd + 1], slot,
                    op0=ALU.mult, op1=ALU.add))

        def mlp_vector(v, tt):
            for ib in range(4):
                for f2 in range(16):
                    g2 = tt * 64 + ib * 16 + f2
                    io = ib * 16 + f2
                    v.wait_ge(S["mmm"].h, g2 + 1)
                    v.wait_ge(S["msil"].h, g2 + 1)
                    VC.pre(v)
                    VC.tag(S["mev"], v.tensor_mul(AR1[:, io * 512:(io + 1) * 512],
                                                  PB[1][:, 0:512], KTS[:, 512:1024]))
                    if tt == 0 and io in (12, 13):
                        p3c_vec_half(v, io - 12)
                    if tt == 0 and 14 <= io < 22:
                        p3a_vec_tb(v, io - 14)
                    if EXPB[tt] <= io < EXPB[tt] + 32:
                        e = io - EXPB[tt]
                        tb = 4 * tt + e // 8
                        k = e % 8
                        gs = tb * 8 + k
                        v.wait_ge(S["emm"].h, gs + 1)
                        VC.pre(v)
                        VC.tag(VS, v.tensor_mul(r2(192, 320),
                                                PB[2 + gs % 2][:, 0:128], IDENT_F))
                        VC.pre(v)
                        VC.tag(S["edg"], v.reduce_sum(ewd[:, k:k + 1],
                                                      r2(192, 320), axis=AXX))
                        if e >= 1:
                            _exp_acc(v, tt, e - 1)
                        if e == 31:
                            _exp_acc(v, tt, 31)
            for f in range(16):
                fd = tt * 16 + f
                v.wait_ge(S["mdn"].h, fd + 1)
                v.wait_ge(S["x2l%d" % (fd % 2)].h, (fd // 2 + 1) * 16)
                VC.pre(v)
                VC.tag(S["msum"], v.tensor_add(
                    OST[:, (fd % 2) * 512:(fd % 2) * 512 + 512],
                    PB[4 + fd % 2][:, 0:512],
                    XST[:, (fd % 2) * 512:(fd % 2) * 512 + 512]))
                v.wait_ge(S["mtr"].h, fd + 1)
                if fd >= 2:
                    v.wait_ge(S["so%d" % (fd % 2)].h, (fd // 2) * 16)
                VC.pre(v)
                VC.tag(S["mte"], v.tensor_copy(
                    RBC[:, (fd % 2) * 512:(fd % 2) * 512 + 512], PB[6][:, 0:512]))

        for tt in range(2):
            with nc.named_scope("experts"):
                for tb in range(4 * tt, 4 * tt + 4):
                    block.gpsimd(lambda g, tb=tb: exp_gpsimd(g, tb))
            with nc.named_scope("mlp"):
                block.sync(lambda sync, tt=tt: mlp_sync(sync, tt))
                block.tensor(lambda t, tt=tt: mlp_tensor(t, tt))
                block.scalar(lambda s, tt=tt: mlp_scalar(s, tt))
                block.vector(lambda v, tt=tt: mlp_vector(v, tt))

        def p3_final(s):
            s.wait_ge(S["so0"].h, 256)
            s.wait_ge(S["so1"].h, 256)
        block.scalar(p3_final)

    ctx.close()
    return nc


import ml_dtypes
BF = ml_dtypes.bfloat16


def make_consts():
    cf = np.zeros((128, 1024), np.float32)
    cf[:, 0:128] = np.eye(128, dtype=np.float32)
    P = np.zeros((128, 128), np.float32)
    for i in range(64):
        P[2 * i, 2 * i + 1] = -1.0
        P[2 * i + 1, 2 * i] = 1.0
    cf[:, 128:256] = P.T
    p = np.arange(128)[:, None]
    l = np.arange(256)[None, :]
    cf[:, 256:512] = (p <= l).astype(np.float32)
    cf[:, 512:768] = (p + 128 <= l).astype(np.float32)
    cf[:, 768:896] = 1.0
    cf[:, 896:904] = (np.arange(8) * 8.0)[None, :]
    cf[:, 904:912] = np.arange(8.0)[None, :]
    cf[:, 912:920] = (np.arange(8.0)[None, :] + 1) * 8.0
    cb = np.zeros((128, 132), BF)
    cb[:, 0:128] = np.eye(128, dtype=np.float32).astype(BF)
    cb[:, 128:129] = np.float32(1.0)
    return cf, cb


def rope_tables(s0):
    inv = 1.0 / 10000.0 ** (np.arange(0, D, 2, dtype=np.float32) / D)
    invf = np.concatenate([inv, inv])
    pos = np.arange(s0, s0 + TPC, dtype=np.float32)
    ang = invf[:, None] * pos[None, :]
    cs = np.zeros((128, 2048), np.float32)
    cs[:, 0:1024] = np.cos(ang)
    cs[:, 1024:2048] = np.sin(ang)
    return cs


def stage_inputs(inp):
    x = np.asarray(inp["L0_attn_x"], np.float32)
    Wqkv = np.asarray(inp["L0_attn_W_qkv"], np.float32)
    Wout = np.asarray(inp["L0_attn_W_out"], np.float32)
    Wq = np.asarray(inp["L0_ffn_W_q"], np.float32)
    keys = np.asarray(inp["L0_ffn_keys"], np.float32)
    de = np.asarray(inp["L0_ffn_down_embed"], np.float32)
    ue = np.asarray(inp["L0_ffn_up_embed"], np.float32)
    gw = np.asarray(inp["L0_ffn_gate_w"], np.float32)
    uw = np.asarray(inp["L0_ffn_up_w"], np.float32)
    dw = np.asarray(inp["L0_ffn_down_w"], np.float32)

    wqkvT = np.ascontiguousarray(Wqkv.T)
    woutT = np.ascontiguousarray(Wout.T)
    cf, cb = make_consts()

    in_maps = []
    for c in range(8):
        b, q = c // 4, c % 4
        s0 = TPC * q
        xt = np.ascontiguousarray(x[b, s0:s0 + TPC].T)
        psel = np.zeros((128, 8), np.float32)
        for r in range(8):
            if r // 4 == b and r % 4 < q:
                psel[:, r] = 1.0
        rtsel = np.zeros((128, 16), np.float32)
        r0 = 2 * b + q // 2
        hsel = q % 2
        rtsel[:, 2 * r0 + hsel] = 1.0
        rtsel[:, 8 + 2 * r0 + hsel] = 1.0
        m = {
            "xT": xt,
            "cs": rope_tables(s0),
            "cf": cf,
            "cbi": cb,
            "wqf": Wq,
            "keysf": np.ascontiguousarray(keys[b]),
            "pself": psel,
            "rtself": rtsel,
            "sh_wqkvT": np.ascontiguousarray(wqkvT[c * 256:(c + 1) * 256]),
            "sh_woutT": np.ascontiguousarray(woutT[c * 256:(c + 1) * 256]),
            "sh_gate": np.ascontiguousarray(gw[c * 256:(c + 1) * 256]),
            "sh_up": np.ascontiguousarray(uw[c * 256:(c + 1) * 256]),
            "sh_down": np.ascontiguousarray(dw[c * 1024:(c + 1) * 1024]),
            "sh_de": np.ascontiguousarray(de[c * 512:(c + 1) * 512]),
            "sh_ue": np.ascontiguousarray(ue[c * 512:(c + 1) * 512]),
        }
        in_maps.append(m)
    return in_maps


from concourse.bass_utils import run_bass_kernel_spmd

_NC_CACHE = {}


def kernel(**inputs):
    in_maps = stage_inputs(inputs)
    if "nc" not in _NC_CACHE:
        nc = build({"upto": "full"})
        nc.finalize()
        _NC_CACHE["nc"] = nc
    nc = _NC_CACHE["nc"]
    res = run_bass_kernel_spmd(nc, in_maps, core_ids=list(range(8)))
    full = np.zeros((2, 4096, 2048), np.float32)
    for c in range(8):
        b, q = c // 4, c % 4
        full[b, q * 1024:(q + 1) * 1024] = np.asarray(res.results[c]["out"])
    return full


# version 21
# speedup vs baseline: 1.0231x; 1.0231x over previous
import numpy as np
from contextlib import ExitStack

import concourse.bacc as bacc
import concourse.mybir as mybir
from concourse.bass import AP

F32 = mybir.dt.float32
BF16 = mybir.dt.bfloat16
I16 = mybir.dt.int16
U32 = mybir.dt.uint32
AF = mybir.ActivationFunctionType
ALU = mybir.AluOpType
AXX = mybir.AxisListType.X

NCORE = 8
TPC = 1024
NH = 16
D = 128
E = 2048
INTER = 8192
NEXP = 4096
EPS = 1e-5
RG = [list(range(NCORE))]

WNAMES = [("wqkvT", E, 3 * E), ("woutT", E, E), ("gate", E, INTER),
          ("up", E, INTER), ("down", INTER, E), ("de", NEXP, E), ("ue", NEXP, E)]


class Sem:
    def __init__(self, h):
        self.h = h
        self.n = 0

    def inc(self, inst, k):
        inst.then_inc(self.h, k)
        self.n += k
        return inst


def build(cfg=None):
    cfg = cfg or {}
    upto = cfg.get("upto", "full")

    nc = bacc.Bacc("TRN2", target_bir_lowering=False, num_swdge_queues=2)

    xT = nc.dram_tensor("xT", [E, TPC], F32, kind="ExternalInput").ap()
    cs = nc.dram_tensor("cs", [128, 2048], F32, kind="ExternalInput").ap()
    cf = nc.dram_tensor("cf", [128, 1024], F32, kind="ExternalInput").ap()
    cbi = nc.dram_tensor("cbi", [128, 132], BF16, kind="ExternalInput").ap()
    wqf = nc.dram_tensor("wqf", [E, 128], F32, kind="ExternalInput").ap()
    keysf = nc.dram_tensor("keysf", [64, 64], F32, kind="ExternalInput").ap()
    pself = nc.dram_tensor("pself", [128, 8], F32, kind="ExternalInput").ap()
    rtself = nc.dram_tensor("rtself", [128, 16], F32, kind="ExternalInput").ap()
    sh = {}
    for nm, rows, cols in WNAMES:
        sh[nm] = nc.dram_tensor("sh_" + nm, [rows // 8, cols], F32,
                                kind="ExternalInput").ap()

    out = nc.dram_tensor("out", [TPC, E], F32, kind="ExternalOutput").ap()
    dbg = {}
    if upto in ("p1", "p2"):
        dbg["qT"] = nc.dram_tensor("dbg_qT", [128, NH * TPC], BF16, kind="ExternalOutput").ap()
        dbg["ktok"] = nc.dram_tensor("dbg_ktok", [128, 8 * NH * 128], BF16, kind="ExternalOutput").ap()
        dbg["v"] = nc.dram_tensor("dbg_v", [128, 8 * E], BF16, kind="ExternalOutput").ap()
        dbg["T"] = nc.dram_tensor("dbg_T", [8, 128, NH * 129], F32, kind="ExternalOutput").ap()
        dbg["st"] = nc.dram_tensor("dbg_st", [128, 64], F32, kind="ExternalOutput").ap()
    if upto == "p2":
        dbg["x2"] = nc.dram_tensor("dbg_x2", [E, TPC], F32, kind="ExternalOutput").ap()
        dbg["oh"] = nc.dram_tensor("dbg_oh", [128, 8192], BF16, kind="ExternalOutput").ap()
        dbg["h2"] = nc.dram_tensor("dbg_h2", [128, 16384], BF16, kind="ExternalOutput").ap()

    agi = {}
    ago = {}
    AGNAMES = [("wv", E, E), ("wqk", E, 2 * E), ("woutT", E, E),
               ("gate_a", E, INTER // 2), ("gate_b", E, INTER // 2),
               ("up_a", E, INTER // 2), ("up_b", E, INTER // 2),
               ("down", INTER, E), ("de", NEXP, E), ("ue", NEXP, E)]
    for nm, rows, cols in AGNAMES:
        agi[nm] = nc.dram_tensor("agi_" + nm, [rows // 8, cols], BF16, kind="Internal").ap()
        ago[nm] = nc.dram_tensor("ago_" + nm, [rows, cols], BF16, kind="Internal",
                                 addr_space="Shared").ap()
    agi_T = [nc.dram_tensor("agi_T%d" % i, [128, 8 * 129], F32, kind="Internal").ap()
             for i in range(2)]
    ago_T = [nc.dram_tensor("ago_T%d" % i, [8, 128, 8 * 129], F32, kind="Internal",
                            addr_space="Shared").ap() for i in range(2)]
    agi_rt = nc.dram_tensor("agi_rt", [128, 256], F32, kind="Internal").ap()
    ago_rt = nc.dram_tensor("ago_rt", [8, 128, 256], F32, kind="Internal",
                            addr_space="Shared").ap()
    x2d = nc.dram_tensor("x2d", [E, TPC], F32, kind="Internal").ap()
    eidxd = nc.dram_tensor("eidxd", [8, 1024], I16, kind="Internal").ap()

    ctx = ExitStack()

    def sb(name, shape, dt):
        return ctx.enter_context(nc.sbuf_tensor(name, shape, dt))

    AR1 = sb("AR1", [128, 32768], BF16)
    AR2 = sb("AR2", [128, 16384], BF16)
    AR4 = sb("AR4", [128, 8192], BF16)
    AR5 = sb("AR5", [128, 16384], BF16)
    WS = sb("WS", [128, 12288], BF16)
    CSB = sb("CSB", [128, 2048], F32)
    CF = sb("CF", [128, 1024], F32)
    CB = sb("CB", [128, 132], BF16)
    STT = sb("STT", [128, 258], F32)
    SST = sb("SST", [128, 2064], F32)
    SBF = sb("SBF", [128, 258], BF16)
    STAT = sb("STAT", [128, 64], F32)
    RBC = sb("RBC", [128, 1024], F32)
    QRAW = sb("QRAW", [128, 1024], F32)
    RTMP = sb("RTMP", [128, 1280], F32)
    KTS = sb("KTS", [128, 1024], BF16)
    ATT = sb("ATT", [128, 512], BF16)
    RBS = sb("RBS", [128, 260], F32)
    RTI = sb("RTI", [128, 8], U32)
    IDX = sb("IDX", [128, 128], I16)
    PSEL = sb("PSEL", [128, 24], F32)
    OST = CSB[:, 0:1024]
    XST = CSB[:, 1024:2048]
    RTB = QRAW[:, 0:512]
    RT2 = QRAW[:, 512:832]
    EWS = QRAW[:, 832:928]
    PRJ = None

    PB = [ctx.enter_context(nc.psum_tensor("pb%d" % i, [128, 512], F32))
          for i in range(8)]

    semnames = ["vself", "sself", "gx", "gld", "gst", "idxz", "cc1",
                "amm", "aev", "ascl", "arow", "abc", "actr",
                "bw0", "bw1", "bmm", "bev",
                "cw0", "cw1", "cmm", "crot", "cse", "csx", "cvm0", "cvf",
                "ckt", "ckte", "cTg", "cTe", "cTd0", "cTd1",
                "misc"]
    S = {nm: Sem(ctx.enter_context(nc.semaphore(nm))) for nm in semnames}
    if cfg.get("upto") != "full":
        S["dbgs"] = Sem(ctx.enter_context(nc.semaphore("dbgs")))
    VS, SS = S["vself"], S["sself"]

    class Chain:
        def __init__(self):
            self.last = None

        def pre(self, e):
            if self.last:
                e.wait_ge(self.last[0], self.last[1])

        def tag(self, sem, inst, k=1):
            sem.inc(inst, k)
            self.last = (sem.h, sem.n)
            return inst

    VC = Chain()
    SC = Chain()

    IDENT_F = CF[:, 0:128]
    PMAT = CF[:, 128:256]
    TRIU = [CF[:, 256:512], CF[:, 512:768]]
    ONES_F = CF[:, 768:896]
    C8 = CF[:, 896:904]
    C1 = CF[:, 904:912]
    IDENT_B = CB[:, 0:128]
    ONESB = CB[:, 128:129]

    qTs = AR1[:, 0:16384].rearrange("p (h t) -> p h t", h=NH)
    ktok = AR1[:, 16384:32768].rearrange("p (m h d) -> p m h d", m=8, h=NH)
    xc = AR2[:, :].rearrange("p (tt c t) -> p tt c t", tt=2, c=16)
    vtok = AR5[:, :].rearrange("p (m e) -> p m e", m=8)
    outh = AR4[:, :].rearrange("p (h l) -> p h l", h=NH)

    with nc.Block() as block:
        def p0_sync(sync):
            S["misc"].inc(sync.dma_start(out=CSB[:, :], in_=cs), 16)
            S["misc"].inc(sync.dma_start(out=CF[:, :], in_=cf), 16)
            S["misc"].inc(sync.dma_start(out=CB[:, :], in_=cbi), 16)
            S["misc"].inc(sync.dma_start(out=PSEL[:, 0:8], in_=pself), 16)
            S["misc"].inc(sync.dma_start(out=PSEL[:, 8:24], in_=rtself), 16)
        with nc.named_scope("p0"):
            block.sync(p0_sync)

        def p0_gpsimd(g):
            S["idxz"].inc(g.memset(IDX[:, :], 0), 1)
            for tt in range(2):
                xv = xT.rearrange("(c p) t -> p c t", p=128)[:, :, tt * 512:(tt + 1) * 512]
                S["gx"].inc(g.dma_start(out=xc[:, tt], in_=xv), 16)
            par = 0
            CONV = [("wv", "wqkvT", 2 * E, E, E), ("wqk", "wqkvT", 0, E, 2 * E),
                    ("woutT", "woutT", 0, E, E),
                    ("gate_a", "gate", 0, E, INTER // 2),
                    ("gate_b", "gate", INTER // 2, E, INTER // 2),
                    ("up_a", "up", 0, E, INTER // 2),
                    ("up_b", "up", INTER // 2, E, INTER // 2),
                    ("down", "down", 0, INTER, E),
                    ("de", "de", 0, NEXP, E), ("ue", "ue", 0, NEXP, E)]
            for nm, snm, c0, rows, cols in CONV:
                nrp = rows // 8 // 128
                src_v = sh[snm].rearrange("(a p) c -> a p c", p=128)
                dst_v = agi[nm].rearrange("(a p) c -> a p c", p=128)
                for a in range(nrp):
                    pos = 0
                    while pos < cols:
                        take = min(4096, cols - pos)
                        if par >= 2:
                            g.wait_ge(S["gst"].h, par * 16)
                        S["gld"].inc(g.dma_start(
                            out=AR4[:, (par % 2) * 4096:(par % 2) * 4096 + take],
                            in_=src_v[a, :, c0 + pos:c0 + pos + take]), 16)
                        g.wait_ge(S["gld"].h, (par + 1) * 16)
                        S["gst"].inc(g.dma_start(
                            out=dst_v[a, :, pos:pos + take],
                            in_=AR4[:, (par % 2) * 4096:(par % 2) * 4096 + take]), 16)
                        par += 1
                        pos += take
                if nm in ("wv", "wqk", "woutT"):
                    g.wait_ge(S["gst"].h, S["gst"].n)
                    S["cc1"].inc(g.collective_compute(
                        "AllGather", ALU.bypass, replica_groups=RG,
                        ins=[agi[nm].opt()], outs=[ago[nm].opt()]), 1)
            g.wait_ge(S["gst"].h, S["gst"].n)
            for nm in ["gate_a", "up_a"]:
                S["cc1"].inc(g.collective_compute(
                    "AllGather", ALU.bypass, replica_groups=RG,
                    ins=[agi[nm].opt()], outs=[ago[nm].opt()]), 1)
        with nc.named_scope("p0conv"):
            block.gpsimd(p0_gpsimd)

        def p1a_tensor(t):
            t.wait_ge(S["gx"].h, 32)
            t.wait_ge(S["misc"].h, 80)
            for blk in range(8):
                tt, tb = blk // 4, blk % 4
                if blk >= 1:
                    t.wait_ge(S["aev"].h, blk)
                for c in range(16):
                    lhs = xc[:, tt, c, tb * 128:tb * 128 + 128]
                    t.matmul(PB[6][:, 0:128], lhs, lhs,
                             start=(c == 0), stop=(c == 15))
                for c in range(16):
                    mm = t.matmul(PB[6][:, 128:129],
                                  xc[:, tt, c, tb * 128:tb * 128 + 128],
                                  ONESB, start=False, stop=(c == 15),
                                  skip_group_check=True)
                S["amm"].inc(mm, 1)
        with nc.named_scope("p1a"):
            block.tensor(p1a_tensor)

        def p1a_vector(v):
            for blk in range(8):
                v.wait_ge(S["amm"].h, blk + 1)
                VC.pre(v)
                VC.tag(VS, v.tensor_mul(QRAW[:, 0:128], PB[6][:, 0:128], IDENT_F))
                VC.pre(v)
                VC.tag(VS, v.reduce_sum(STAT[:, blk:blk + 1], QRAW[:, 0:128], axis=AXX))
                VC.pre(v)
                VC.tag(S["aev"], v.tensor_copy(STAT[:, 8 + blk:9 + blk],
                                               PB[6][:, 128:129]))
            VC.pre(v)
            VC.tag(VS, v.tensor_scalar_mul(STAT[:, 16:24], STAT[:, 8:16], 1.0 / E))
            VC.pre(v)
            VC.tag(VS, v.tensor_mul(STAT[:, 40:48], STAT[:, 16:24], STAT[:, 16:24]))
            VC.pre(v)
            VC.tag(VS, v.scalar_tensor_tensor(STAT[:, 48:56], STAT[:, 0:8], 1.0 / E,
                                              STAT[:, 40:48], op0=ALU.mult,
                                              op1=ALU.subtract))
            VC.pre(v)
            VC.tag(S["ascl"], v.tensor_scalar_add(STAT[:, 48:56], STAT[:, 48:56], EPS))
        block.vector(p1a_vector)

        def p1a_scalar(s):
            s.wait_ge(S["ascl"].h, 1)
            SC.pre(s)
            SC.tag(S["ascl"], s.activation(STAT[:, 56:64], STAT[:, 48:56], AF.Sqrt))
        block.scalar(p1a_scalar)

        def p1a_vector2(v):
            v.wait_ge(S["ascl"].h, 2)
            VC.pre(v)
            VC.tag(VS, v.reciprocal(STAT[:, 40:48], STAT[:, 56:64]))
            VC.pre(v)
            VC.tag(VS, v.tensor_mul(STAT[:, 40:48], STAT[:, 40:48], STAT[:, 48:56]))
            VC.pre(v)
            VC.tag(VS, v.tensor_add(STAT[:, 40:48], STAT[:, 40:48], STAT[:, 56:64]))
            VC.pre(v)
            VC.tag(VS, v.tensor_scalar_mul(STAT[:, 40:48], STAT[:, 40:48], 0.5))
            VC.pre(v)
            VC.tag(VS, v.reciprocal(STAT[:, 24:32], STAT[:, 40:48]))
            VC.pre(v)
            VC.tag(S["ascl"], v.tensor_scalar_mul(STAT[:, 32:40], STAT[:, 16:24],
                                                  -1.0))
        block.vector(p1a_vector2)

        def p1a_sync2(sync):
            sync.wait_ge(S["ascl"].h, 3)
            for b in range(8):
                S["arow"].inc(sync.dma_start(
                    out=AP(QRAW, b * 128, [[1024, 1], [1, 128]]),
                    in_=STAT[:, 32 + b:33 + b]), 16)
                S["arow"].inc(sync.dma_start(
                    out=AP(RTMP, b * 128, [[1280, 1], [1, 128]]),
                    in_=STAT[:, 24 + b:25 + b]), 16)
        block.sync(p1a_sync2)

        def p1a_tensor2(t):
            t.wait_ge(S["arow"].h, 256)
            t.wait_ge(S["aev"].h, 8)
            for tt in range(2):
                t.matmul(PB[7 - tt][:, 0:512], ONES_F[0:1, 0:128],
                         AP(QRAW, tt * 512, [[1024, 1], [1, 512]]),
                         start=True, stop=True)
                S["abc"].inc(t.matmul(PB[5 - tt][:, 0:512], ONES_F[0:1, 0:128],
                                      AP(RTMP, tt * 512, [[1280, 1], [1, 512]]),
                                      start=True, stop=True), 1)
        block.tensor(p1a_tensor2)

        def p1a_scalar2(s):
            for tt in range(2):
                s.wait_ge(S["abc"].h, tt + 1)
                SC.pre(s)
                SC.tag(SS, s.activation(KTS[:, tt * 512:tt * 512 + 512],
                                        PB[7 - tt][:, 0:512], AF.Copy))
                SC.pre(s)
                SC.tag(S["ascl"], s.activation(RBC[:, tt * 512:tt * 512 + 512],
                                               PB[5 - tt][:, 0:512], AF.Copy))
        block.scalar(p1a_scalar2)

        def p1a_vector3(v):
            v.wait_ge(S["ascl"].h, 5)
            for tt in range(2):
                for c in range(16):
                    VC.pre(v)
                    last = (tt == 1 and c == 15)
                    VC.tag(S["actr"] if last else VS,
                           v.tensor_add(xc[:, tt, c], xc[:, tt, c],
                                        KTS[:, tt * 512:tt * 512 + 512]))
        block.vector(p1a_vector3)

        wv16v = ago["wv"].rearrange("(c p) j -> p c j", p=128)
        wqk16v = ago["wqk"].rearrange("(c p) j -> p c j", p=128)

        def p1b_sync(sync):
            sync.wait_ge(S["cc1"].h, 1)
            for eg in range(8):
                if eg >= 2:
                    sync.wait_ge(S["bev"].h, (eg - 1) * 8)
                wv = wv16v[:, :, eg * 256:(eg + 1) * 256]
                S["bw%d" % (eg % 2)].inc(sync.dma_start(
                    out=WS[:, (eg % 2) * 4096:(eg % 2) * 4096 + 4096], in_=wv), 16)
        with nc.named_scope("p1b"):
            block.sync(p1b_sync)

        def p1b_tensor(t):
            t.wait_ge(S["actr"].h, 1)
            for eg in range(8):
                t.wait_ge(S["bw%d" % (eg % 2)].h, (eg // 2 + 1) * 16)
                for mb in range(8):
                    it = eg * 8 + mb
                    if it >= 2:
                        t.wait_ge(S["bev"].h, it - 1)
                    tt, tb = mb // 4, mb % 4
                    for c in range(16):
                        mm = t.matmul(PB[it % 2][:, 0:256],
                                      xc[:, tt, c, tb * 128:tb * 128 + 128],
                                      WS[:, (eg % 2) * 4096 + c * 256:
                                         (eg % 2) * 4096 + (c + 1) * 256],
                                      start=(c == 0), stop=(c == 15))
                    S["bmm"].inc(mm, 1)
        with nc.named_scope("p1b"):
            block.tensor(p1b_tensor)

        def p1b_scalar(s):
            for eg in range(8):
                for mb in range(8):
                    it = eg * 8 + mb
                    s.wait_ge(S["bmm"].h, it + 1)
                    SC.pre(s)
                    SC.tag(S["bev"], s.activation(
                        vtok[:, mb, eg * 256:(eg + 1) * 256], PB[it % 2][:, 0:256],
                        AF.Copy, scale=STAT[:, 24 + mb:25 + mb]))
        with nc.named_scope("p1b"):
            block.scalar(p1b_scalar)

        def dec(u):
            return u // 4, (u // 2) % 2, u % 2

        def p1c_sync(sync):
            sync.wait_ge(S["cc1"].h, 2)
            for h in range(NH):
                if h < 2:
                    sync.wait_ge(S["bmm"].h, 64)
                else:
                    sync.wait_ge(S["cmm"].h, (h - 1) * 4)
                for kk in range(2):
                    wv = wqk16v[:, :, kk * E + h * 128:kk * E + (h + 1) * 128]
                    S["cw%d" % (h % 2)].inc(sync.dma_start(
                        out=WS[:, (h % 2) * 4096 + kk * 2048:
                               (h % 2) * 4096 + (kk + 1) * 2048], in_=wv), 16)
        with nc.named_scope("p1c"):
            block.sync(p1c_sync)

        def rslot(u):
            tn = QRAW if u % 4 < 2 else RBC
            return tn[:, (u % 2) * 512:(u % 2) * 512 + 512]
        CSRB = S["arow"].n

        def p1c_tensor(t):
            t.wait_ge(S["actr"].h, 1)
            t.wait_ge(S["bev"].h, 64)
            for u in range(64 + 4):
                if 0 <= u - 3 < 64:
                    uT = u - 3
                    hT, kkT, ttT = dec(uT)
                    if kkT == 1:
                        ksT = 2 * hT + ttT
                        t.wait_ge(S["ckte"].h, ksT + 1)
                        if hT == 0 and ttT == 0:
                            t.wait_ge(S["bev"].h, 64)
                        if ttT == 0 and hT >= 1:
                            t.wait_ge(S["cTe"].h, hT)
                        for i in range(4):
                            mb = ttT * 4 + i
                            first = (ttT == 0 and i == 0)
                            t.matmul(PB[6][:, 0:128], ktok[:, mb, hT],
                                     vtok[:, mb, hT * 128:(hT + 1) * 128],
                                     start=first, stop=False,
                                     skip_group_check=True)
                            mm = t.matmul(PB[6][:, 128:129], ktok[:, mb, hT],
                                          ONESB, start=False,
                                          stop=(ttT == 1 and i == 3),
                                          skip_group_check=True)
                        S["cTg"].inc(mm, 1)
                if 0 <= u - 2 < 64:
                    uk = u - 2
                    h2_, kk2, tt2 = dec(uk)
                    if kk2 == 1:
                        ks = 2 * h2_ + tt2
                        t.wait_ge(S["cvf"].h, uk + 1)
                        if ks >= 1:
                            t.wait_ge(S["ckte"].h, ks)
                        for i in range(4):
                            mm = t.matmul(PB[7][:, i * 128:(i + 1) * 128],
                                          KTS[:, (ks % 2) * 512 + i * 128:
                                              (ks % 2) * 512 + (i + 1) * 128],
                                          IDENT_B, start=(i == 0), stop=(i == 3))
                        S["ckt"].inc(mm, 1)
                if u < 64:
                    h, kk, tt = dec(u)
                    if kk == 0 and tt == 0:
                        t.wait_ge(S["cw%d" % (h % 2)].h, (h // 2 + 1) * 32)
                    if u >= 4:
                        t.wait_ge(S["cse"].h, u - 3)
                    base = (h % 2) * 4096 + kk * 2048
                    for c in range(16):
                        mm = t.matmul(PB[u % 4][:, 0:512],
                                      WS[:, base + c * 128:base + (c + 1) * 128],
                                      xc[:, tt, c],
                                      start=(c == 0), stop=(c == 15))
                    S["cmm"].inc(mm, 1)
                if 0 <= u - 1 < 64:
                    up = u - 1
                    t.wait_ge(S["cse"].h, up + 1)
                    if up >= 2:
                        t.wait_ge(S["cvm0"].h, up - 1)
                    S["crot"].inc(t.matmul(
                        PB[4 + up % 2][:, 0:512], PMAT,
                        rslot(up),
                        start=True, stop=True), 1)
        with nc.named_scope("p1c"):
            block.tensor(p1c_tensor)

        def p1c_scalar(s):
            for u in range(64 + 1):
                if u < 64:
                    h, kk, tt = dec(u)
                    s.wait_ge(S["cmm"].h, u + 1)
                    if u in (2, 3):
                        s.wait_ge(S["arow"].h, CSRB + 1)
                    if u >= 4:
                        s.wait_ge(S["cvm0"].h, u - 3)
                    SC.pre(s)
                    SC.tag(S["cse"], s.activation(
                        rslot(u), PB[u % 4][:, 0:512], AF.Copy))
                if u >= 1:
                    up = u - 1
                    s.wait_ge(S["cvm0"].h, up + 1)
                    SC.pre(s)
                    SC.tag(S["csx"], s.activation(
                        RTMP[:, 1024:1280].bitcast(BF16),
                        rslot(up), AF.Exp))
                    hp, kkp, ttp = dec(up)
                    if kkp == 1:
                        ks = 2 * hp + ttp
                        s.wait_ge(S["ckt"].h, ks + 1)
                        for i in range(4):
                            SC.pre(s)
                            SC.tag(S["ckte"] if i == 3 else SS,
                                   s.activation(ktok[:, ttp * 4 + i, hp],
                                                PB[7][:, i * 128:(i + 1) * 128],
                                                AF.Copy))
                    if kkp == 1 and ttp == 1:
                        s.wait_ge(S["cTg"].h, 2 * hp + 2)
                        if hp >= 2:
                            s.wait_ge(S["cTd%d" % (hp % 2)].h, (hp // 2) * 16)
                        SC.pre(s)
                        SC.tag(S["cTe"], s.activation(
                            STT[:, (hp % 2) * 129:(hp % 2) * 129 + 129],
                            PB[6][:, 0:129], AF.Copy))
                        s.wait_ge(S["cTe"].h, S["cTe"].n)
                        S["cTd%d" % (hp % 2)].inc(s.dma_start(
                            out=agi_T[hp // 8][:, (hp % 8) * 129:(hp % 8 + 1) * 129],
                            in_=STT[:, (hp % 2) * 129:(hp % 2) * 129 + 129]), 16)
        with nc.named_scope("p1c"):
            block.scalar(p1c_scalar)

        def p1c_vector(v):
            v.wait_ge(S["misc"].h, 80)
            v.wait_ge(S["ascl"].h, 5)
            for i in range(4):
                VC.pre(v)
                VC.tag(S["arow"] if i == 3 else VS,
                       v.tensor_mul(CSB[:, i * 512:(i + 1) * 512],
                                    CSB[:, i * 512:(i + 1) * 512],
                                    RBC[:, (i % 2) * 512:(i % 2) * 512 + 512]))
            for u in range(64):
                h, kk, tt = dec(u)
                v.wait_ge(S["crot"].h, u + 1)
                A = RTMP[:, 0:512]
                B = RTMP[:, 512:1024]
                VC.pre(v)
                VC.tag(VS, v.tensor_mul(A, rslot(u),
                                        CSB[:, tt * 512:(tt + 1) * 512]))
                VC.pre(v)
                VC.tag(VS, v.tensor_mul(B, PB[4 + u % 2][:, 0:512],
                                        CSB[:, 1024 + tt * 512:1024 + (tt + 1) * 512]))
                VC.pre(v)
                VC.tag(VS, v.tensor_add(B, A, B))
                VC.pre(v)
                VC.tag(S["cvm0"], v.tensor_scalar_min(rslot(u), B, 0.0))
                VC.pre(v)
                VC.tag(VS, v.tensor_scalar_max(A, B, 0.0))
                v.wait_ge(S["csx"].h, u + 1)
                VC.pre(v)
                if kk == 0:
                    dst = qTs[:, h, tt * 512:(tt + 1) * 512]
                else:
                    ks = 2 * h + tt
                    dst = KTS[:, (ks % 2) * 512:(ks % 2 + 1) * 512]
                VC.tag(S["cvf"], v.tensor_add(dst, A, RTMP[:, 1024:1280].bitcast(BF16)))
        with nc.named_scope("p1c"):
            block.vector(p1c_vector)

        def p1d_gpsimd(g):
            for i in range(2):
                g.wait_ge(S["cTd0"].h, (i + 1) * 64)
                g.wait_ge(S["cTd1"].h, (i + 1) * 64)
                S["cc1"].inc(g.collective_compute(
                    "AllGather", ALU.bypass, replica_groups=RG,
                    ins=[agi_T[i].opt()], outs=[ago_T[i].opt()]), 1)
            for nm in ["down", "de", "ue", "gate_b", "up_b"]:
                S["cc1"].inc(g.collective_compute(
                    "AllGather", ALU.bypass, replica_groups=RG,
                    ins=[agi[nm].opt()], outs=[ago[nm].opt()]), 1)
        with nc.named_scope("p1d"):
            block.gpsimd(p1d_gpsimd)

        if upto == "p1":
            def dbg_sync(sync):
                sync.wait_ge(S["cc1"].h, 12)
                sync.wait_ge(S["cvf"].h, 64)
                S["dbgs"].inc(sync.dma_start(out=dbg["qT"], in_=AR1[:, 0:16384]), 16)
                S["dbgs"].inc(sync.dma_start(out=dbg["ktok"], in_=AR1[:, 16384:32768]), 16)
                S["dbgs"].inc(sync.dma_start(out=dbg["v"], in_=AR5[:, :]), 16)
                for i in range(2):
                    S["dbgs"].inc(sync.dma_start(
                        out=dbg["T"].rearrange("r p (i c) -> i r p c", i=2)[i],
                        in_=ago_T[i]), 16)
                S["dbgs"].inc(sync.dma_start(out=dbg["st"], in_=STAT[:, :]), 16)
                S["dbgs"].inc(sync.dma_start(
                    out=out[0:128, 0:1024], in_=QRAW[:, 0:1024]), 16)
                sync.wait_ge(S["dbgs"].h, 96)
            block.sync(dbg_sync)
            ctx.close()
            return nc

        for nm in ["e2w", "e2s", "wo0", "wo1", "gau0", "gau1",
                   "d2kt", "d2at", "d2dn", "d2rb", "d2nm", "d2tu", "d2sb",
                   "x2s0", "x2s1",
                   "d2ke", "d2am", "d2dr", "d2re", "d2oh", "d2su",
                   "womm", "woev",
                   "l2mm", "l2ev", "l2sc", "l2row", "l2bc", "l2h2",
                   "rw", "rmm", "rev", "rpk", "rcl", "rsh",
                   "gat0", "gat1", "emm",
                   "mg0", "mg1", "mu0", "mu1", "md0", "md1",
                   "mmm", "msil", "mev", "x2l0", "x2l1"]:
            S[nm] = Sem(ctx.enter_context(nc.semaphore(nm)))

        pselsb = PSEL[:, 0:8]
        rtselsb = PSEL[:, 8:24]
        SGS = SST[:, 0:2048]
        SGZ = SST[:, 2048:2064]
        x2bf = AR5[:, 0:8192]
        exp_tok = AR5[:, 8192:16384]
        proj = AR5[:, 0:2048].bitcast(F32)
        deT = [AR4[:, 0:2048], AR4[:, 2048:4096]]
        ueg = [AR4[:, 4096:6144], AR4[:, 6144:8192]]
        h2v = AR2[:, :].rearrange("p (c t) -> p c t", c=16)

        ar4f = AR4[:, :].bitcast(F32)
        wsf = WS[:, 8192:12288].bitcast(F32)

        def p2i_bufs(half, sl):
            if half == 0:
                return (ar4f[:, sl * 1024:sl * 1024 + 1024],
                        ar4f[:, 2048 + sl * 8:2048 + sl * 8 + 8])
            return (wsf[:, sl * 1024:sl * 1024 + 1024],
                    STT[:, sl * 8:sl * 8 + 8])

        S["e2w0"], S["e2w1"] = S["gx"], S["gld"]
        E2WB = [S["e2w0"].n, S["e2w1"].n]

        def p2i_gpsimd(g):
            for half in range(2):
                g.wait_ge(S["cc1"].h, 6 + half)
                for r in range(8):
                    idx = half * 8 + r
                    p = idx % 2
                    if idx >= 2:
                        g.wait_ge(S["e2s"].h, idx - 1)
                        g.wait_ge(S["e2w%d" % p].h, E2WB[p] + (idx // 2) * 32)
                    sbuf, zbuf = p2i_bufs(half, p)
                    S["e2w%d" % p].inc(g.dma_start(
                        out=sbuf,
                        in_=AP(ago_T[half].tensor, r * 128 * 1032,
                               [[1032, 128], [129, 8], [1, 128]])), 16)
                    S["e2w%d" % p].inc(g.dma_start(
                        out=zbuf,
                        in_=AP(ago_T[half].tensor, r * 128 * 1032 + 128,
                               [[1032, 128], [129, 8]]),
                        allow_slow_non_contiguous=True), 16)
        with nc.named_scope("p2i"):
            block.gpsimd(p2i_gpsimd)

        def p2i_vec_half(v, half):
            if True:
                for r in range(8):
                    idx = half * 8 + r
                    p = idx % 2
                    sbuf, zbuf = p2i_bufs(half, p)
                    v.wait_ge(S["e2w%d" % p].h, E2WB[p] + (idx // 2 + 1) * 32)
                    sgs_h = SGS[:, half * 1024:half * 1024 + 1024]
                    sgz_h = SGZ[:, half * 8:half * 8 + 8]
                    VC.pre(v)
                    if r == 0:
                        VC.tag(VS, v.tensor_scalar_mul(sgs_h, sbuf,
                                                       pselsb[:, 0:1]))
                        VC.pre(v)
                        VC.tag(S["e2s"], v.tensor_scalar_mul(sgz_h, zbuf,
                                                             pselsb[:, 0:1]))
                    else:
                        VC.tag(VS, v.scalar_tensor_tensor(
                            sgs_h, sbuf, pselsb[:, r:r + 1], sgs_h,
                            op0=ALU.mult, op1=ALU.add))
                        VC.pre(v)
                        VC.tag(S["e2s"], v.scalar_tensor_tensor(
                            sgz_h, zbuf, pselsb[:, r:r + 1], sgz_h,
                            op0=ALU.mult, op1=ALU.add))
        def p2i_vector(v):
            p2i_vec_half(v, 0)
        with nc.named_scope("p2i"):
            block.vector(p2i_vector)

        TRIUB = TRIU

        def emit_p2():
            def sstep(c, h):
                return 32 * (c // 2) + (h // 8) * 16 + (c % 2) * 8 + (h % 8)

            def sinv(u2):
                tt_, sp = u2 // 32, u2 % 32
                return (2 * tt_ + (sp // 8) % 2, (sp // 16) * 8 + sp % 8)

            def ntu(u2):
                return sum(1 for q in range(u2) if sinv(q)[0] < 3)

            def scan_tensor(t, c, h):
                u2 = sstep(c, h)
                if u2 == 0:
                    t.wait_ge(S["e2s"].h, 8)
                    t.wait_ge(S["ckte"].h, 32)
                    t.wait_ge(S["cTe"].h, 16)
                    t.wait_ge(S["cvf"].h, 64)
                if u2 == 16:
                    t.wait_ge(S["e2s"].h, 16)
                def emit_kts(u2k):
                    ck, hk = sinv(u2k)
                    if u2k >= 1:
                        t.wait_ge(S["d2ke"].h, u2k)
                    if u2k == 32:
                        t.wait_ge(S["l2ev"].h, 4)
                    for i in range(2):
                        mm = t.matmul(PB[6][:, i * 128:(i + 1) * 128],
                                      ktok[:, 2 * ck + i, hk], IDENT_B,
                                      start=(i == 0), stop=(i == 1))
                    S["d2kt"].inc(mm, 1)
                if u2 % 32 == 0:
                    emit_kts(u2)
                t.wait_ge(S["d2ke"].h, u2 + 1)
                if u2 >= 1:
                    t.wait_ge(S["d2am"].h, 2 * u2)
                for mb in range(2):
                    mm = t.matmul(PB[mb][:, 0:256], KTS[:, mb * 128:(mb + 1) * 128],
                                  qTs[:, h, c * 256:(c + 1) * 256],
                                  start=True, stop=True)
                S["d2at"].inc(mm, 1)
                if u2 + 1 < 64 and (u2 + 1) % 32 != 0:
                    emit_kts(u2 + 1)
                t.wait_ge(S["d2am"].h, 2 * u2 + 2)
                if u2 >= 1:
                    t.wait_ge(S["d2dr"].h, u2)
                t.wait_ge(S["d2sb"].h, u2 + 1)
                first = True
                for j in range(2):
                    for mb in range(2):
                        mm = t.matmul(PB[2][:, j:j + 1],
                                      ATT[:, mb * 256 + j * 128:mb * 256 + (j + 1) * 128],
                                      ONESB, start=first, stop=False,
                                      skip_group_check=True)
                        first = False
                    mm = t.matmul(PB[2][:, j:j + 1],
                                  qTs[:, h, c * 256 + j * 128:c * 256 + (j + 1) * 128],
                                  SBF[:, 256 + (u2 % 2):257 + (u2 % 2)],
                                  start=False, stop=(j == 1), skip_group_check=True)
                S["d2dn"].inc(mm, 1)
                if u2 >= 1:
                    t.wait_ge(S["d2oh"].h, u2)
                for mb in range(2):
                    t.matmul(PB[4][:, 0:256], vtok[:, 2 * c + mb, h * 128:(h + 1) * 128],
                             ATT[:, mb * 256:(mb + 1) * 256],
                             start=(mb == 0), stop=False, skip_group_check=True)
                mm = t.matmul(PB[4][:, 0:256], SBF[:, (u2 % 2) * 128:(u2 % 2 + 1) * 128],
                              qTs[:, h, c * 256:(c + 1) * 256],
                              start=False, stop=True, skip_group_check=True)
                S["d2nm"].inc(mm, 1)
                t.wait_ge(S["d2dr"].h, u2 + 1)
                if u2 >= 1:
                    t.wait_ge(S["d2re"].h, u2)
                for j in range(2):
                    mm = t.matmul(PB[3][:, j * 128:(j + 1) * 128],
                                  RBS[:, 258 + j:259 + j].to_broadcast([128, 128]),
                                  IDENT_F, start=(j == 0), stop=(j == 1))
                S["d2rb"].inc(mm, 1)
                if c < 3:
                    if u2 == 32:
                        t.wait_ge(S["woev"].h, 16)
                    if ntu(u2) >= 1:
                        t.wait_ge(S["d2su"].h, ntu(u2))
                    first = True
                    for mb in range(2):
                        t.matmul(PB[5][:, 0:128], ktok[:, 2 * c + mb, h],
                                 vtok[:, 2 * c + mb, h * 128:(h + 1) * 128],
                                 start=first, stop=False, skip_group_check=True)
                        first = False
                        mm = t.matmul(PB[5][:, 128:129], ktok[:, 2 * c + mb, h],
                                      ONESB, start=False, stop=(mb == 1),
                                      skip_group_check=True)
                    S["d2tu"].inc(mm, 1)

            def scan_scalar(s, c, h):
                u2 = sstep(c, h)
                s.wait_ge(S["d2kt"].h, u2 + 1)
                SC.pre(s)
                SC.tag(S["d2ke"], s.activation(KTS[:, 0:256], PB[6][:, 0:256], AF.Copy))

            def emit_sbf(v, u2s):
                hs = sinv(u2s)[1]
                VC.pre(v)
                VC.tag(VS, v.tensor_copy(SBF[:, (u2s % 2) * 128:(u2s % 2 + 1) * 128],
                                         SGS[:, hs * 128:(hs + 1) * 128]))
                VC.pre(v)
                VC.tag(S["d2sb"], v.tensor_copy(
                    SBF[:, 256 + (u2s % 2):257 + (u2s % 2)], SGZ[:, hs:hs + 1]))

            def scan_vector(v, c, h):
                u2 = sstep(c, h)
                if u2 == 0:
                    emit_sbf(v, 0)
                v.wait_ge(S["d2at"].h, u2 + 1)
                for mb in range(2):
                    VC.pre(v)
                    VC.tag(S["d2am"] if True else VS,
                           v.tensor_mul(ATT[:, mb * 256:(mb + 1) * 256],
                                        PB[mb][:, 0:256], TRIUB[mb]))
                v.wait_ge(S["d2dn"].h, u2 + 1)
                VC.pre(v)
                VC.tag(VS, v.tensor_scalar_max(RBS[:, 256:258], PB[2][:, 0:2], 1e-6))
                VC.pre(v)
                VC.tag(S["d2dr"], v.reciprocal(RBS[:, 258:260], RBS[:, 256:258]))
                if u2 + 1 < 64:
                    emit_sbf(v, u2 + 1)
                v.wait_ge(S["d2rb"].h, u2 + 1)
                VC.pre(v)
                VC.tag(S["d2re"], v.tensor_copy(RBS[:, 0:256], PB[3][:, 0:256]))
                v.wait_ge(S["d2nm"].h, u2 + 1)
                VC.pre(v)
                VC.tag(S["d2oh"], v.tensor_mul(
                    outh[:, h, (c % 2) * 256:(c % 2) * 256 + 256],
                    PB[4][:, 0:256], RBS[:, 0:256]))
                if c < 3:
                    v.wait_ge(S["d2tu"].h, ntu(u2) + 1)
                    VC.pre(v)
                    VC.tag(VS, v.tensor_add(SGS[:, h * 128:(h + 1) * 128],
                                            SGS[:, h * 128:(h + 1) * 128],
                                            PB[5][:, 0:128]))
                    VC.pre(v)
                    VC.tag(S["d2su"], v.tensor_add(SGZ[:, h:h + 1], SGZ[:, h:h + 1],
                                                   PB[5][:, 128:129]))

            S["xo0"], S["xo1"] = S["bw0"], S["bw1"]
            XOB = [S["xo0"].n, S["xo1"].n]

            def wout_sync(sync, tt):
                if tt == 0:
                    sync.wait_ge(S["cc1"].h, 3)
                    sync.wait_ge(S["cvf"].h, 64)
                    sync.wait_ge(S["cmm"].h, 64)
                for f in range(16):
                    g = tt * 16 + f
                    if g >= 2:
                        sync.wait_ge(S["womm"].h, g - 1)
                        sync.wait_ge(S["woev"].h, g - 1)
                    wv = ago["woutT"].rearrange("(c p) j -> p c j", p=128)[
                        :, :, f * 128:(f + 1) * 128]
                    S["wo%d" % (g % 2)].inc(sync.dma_start(
                        out=WS[:, (g % 2) * 2048:(g % 2) * 2048 + 2048], in_=wv), 16)
                    xv = xT.rearrange("(c p) t -> c p t", p=128)[
                        f, :, tt * 512:(tt + 1) * 512]
                    S["xo%d" % (g % 2)].inc(sync.dma_start(
                        out=XST[:, (g % 2) * 512:(g % 2) * 512 + 512], in_=xv), 16)

            def wout_tensor(t, tt):
                t.wait_ge(S["d2oh"].h, (tt + 1) * 32)
                for f in range(16):
                    g = tt * 16 + f
                    t.wait_ge(S["wo%d" % (g % 2)].h, (g // 2 + 1) * 16)
                    if g % 2 == 1:
                        t.wait_ge(S["d2su"].h, 32 + tt * 16)
                    if g >= 2:
                        t.wait_ge(S["woev"].h, g - 1)
                    for ec in range(16):
                        mm = t.matmul(PB[7 - 2 * (g % 2)][:, 0:512],
                                      WS[:, (g % 2) * 2048 + ec * 128:
                                         (g % 2) * 2048 + (ec + 1) * 128],
                                      outh[:, ec, :],
                                      start=(ec == 0), stop=(ec == 15))
                    S["womm"].inc(mm, 1)

            def wout_vector(v, tt):
                for f in range(16):
                    g = tt * 16 + f
                    v.wait_ge(S["womm"].h, g + 1)
                    v.wait_ge(S["xo%d" % (g % 2)].h, XOB[g % 2] + (g // 2 + 1) * 16)
                    if g >= 2:
                        v.wait_ge(S["x2s%d" % (g % 2)].h, (g // 2) * 16)
                    VC.pre(v)
                    VC.tag(VS, v.tensor_add(OST[:, (g % 2) * 512:(g % 2) * 512 + 512],
                                            PB[7 - 2 * (g % 2)][:, 0:512],
                                            XST[:, (g % 2) * 512:(g % 2) * 512 + 512]))
                    VC.pre(v)
                    VC.tag(S["woev"], v.tensor_copy(
                        x2bf[:, f * 512:(f + 1) * 512],
                        OST[:, (g % 2) * 512:(g % 2) * 512 + 512]))

            def wout_scalar(s, tt):
                for f in range(16):
                    g = tt * 16 + f
                    s.wait_ge(S["woev"].h, g + 1)
                    S["x2s%d" % (g % 2)].inc(s.dma_start(
                        out=AP(x2d.tensor, f * 128 * TPC + tt * 512,
                               [[TPC, 128], [1, 512]]),
                        in_=OST[:, (g % 2) * 512:(g % 2) * 512 + 512]), 16)

            for tt in range(2):
                for g4 in range(4):
                    c = 2 * tt + (g4 % 2)
                    hb = (g4 // 2) * 8

                    def mk_t(c=c, hb=hb):
                        def f(t):
                            for h in range(hb, hb + 8):
                                scan_tensor(t, c, h)
                        return f

                    def mk_s(c=c, hb=hb):
                        def f(s):
                            for h in range(hb, hb + 8):
                                scan_scalar(s, c, h)
                        return f

                    def mk_v(c=c, hb=hb):
                        def f(v):
                            for h in range(hb, hb + 8):
                                if c == 1 and h == 7:
                                    p2i_vec_half(v, 1)
                                scan_vector(v, c, h)
                        return f
                    with nc.named_scope("scan"):
                        block.tensor(mk_t())
                        block.scalar(mk_s())
                        block.vector(mk_v())
                with nc.named_scope("wout"):
                    block.sync(lambda sync, tt=tt: wout_sync(sync, tt))
                    block.tensor(lambda t, tt=tt: wout_tensor(t, tt))
                    block.vector(lambda v, tt=tt: wout_vector(v, tt))
                    block.scalar(lambda s, tt=tt: wout_scalar(s, tt))
                with nc.named_scope("ln2"):
                    emit_ln2(tt)

        def emit_ln2(tt):
            def l2_tensor(t):
                base_ev = S["l2ev"].n
                t.wait_ge(S["woev"].h, (tt + 1) * 16)
                for blk in range(4):
                    if S["l2mm"].n >= 1:
                        t.wait_ge(S["l2ev"].h, S["l2mm"].n)
                    for c in range(16):
                        lhs = x2bf[:, c * 512 + blk * 128:c * 512 + blk * 128 + 128]
                        t.matmul(PB[6][:, 0:128], lhs, lhs,
                                 start=(c == 0), stop=(c == 15))
                    for c in range(16):
                        mm = t.matmul(PB[6][:, 128:129],
                                      x2bf[:, c * 512 + blk * 128:c * 512 + blk * 128 + 128],
                                      ONESB, start=False, stop=(c == 15),
                                      skip_group_check=True)
                    S["l2mm"].inc(mm, 1)
            block.tensor(l2_tensor)

            def l2_vector(v):
                base = S["l2mm"].n - 4
                for blk in range(4):
                    v.wait_ge(S["l2mm"].h, base + blk + 1)
                    VC.pre(v)
                    VC.tag(VS, v.tensor_mul(RT2[:, 192:320], PB[6][:, 0:128], IDENT_F))
                    VC.pre(v)
                    VC.tag(VS, v.reduce_sum(STAT[:, blk:blk + 1], RT2[:, 192:320],
                                            axis=AXX))
                    VC.pre(v)
                    VC.tag(S["l2ev"], v.tensor_copy(STAT[:, 8 + blk:9 + blk],
                                                    PB[6][:, 128:129]))
                VC.pre(v)
                VC.tag(VS, v.tensor_scalar_mul(STAT[:, 16:20], STAT[:, 8:12], 1.0 / E))
                VC.pre(v)
                VC.tag(VS, v.tensor_mul(STAT[:, 40:44], STAT[:, 16:20], STAT[:, 16:20]))
                VC.pre(v)
                VC.tag(VS, v.scalar_tensor_tensor(STAT[:, 48:52], STAT[:, 0:4], 1.0 / E,
                                                  STAT[:, 40:44], op0=ALU.mult,
                                                  op1=ALU.subtract))
                VC.pre(v)
                VC.tag(S["l2sc"], v.tensor_scalar_add(STAT[:, 48:52], STAT[:, 48:52], EPS))
            block.vector(l2_vector)

            def l2_scalar(s):
                s.wait_ge(S["l2sc"].h, 3 * tt + 1)
                SC.pre(s)
                SC.tag(S["l2sc"], s.activation(STAT[:, 56:60], STAT[:, 48:52], AF.Sqrt))
            block.scalar(l2_scalar)

            def l2_vector2(v):
                v.wait_ge(S["l2sc"].h, 3 * tt + 2)
                VC.pre(v)
                VC.tag(VS, v.reciprocal(STAT[:, 40:44], STAT[:, 56:60]))
                VC.pre(v)
                VC.tag(VS, v.tensor_mul(STAT[:, 40:44], STAT[:, 40:44], STAT[:, 48:52]))
                VC.pre(v)
                VC.tag(VS, v.tensor_add(STAT[:, 40:44], STAT[:, 40:44], STAT[:, 56:60]))
                VC.pre(v)
                VC.tag(VS, v.tensor_scalar_mul(STAT[:, 40:44], STAT[:, 40:44], 0.5))
                VC.pre(v)
                VC.tag(VS, v.reciprocal(STAT[:, 24:28], STAT[:, 40:44]))
                VC.pre(v)
                VC.tag(S["l2sc"], v.tensor_scalar_mul(STAT[:, 32:36], STAT[:, 16:20],
                                                      -1.0))
            block.vector(l2_vector2)

            def l2_sync(sync):
                sync.wait_ge(S["l2sc"].h, 3 * tt + 3)
                for b in range(4):
                    S["l2row"].inc(sync.dma_start(
                        out=AP(QRAW, b * 128, [[1024, 1], [1, 128]]),
                        in_=STAT[:, 32 + b:33 + b]), 16)
                    S["l2row"].inc(sync.dma_start(
                        out=AP(QRAW, 32 * 1024 + b * 128, [[1024, 1], [1, 128]]),
                        in_=STAT[:, 24 + b:25 + b]), 16)
            block.sync(l2_sync)

            def l2_tensor2(t):
                t.wait_ge(S["l2row"].h, (tt + 1) * 128)
                t.matmul(PB[4][:, 0:512], ONES_F[0:1, 0:128],
                         AP(QRAW, 0, [[1024, 1], [1, 512]]), start=True, stop=True)
                S["l2bc"].inc(t.matmul(PB[5][:, 0:512], ONES_F[32:33, 0:128],
                                       AP(QRAW, 32 * 1024, [[1024, 1], [1, 512]]),
                                       start=True, stop=True), 1)
            block.tensor(l2_tensor2)

            def l2_vector3(v):
                v.wait_ge(S["l2bc"].h, tt + 1)
                for c in range(16):
                    VC.pre(v)
                    VC.tag(VS, v.tensor_add(KTS[:, 512:1024],
                                            x2bf[:, c * 512:(c + 1) * 512],
                                            PB[4][:, 0:512]))
                    VC.pre(v)
                    last = (c == 15)
                    VC.tag(S["l2h2"] if last else VS,
                           v.tensor_mul(h2v[:, c, tt * 512:(tt + 1) * 512],
                                        KTS[:, 512:1024], PB[5][:, 0:512]))
            block.vector(l2_vector3)

        emit_p2()

        if upto == "p2":
            def dbg2_sync(sync):
                sync.wait_ge(S["l2h2"].h, 2)
                sync.wait_ge(S["x2s0"].h, 256)
                sync.wait_ge(S["x2s1"].h, 256)
                S["dbgs"].inc(sync.dma_start(out=dbg["x2"], in_=x2d), 16)
                S["dbgs"].inc(sync.dma_start(out=dbg["h2"], in_=AR2[:, :]), 16)
                S["dbgs"].inc(sync.dma_start(out=dbg["oh"], in_=AR4[:, :]), 16)
                S["dbgs"].inc(sync.dma_start(out=dbg["qT"], in_=AR1[:, 0:16384]), 16)
                S["dbgs"].inc(sync.dma_start(out=dbg["ktok"], in_=AR1[:, 16384:32768]), 16)
                S["dbgs"].inc(sync.dma_start(out=dbg["v"], in_=AR5[:, :]), 16)
                for i in range(2):
                    S["dbgs"].inc(sync.dma_start(
                        out=dbg["T"].rearrange("r p (i c) -> i r p c", i=2)[i],
                        in_=ago_T[i]), 16)
                S["dbgs"].inc(sync.dma_start(out=dbg["st"], in_=STAT[:, :]), 16)
                S["dbgs"].inc(sync.dma_start(
                    out=out[0:128, 0:1024], in_=QRAW[:, 0:1024]), 16)
                sync.wait_ge(S["dbgs"].h, 144)
            block.sync(dbg2_sync)
            ctx.close()
            return nc

        for nm in ["rtp", "rsd", "eix", "eexp", "enm", "gid", "gbk", "edg",
                   "esl", "eac", "mdq", "msum", "so0", "so1",
                   "mdn", "mtr", "mte"]:
            S[nm] = Sem(ctx.enter_context(nc.semaphore(nm)))

        def r2(a, b):
            return QRAW[:, 512 + a:512 + b]

        smw_all = QRAW[:, 832:896]
        ewd = QRAW[:, 896:904]
        ews = QRAW[:, 904:912]
        ewp = QRAW[:, 912:920]

        def p3r_gpsimd(g):
            g.wait_ge(S["womm"].h, 32)
            wv = wqf.rearrange("(c p) j -> p c j", p=128)
            S["rw"].inc(g.dma_start(out=WS[:, 0:2048], in_=wv), 16)
            keys32a = ATT[0:64, 0:128].bitcast(F32)
            keys32b = ATT[64:128, 0:128].bitcast(F32)
            S["rw"].inc(g.dma_start(out=keys32a, in_=keysf), 16)
            S["rw"].inc(g.dma_start(out=keys32b, in_=keysf), 16)
        with nc.named_scope("p3r"):
            block.gpsimd(p3r_gpsimd)

        def p3r_tensor(t):
            t.wait_ge(S["l2h2"].h, 2)
            t.wait_ge(S["rw"].h, 48)
            for tt in range(2):
                if tt == 1:
                    t.wait_ge(S["rev"].h, 1)
                for c in range(16):
                    mm = t.matmul(PB[0][:, 0:512], WS[:, c * 128:(c + 1) * 128],
                                  h2v[:, c, tt * 512:(tt + 1) * 512],
                                  start=(c == 0), stop=(c == 15))
                S["rmm"].inc(mm, 1)
            t.wait_ge(S["rev"].h, 2)
            for sb in range(8):
                for j in range(2):
                    gidx = sb * 2 + j
                    if gidx >= 1:
                        t.wait_ge(S["rtp"].h, gidx)
                    keys32 = (ATT[0:64, 0:128] if j == 0
                              else ATT[64:128, 0:128]).bitcast(F32)
                    mm = t.matmul(PB[1][:, 0:64],
                                  proj[j * 64:(j + 1) * 64, sb * 128:(sb + 1) * 128],
                                  keys32,
                                  start=True, stop=True)
                    S["rmm"].inc(mm, 1)
        with nc.named_scope("p3r"):
            block.tensor(p3r_tensor)

        def p3r_scalar(s):
            for tt in range(2):
                s.wait_ge(S["rmm"].h, tt + 1)
                SC.pre(s)
                SC.tag(S["rev"], s.activation(proj[:, tt * 512:(tt + 1) * 512],
                                              PB[0][:, 0:512], AF.Copy))
        with nc.named_scope("p3r"):
            block.scalar(p3r_scalar)

        def p3r_vector(v):
            for sb in range(8):
                for j in range(2):
                    gidx = sb * 2 + j
                    v.wait_ge(S["rmm"].h, 2 + gidx + 1)
                    VC.pre(v)
                    VC.tag(VS, v.tensor_copy(r2(0, 64), PB[1][:, 0:64]))
                    VC.pre(v)
                    VC.tag(VS, v.max(r2(64, 72), r2(0, 64)))
                    VC.pre(v)
                    VC.tag(VS, v.max_index(RTI[:, 0:8], r2(64, 72), r2(0, 64)))
                    VC.pre(v)
                    VC.tag(VS, v.tensor_copy(
                        QRAW[:, j * 128 + sb * 16:j * 128 + sb * 16 + 8],
                        r2(64, 72)))
                    VC.pre(v)
                    VC.tag(S["rtp"], v.tensor_copy(
                        QRAW[:, j * 128 + sb * 16 + 8:j * 128 + sb * 16 + 16],
                        RTI[:, 0:8]))
        with nc.named_scope("p3r"):
            block.vector(p3r_vector)

        def p3r_gpsimd2(g):
            g.wait_ge(S["rtp"].h, 16)
            S["rpk"].inc(g.dma_start(out=agi_rt, in_=QRAW[:, 0:256]), 16)
            g.wait_ge(S["rpk"].h, 16)
            S["cc1"].inc(g.collective_compute(
                "AllGather", ALU.bypass, replica_groups=RG,
                ins=[agi_rt.opt()], outs=[ago_rt.opt()]), 1)
            g.wait_ge(S["cc1"].h, 13)
            g.wait_ge(S["x2s0"].h, 256)
            g.wait_ge(S["x2s1"].h, 256)
            for i in range(16):
                r = (i % 8) // 2 + (4 if i >= 8 else 0)
                hh = i % 2
                src = AP(ago_rt.tensor, r * 128 * 256 + hh * 64,
                         [[256, 128], [128, 2], [1, 64]])
                S["rcl"].inc(g.dma_start(out=CSB[:, i * 128:(i + 1) * 128],
                                         in_=src), 16)
        with nc.named_scope("p3r"):
            block.gpsimd(p3r_gpsimd2)

        def p3c_vec_half(v, half):
            v.wait_ge(S["rcl"].h, 256)
            for i in range(half * 8, half * 8 + 8):
                dst = QRAW[:, 0:128] if i < 8 else QRAW[:, 128:256]
                sel = rtselsb[:, i:i + 1]
                VC.pre(v)
                if i % 8 == 0:
                    VC.tag(VS, v.tensor_scalar_mul(dst, CSB[:, i * 128:(i + 1) * 128],
                                                   sel))
                else:
                    tag = S["rsh"] if (i % 8 == 7) else VS
                    VC.tag(tag, v.scalar_tensor_tensor(
                        dst, CSB[:, i * 128:(i + 1) * 128], sel, dst,
                        op0=ALU.mult, op1=ALU.add))

        def p3c_gpsimd2(g):
            g.wait_ge(S["rsh"].h, 2)
            for qy in range(2):
                for j in range(2):
                    for tbp in range(2):
                        src = AP(QRAW, (64 * tbp) * 1024 + qy * 128 + j * 64,
                                 [[1024, 64], [16, 4], [1, 16]])
                        dst = AP(QRAW, j * 1024 + 256 + qy * 128 + 16 * tbp,
                                 [[2048, 64], [32, 4], [1, 16]])
                        S["rsd"].inc(g.dma_start(out=dst, in_=src), 16)
        with nc.named_scope("p3c"):
            block.gpsimd(p3c_gpsimd2)

        def p3a_vec_tb(v, tb):
            if tb == 0:
                v.wait_ge(S["rsd"].h, 128)
            m2 = AP(QRAW, 512 + 192, [[1024, 128], [8, 8], [1, 8]])
            if True:
                sxs = AP(QRAW, 256 + tb * 16, [[1024, 128], [1, 8], [0, 8]])
                sys_ = AP(QRAW, 384 + tb * 16, [[1024, 128], [0, 8], [1, 8]])
                VC.pre(v)
                VC.tag(VS, v.tensor_tensor(r2(64, 128), sxs, sys_, op=ALU.add))
                VC.pre(v)
                VC.tag(VS, v.max(r2(128, 136), r2(64, 128)))
                VC.pre(v)
                VC.tag(VS, v.max_index(RTI[:, 0:8], r2(128, 136), r2(64, 128)))
                VC.pre(v)
                VC.tag(VS, v.tensor_copy(r2(136, 144), RTI[:, 0:8]))
                VC.pre(v)
                VC.tag(VS, v.reduce_max(r2(144, 145), r2(128, 136), axis=AXX))
                VC.pre(v)
                VC.tag(S["enm"], v.tensor_scalar_mul(r2(145, 146), r2(144, 145),
                                                     -1.0))
                v.wait_ge(S["eexp"].h, tb + 1)
                VC.pre(v)
                VC.tag(VS, v.reciprocal(r2(161, 162), r2(160, 161)))
                VC.pre(v)
                VC.tag(VS, v.tensor_scalar_mul(smw_all[:, tb * 8:(tb + 1) * 8],
                                               r2(152, 160), r2(161, 162)))
                posv = AP(QRAW, 512 + 136, [[1024, 128], [1, 8], [0, 8]])
                c8pv = AP(CF, 912, [[1024, 128], [0, 8], [1, 8]])
                VC.pre(v)
                VC.tag(VS, v.tensor_tensor(r2(192, 256), posv, c8pv, op=ALU.is_ge))
                VC.pre(v)
                VC.tag(VS, v.reduce_sum(r2(170, 178), m2, axis=AXX))
                VC.pre(v)
                VC.tag(VS, v.scalar_tensor_tensor(r2(162, 170), r2(170, 178),
                                                  -8.0, r2(136, 144),
                                                  op0=ALU.mult, op1=ALU.add))
                pdv = AP(QRAW, 512 + 170, [[1024, 128], [1, 8], [0, 8]])
                c1v = AP(CF, 904, [[1024, 128], [0, 8], [1, 8]])
                ixv = AP(QRAW, 256 + tb * 16 + 8, [[1024, 128], [0, 8], [1, 8]])
                VC.pre(v)
                VC.tag(VS, v.tensor_tensor(r2(192, 256), pdv, c1v, op=ALU.is_equal))
                VC.pre(v)
                VC.tag(VS, v.tensor_tensor(r2(192, 256), m2, ixv, op=ALU.mult))
                VC.pre(v)
                VC.tag(VS, v.reduce_sum(r2(178, 186), m2, axis=AXX))
                pmv = AP(QRAW, 512 + 162, [[1024, 128], [1, 8], [0, 8]])
                iyv = AP(QRAW, 384 + tb * 16 + 8, [[1024, 128], [0, 8], [1, 8]])
                VC.pre(v)
                VC.tag(VS, v.tensor_tensor(r2(192, 256), pmv, c1v, op=ALU.is_equal))
                VC.pre(v)
                VC.tag(VS, v.tensor_tensor(r2(192, 256), m2, iyv, op=ALU.mult))
                VC.pre(v)
                VC.tag(VS, v.reduce_sum(r2(186, 194), m2, axis=AXX))
                VC.pre(v)
                VC.tag(S["eix"], v.scalar_tensor_tensor(
                    r2(208, 216), r2(178, 186), 64.0, r2(186, 194),
                    op0=ALU.mult, op1=ALU.add))
                v.wait_ge(S["gid"].h, (tb + 1) * 16)

        def p3a_scl_tb(s, tb):
            s.wait_ge(S["enm"].h, tb + 1)
            SC.pre(s)
            SC.tag(S["eexp"], s.activation(r2(152, 160), r2(128, 136), AF.Exp,
                                           bias=r2(145, 146),
                                           accum_out=r2(160, 161)))

        def p3a_gpsimd(g):
            for tb in range(8):
                g.wait_ge(S["eix"].h, tb + 1)
                S["gid"].inc(g.dma_start(
                    out=AP(eidxd.tensor, tb * 1024, [[8, 128], [1, 8]]),
                    in_=r2(208, 216)), 16)
                g.wait_ge(S["gid"].h, (tb + 1) * 16)
        with nc.named_scope("p3a"):
            block.gpsimd(p3a_gpsimd)

        gate16v = [ago["gate_a"].rearrange("(a p) j -> a p j", p=128),
                   ago["gate_b"].rearrange("(a p) j -> a p j", p=128)]
        up16v = [ago["up_a"].rearrange("(a p) j -> a p j", p=128),
                 ago["up_b"].rearrange("(a p) j -> a p j", p=128)]
        down16v = ago["down"].rearrange("(a p) j -> a p j", p=128)

        S["mgt"] = S["womm"]
        MGTB = S["mgt"].n
        GAUB = [S["gau0"].n, S["gau1"].n]
        S["md2"], S["md3"] = S["wo0"], S["x2s0"]
        MDB = [S["md0"].n, S["md1"].n, S["md2"].n, S["md3"].n]

        def exp_gpsimd(g, tb):
            if tb >= 2:
                g.wait_ge(S["gat0"].h, (tb - 1) * 4 * 16)
                g.wait_ge(S["gat1"].h, (tb - 1) * 4 * 16)
                g.wait_ge(S["gau0"].h, GAUB[0] + (tb - 1) * 4 * 16)
                g.wait_ge(S["gau1"].h, GAUB[1] + (tb - 1) * 4 * 16)
            if tb == 0:
                g.wait_ge(S["idxz"].h, 1)
            src = AP(eidxd.tensor, tb * 1024, [[8, 16], [1, 8], [128, 8]])
            for pr in range(4):
                S["gbk"].inc(g.dma_start(
                    out=IDX[pr * 16:(pr + 1) * 16,
                            (tb % 2) * 64:(tb % 2) * 64 + 64], in_=src,
                    allow_slow_non_contiguous=True), 16)
            g.wait_ge(S["gbk"].h, S["gbk"].n)
            for k in range(8):
                gs = tb * 8 + k
                if gs >= 2:
                    g.wait_ge(S["emm"].h, gs - 1)
                    g.wait_ge(S["gat%d" % (gs % 2)].h, (gs // 2) * 16)
                S["gat%d" % (gs % 2)].inc(g.dma_gather(
                    deT[gs % 2].rearrange("p (a c) -> p a c", a=16),
                    ago["de"],
                    IDX[:, (tb % 2) * 64 + k * 8:(tb % 2) * 64 + k * 8 + 8],
                    128, 128, E, elem_step=E, transpose=True, queue_num=0), 16)
                if gs >= 2:
                    g.wait_ge(S["eac"].h, gs - 1)
                    g.wait_ge(S["gau%d" % (gs % 2)].h, GAUB[gs % 2] + (gs // 2) * 16)
                S["gau%d" % (gs % 2)].inc(g.dma_gather(
                    ueg[gs % 2].rearrange("p (a c) -> p a c", a=1),
                    ago["ue"],
                    IDX[:, (tb % 2) * 64 + k * 8:(tb % 2) * 64 + k * 8 + 8],
                    128, 128, E, elem_step=E, queue_num=1), 16)

        def mlp_sync(sync, tt):
            for ib in range(4):
                for f2 in range(16):
                    g2 = tt * 64 + ib * 16 + f2
                    if g2 == 0:
                        sync.wait_ge(S["rmm"].h, 2)
                        sync.wait_ge(S["cc1"].h, 5)
                    if g2 >= 2:
                        sync.wait_ge(S["mmm"].h, g2 - 1)
                    io = ib * 16 + f2
                    if io == 32 and tt == 0:
                        sync.wait_ge(S["cc1"].h, 12)
                    S["mg%d" % (g2 % 2)].inc(sync.dma_start(
                        out=WS[:, (g2 % 2) * 2048:(g2 % 2) * 2048 + 2048],
                        in_=gate16v[io // 32][:, :, (io % 32) * 128:
                                              (io % 32 + 1) * 128].rearrange(
                            "a p j -> p a j")), 16)
                    if g2 in (64, 65):
                        sync.wait_ge(S["mdq"].h, 64)
                    S["mu%d" % (g2 % 2)].inc(sync.dma_start(
                        out=WS[:, 4096 + (g2 % 2) * 2048:4096 + (g2 % 2) * 2048 + 2048],
                        in_=up16v[io // 32][:, :, (io % 32) * 128:
                                            (io % 32 + 1) * 128].rearrange(
                            "a p j -> p a j")), 16)
            for f in range(16):
                fd = tt * 16 + f
                if f >= 4 and (fd - 4) % 2 == 1:
                    sync.wait_ge(S["mte"].h, fd - 3)
                    S["so1"].inc(sync.dma_start(
                        out=AP(out.tensor, (tt * 512) * E + (f - 4) * 128,
                               [[E, 128], [128 * E, 4], [1, 128]]),
                        in_=RBC[:, ((fd - 4) % 2) * 512:((fd - 4) % 2) * 512 + 512]),
                        16)
                for qq in (0, 2):
                    qg = fd * 4 + qq
                    sl = qg % 4
                    if qg == 0:
                        sync.wait_ge(S["cc1"].h, 8)
                    if qg < tt * 64 + 4:
                        sync.wait_ge(S["mmm"].h, (tt + 1) * 64)
                    if qg >= 4:
                        sync.wait_ge(S["mdq"].h, qg - 3)
                        sync.wait_ge(S["md%d" % sl].h, MDB[sl] + (qg // 4) * 16)
                    S["md%d" % sl].inc(sync.dma_start(
                        out=WS[:, 4096 + sl * 2048:4096 + sl * 2048 + 2048],
                        in_=down16v[qq * 16:(qq + 1) * 16, :,
                                    f * 128:(f + 1) * 128].rearrange(
                            "a p j -> p a j")), 16)
            for f in (13, 15):
                fd = tt * 16 + f
                sync.wait_ge(S["mte"].h, fd + 1)
                S["so1"].inc(sync.dma_start(
                    out=AP(out.tensor, (tt * 512) * E + f * 128,
                           [[E, 128], [128 * E, 4], [1, 128]]),
                    in_=RBC[:, 512:1024]), 16)

        EXPB = [25, 2]

        def mlp_tensor(t, tt):
            if tt == 0:
                t.wait_ge(S["rmm"].h, 18)
                t.wait_ge(S["rtp"].h, 16)
            for ib in range(4):
                for f2 in range(16):
                    g2 = tt * 64 + ib * 16 + f2
                    io = ib * 16 + f2
                    t.wait_ge(S["mg%d" % (g2 % 2)].h, (g2 // 2 + 1) * 16)
                    if g2 >= 1:
                        t.wait_ge(S["msil"].h, g2)
                    for c in range(16):
                        mmg = t.matmul(PB[0][:, 0:512],
                                       WS[:, (g2 % 2) * 2048 + c * 128:
                                          (g2 % 2) * 2048 + (c + 1) * 128],
                                       h2v[:, c, tt * 512:(tt + 1) * 512],
                                       start=(c == 0), stop=(c == 15))
                    S["mgt"].inc(mmg, 1)
                    if EXPB[tt] <= io < EXPB[tt] + 32:
                        e = io - EXPB[tt]
                        tb = 4 * tt + e // 8
                        k = e % 8
                        gs = tb * 8 + k
                        t.wait_ge(S["gat%d" % (gs % 2)].h, (gs // 2 + 1) * 16)
                        if gs >= 2:
                            t.wait_ge(S["edg"].h, gs - 1)
                        for f in range(16):
                            mm = t.matmul(PB[2 + gs % 2][:, 0:128],
                                          h2v[:, f, tb * 128:(tb + 1) * 128],
                                          deT[gs % 2][:, f * 128:(f + 1) * 128],
                                          start=(f == 0), stop=(f == 15))
                        S["emm"].inc(mm, 1)
                    t.wait_ge(S["mu%d" % (g2 % 2)].h, (g2 // 2 + 1) * 16)
                    if g2 >= 1:
                        t.wait_ge(S["mev"].h, g2)
                    for c in range(16):
                        mm = t.matmul(PB[1][:, 0:512],
                                      WS[:, 4096 + (g2 % 2) * 2048 + c * 128:
                                         4096 + (g2 % 2) * 2048 + (c + 1) * 128],
                                      h2v[:, c, tt * 512:(tt + 1) * 512],
                                      start=(c == 0), stop=(c == 15))
                    S["mmm"].inc(mm, 1)
            t.wait_ge(S["eac"].h, (tt * 4 + 4) * 8)
            for f in range(16):
                fd = tt * 16 + f
                t.wait_ge(S["mev"].h, tt * 64 + 64)
                if fd >= 1:
                    t.wait_ge(S["msum"].h, fd)
                first = True
                for qq in range(4):
                    qg = fd * 4 + qq
                    t.wait_ge(S["md%d" % (qg % 4)].h,
                              MDB[qg % 4] + (qg // 4 + 1) * 16)
                    for c in range(16):
                        io = qq * 16 + c
                        mm = t.matmul(PB[4 + fd % 2][:, 0:512],
                                      WS[:, 4096 + (qg % 4) * 2048 + c * 128:
                                         4096 + (qg % 4) * 2048 + (c + 1) * 128],
                                      AR1[:, io * 512:(io + 1) * 512],
                                      start=first, stop=False,
                                      skip_group_check=True)
                        first = False
                    S["mdq"].inc(mm, 1)
                for mt in range(4):
                    mm = t.matmul(PB[4 + fd % 2][:, mt * 128:(mt + 1) * 128],
                                  exp_tok[:, mt * 2048 + f * 128:mt * 2048 + (f + 1) * 128],
                                  IDENT_B, start=False, stop=(mt == 3),
                                  skip_group_check=True)
                S["mdn"].inc(mm, 1)
                t.wait_ge(S["msum"].h, fd + 1)
                if fd >= 1:
                    t.wait_ge(S["mte"].h, fd)
                for mt in range(4):
                    mm = t.matmul(PB[6][:, mt * 128:(mt + 1) * 128],
                                  OST[:, (fd % 2) * 512 + mt * 128:
                                      (fd % 2) * 512 + (mt + 1) * 128],
                                  IDENT_F, start=(mt == 0), stop=(mt == 3))
                S["mtr"].inc(mm, 1)

        def mlp_scalar(s, tt):
            for ib in range(4):
                for f2 in range(16):
                    g2 = tt * 64 + ib * 16 + f2
                    io = ib * 16 + f2
                    s.wait_ge(S["mgt"].h, MGTB + g2 + 1)
                    if g2 >= 1:
                        s.wait_ge(S["mev"].h, g2)
                    SC.pre(s)
                    SC.tag(S["msil"], s.activation(KTS[:, 512:1024], PB[0][:, 0:512],
                                                   AF.Silu))
                    if tt == 0 and 14 <= io < 22:
                        p3a_scl_tb(s, io - 14)
                    if EXPB[tt] <= io < EXPB[tt] + 32:
                        e = io - EXPB[tt]
                        gs = 32 * tt + e
                        k = e % 8
                        s.wait_ge(S["edg"].h, gs + 1)
                        SC.pre(s)
                        SC.tag(S["esl"], s.activation(ews[:, k:k + 1],
                                                      ewd[:, k:k + 1], AF.Silu))
            for f in range(16):
                fd = tt * 16 + f
                for qq in (1, 3):
                    qg = fd * 4 + qq
                    sl = qg % 4
                    if qg == 1:
                        s.wait_ge(S["cc1"].h, 8)
                    if qg < tt * 64 + 4:
                        s.wait_ge(S["mmm"].h, (tt + 1) * 64)
                    if qg >= 4:
                        s.wait_ge(S["mdq"].h, qg - 3)
                        s.wait_ge(S["md%d" % sl].h, MDB[sl] + (qg // 4) * 16)
                    S["md%d" % sl].inc(s.dma_start(
                        out=WS[:, 4096 + sl * 2048:4096 + sl * 2048 + 2048],
                        in_=down16v[qq * 16:(qq + 1) * 16, :,
                                    f * 128:(f + 1) * 128].rearrange(
                            "a p j -> p a j")), 16)
                if fd >= 2:
                    s.wait_ge(S["msum"].h, fd - 1)
                S["x2l%d" % (fd % 2)].inc(s.dma_start(
                    out=XST[:, (fd % 2) * 512:(fd % 2) * 512 + 512],
                    in_=AP(x2d.tensor, f * 128 * TPC + tt * 512,
                           [[TPC, 128], [1, 512]])), 16)
                if fd % 2 == 0:
                    s.wait_ge(S["mte"].h, fd + 1)
                    S["so0"].inc(s.dma_start(
                        out=AP(out.tensor, (tt * 512) * E + f * 128,
                               [[E, 128], [128 * E, 4], [1, 128]]),
                        in_=RBC[:, 0:512]), 16)

        def _exp_acc(v, tt, d):
            tb = 4 * tt + d // 8
            kd = d % 8
            gd = tb * 8 + kd
            v.wait_ge(S["esl"].h, gd + 1)
            VC.pre(v)
            VC.tag(VS, v.tensor_mul(ewp[:, kd:kd + 1], ews[:, kd:kd + 1],
                                    smw_all[:, tb * 8 + kd:tb * 8 + kd + 1]))
            v.wait_ge(S["gau%d" % (gd % 2)].h, GAUB[gd % 2] + (gd // 2 + 1) * 16)
            slot = exp_tok[:, (tb % 4) * 2048:(tb % 4 + 1) * 2048]
            VC.pre(v)
            if kd == 0:
                VC.tag(S["eac"], v.tensor_scalar_mul(
                    slot, ueg[gd % 2][:, 0:2048], ewp[:, 0:1]))
            else:
                VC.tag(S["eac"], v.scalar_tensor_tensor(
                    slot, ueg[gd % 2][:, 0:2048], ewp[:, kd:kd + 1], slot,
                    op0=ALU.mult, op1=ALU.add))

        def mlp_vector(v, tt):
            for ib in range(4):
                for f2 in range(16):
                    g2 = tt * 64 + ib * 16 + f2
                    io = ib * 16 + f2
                    v.wait_ge(S["mmm"].h, g2 + 1)
                    v.wait_ge(S["msil"].h, g2 + 1)
                    VC.pre(v)
                    VC.tag(S["mev"], v.tensor_mul(AR1[:, io * 512:(io + 1) * 512],
                                                  PB[1][:, 0:512], KTS[:, 512:1024]))
                    if tt == 0 and io in (12, 13):
                        p3c_vec_half(v, io - 12)
                    if tt == 0 and 14 <= io < 22:
                        p3a_vec_tb(v, io - 14)
                    if EXPB[tt] <= io < EXPB[tt] + 32:
                        e = io - EXPB[tt]
                        tb = 4 * tt + e // 8
                        k = e % 8
                        gs = tb * 8 + k
                        v.wait_ge(S["emm"].h, gs + 1)
                        VC.pre(v)
                        VC.tag(VS, v.tensor_mul(r2(192, 320),
                                                PB[2 + gs % 2][:, 0:128], IDENT_F))
                        VC.pre(v)
                        VC.tag(S["edg"], v.reduce_sum(ewd[:, k:k + 1],
                                                      r2(192, 320), axis=AXX))
                        if e >= 1:
                            _exp_acc(v, tt, e - 1)
                        if e == 31:
                            _exp_acc(v, tt, 31)
            for f in range(16):
                fd = tt * 16 + f
                v.wait_ge(S["mdn"].h, fd + 1)
                v.wait_ge(S["x2l%d" % (fd % 2)].h, (fd // 2 + 1) * 16)
                VC.pre(v)
                VC.tag(S["msum"], v.tensor_add(
                    OST[:, (fd % 2) * 512:(fd % 2) * 512 + 512],
                    PB[4 + fd % 2][:, 0:512],
                    XST[:, (fd % 2) * 512:(fd % 2) * 512 + 512]))
                v.wait_ge(S["mtr"].h, fd + 1)
                if fd >= 2:
                    v.wait_ge(S["so%d" % (fd % 2)].h, (fd // 2) * 16)
                VC.pre(v)
                VC.tag(S["mte"], v.tensor_copy(
                    RBC[:, (fd % 2) * 512:(fd % 2) * 512 + 512], PB[6][:, 0:512]))

        for tt in range(2):
            with nc.named_scope("experts"):
                for tb in range(4 * tt, 4 * tt + 4):
                    block.gpsimd(lambda g, tb=tb: exp_gpsimd(g, tb))
            with nc.named_scope("mlp"):
                block.sync(lambda sync, tt=tt: mlp_sync(sync, tt))
                block.tensor(lambda t, tt=tt: mlp_tensor(t, tt))
                block.scalar(lambda s, tt=tt: mlp_scalar(s, tt))
                block.vector(lambda v, tt=tt: mlp_vector(v, tt))

        def p3_final(s):
            s.wait_ge(S["so0"].h, 256)
            s.wait_ge(S["so1"].h, 256)
        block.scalar(p3_final)

    ctx.close()
    return nc


import ml_dtypes
BF = ml_dtypes.bfloat16


def make_consts():
    cf = np.zeros((128, 1024), np.float32)
    cf[:, 0:128] = np.eye(128, dtype=np.float32)
    P = np.zeros((128, 128), np.float32)
    for i in range(64):
        P[2 * i, 2 * i + 1] = -1.0
        P[2 * i + 1, 2 * i] = 1.0
    cf[:, 128:256] = P.T
    p = np.arange(128)[:, None]
    l = np.arange(256)[None, :]
    cf[:, 256:512] = (p <= l).astype(np.float32)
    cf[:, 512:768] = (p + 128 <= l).astype(np.float32)
    cf[:, 768:896] = 1.0
    cf[:, 896:904] = (np.arange(8) * 8.0)[None, :]
    cf[:, 904:912] = np.arange(8.0)[None, :]
    cf[:, 912:920] = (np.arange(8.0)[None, :] + 1) * 8.0
    cb = np.zeros((128, 132), BF)
    cb[:, 0:128] = np.eye(128, dtype=np.float32).astype(BF)
    cb[:, 128:129] = np.float32(1.0)
    return cf, cb


def rope_tables(s0):
    inv = 1.0 / 10000.0 ** (np.arange(0, D, 2, dtype=np.float32) / D)
    invf = np.concatenate([inv, inv])
    pos = np.arange(s0, s0 + TPC, dtype=np.float32)
    ang = invf[:, None] * pos[None, :]
    cs = np.zeros((128, 2048), np.float32)
    cs[:, 0:1024] = np.cos(ang)
    cs[:, 1024:2048] = np.sin(ang)
    return cs


def stage_inputs(inp):
    x = np.asarray(inp["L0_attn_x"], np.float32)
    Wqkv = np.asarray(inp["L0_attn_W_qkv"], np.float32)
    Wout = np.asarray(inp["L0_attn_W_out"], np.float32)
    Wq = np.asarray(inp["L0_ffn_W_q"], np.float32)
    keys = np.asarray(inp["L0_ffn_keys"], np.float32)
    de = np.asarray(inp["L0_ffn_down_embed"], np.float32)
    ue = np.asarray(inp["L0_ffn_up_embed"], np.float32)
    gw = np.asarray(inp["L0_ffn_gate_w"], np.float32)
    uw = np.asarray(inp["L0_ffn_up_w"], np.float32)
    dw = np.asarray(inp["L0_ffn_down_w"], np.float32)

    wqkvT = np.ascontiguousarray(Wqkv.T)
    woutT = np.ascontiguousarray(Wout.T)
    cf, cb = make_consts()

    in_maps = []
    for c in range(8):
        b, q = c // 4, c % 4
        s0 = TPC * q
        xt = np.ascontiguousarray(x[b, s0:s0 + TPC].T)
        psel = np.zeros((128, 8), np.float32)
        for r in range(8):
            if r // 4 == b and r % 4 < q:
                psel[:, r] = 1.0
        rtsel = np.zeros((128, 16), np.float32)
        r0 = 2 * b + q // 2
        hsel = q % 2
        rtsel[:, 2 * r0 + hsel] = 1.0
        rtsel[:, 8 + 2 * r0 + hsel] = 1.0
        m = {
            "xT": xt,
            "cs": rope_tables(s0),
            "cf": cf,
            "cbi": cb,
            "wqf": Wq,
            "keysf": np.ascontiguousarray(keys[b]),
            "pself": psel,
            "rtself": rtsel,
            "sh_wqkvT": np.ascontiguousarray(wqkvT[c * 256:(c + 1) * 256]),
            "sh_woutT": np.ascontiguousarray(woutT[c * 256:(c + 1) * 256]),
            "sh_gate": np.ascontiguousarray(gw[c * 256:(c + 1) * 256]),
            "sh_up": np.ascontiguousarray(uw[c * 256:(c + 1) * 256]),
            "sh_down": np.ascontiguousarray(dw[c * 1024:(c + 1) * 1024]),
            "sh_de": np.ascontiguousarray(de[c * 512:(c + 1) * 512]),
            "sh_ue": np.ascontiguousarray(ue[c * 512:(c + 1) * 512]),
        }
        in_maps.append(m)
    return in_maps


from concourse.bass_utils import run_bass_kernel_spmd

_NC_CACHE = {}


def kernel(**inputs):
    in_maps = stage_inputs(inputs)
    if "nc" not in _NC_CACHE:
        nc = build({"upto": "full"})
        nc.finalize()
        _NC_CACHE["nc"] = nc
    nc = _NC_CACHE["nc"]
    res = run_bass_kernel_spmd(nc, in_maps, core_ids=list(range(8)))
    full = np.zeros((2, 4096, 2048), np.float32)
    for c in range(8):
        b, q = c // 4, c % 4
        full[b, q * 1024:(q + 1) * 1024] = np.asarray(res.results[c]["out"])
    return full


# version 22 (final)
# speedup vs baseline: 1.0500x; 1.0263x over previous
import numpy as np
from contextlib import ExitStack

import concourse.bacc as bacc
import concourse.mybir as mybir
from concourse.bass import AP

F32 = mybir.dt.float32
BF16 = mybir.dt.bfloat16
I16 = mybir.dt.int16
U32 = mybir.dt.uint32
AF = mybir.ActivationFunctionType
ALU = mybir.AluOpType
AXX = mybir.AxisListType.X

NCORE = 8
TPC = 1024
NH = 16
D = 128
E = 2048
INTER = 8192
NEXP = 4096
EPS = 1e-5
RG = [list(range(NCORE))]

WNAMES = [("wqkvT", E, 3 * E), ("woutT", E, E), ("gate", E, INTER),
          ("up", E, INTER), ("down", INTER, E), ("de", NEXP, E), ("ue", NEXP, E)]


class Sem:
    def __init__(self, h):
        self.h = h
        self.n = 0

    def inc(self, inst, k):
        inst.then_inc(self.h, k)
        self.n += k
        return inst


def build(cfg=None):
    cfg = cfg or {}
    upto = cfg.get("upto", "full")

    nc = bacc.Bacc("TRN2", target_bir_lowering=False, num_swdge_queues=2)

    xT = nc.dram_tensor("xT", [E, TPC], F32, kind="ExternalInput").ap()
    cs = nc.dram_tensor("cs", [128, 2048], F32, kind="ExternalInput").ap()
    cf = nc.dram_tensor("cf", [128, 1024], F32, kind="ExternalInput").ap()
    cbi = nc.dram_tensor("cbi", [128, 132], BF16, kind="ExternalInput").ap()
    wqf = nc.dram_tensor("wqf", [E, 128], F32, kind="ExternalInput").ap()
    keysf = nc.dram_tensor("keysf", [64, 64], F32, kind="ExternalInput").ap()
    pself = nc.dram_tensor("pself", [128, 8], F32, kind="ExternalInput").ap()
    rtself = nc.dram_tensor("rtself", [128, 16], F32, kind="ExternalInput").ap()
    sh = {}
    for nm, rows, cols in WNAMES:
        sh[nm] = nc.dram_tensor("sh_" + nm, [rows // 8, cols], F32,
                                kind="ExternalInput").ap()

    out = nc.dram_tensor("out", [TPC, E], F32, kind="ExternalOutput").ap()
    dbg = {}
    if upto in ("p1", "p2"):
        dbg["qT"] = nc.dram_tensor("dbg_qT", [128, NH * TPC], BF16, kind="ExternalOutput").ap()
        dbg["ktok"] = nc.dram_tensor("dbg_ktok", [128, 8 * NH * 128], BF16, kind="ExternalOutput").ap()
        dbg["v"] = nc.dram_tensor("dbg_v", [128, 8 * E], BF16, kind="ExternalOutput").ap()
        dbg["T"] = nc.dram_tensor("dbg_T", [8, 128, NH * 129], F32, kind="ExternalOutput").ap()
        dbg["st"] = nc.dram_tensor("dbg_st", [128, 64], F32, kind="ExternalOutput").ap()
    if upto == "p2":
        dbg["x2"] = nc.dram_tensor("dbg_x2", [E, TPC], F32, kind="ExternalOutput").ap()
        dbg["oh"] = nc.dram_tensor("dbg_oh", [128, 8192], BF16, kind="ExternalOutput").ap()
        dbg["h2"] = nc.dram_tensor("dbg_h2", [128, 16384], BF16, kind="ExternalOutput").ap()

    agi = {}
    ago = {}
    AGNAMES = [("wv", E, E), ("wqk", E, 2 * E), ("woutT", E, E),
               ("gate_a", E, INTER // 2), ("gate_b", E, INTER // 2),
               ("up_a", E, INTER // 2), ("up_b", E, INTER // 2),
               ("down", INTER, E), ("de", NEXP, E), ("ue", NEXP, E)]
    for nm, rows, cols in AGNAMES:
        agi[nm] = nc.dram_tensor("agi_" + nm, [rows // 8, cols], BF16, kind="Internal").ap()
        ago[nm] = nc.dram_tensor("ago_" + nm, [rows, cols], BF16, kind="Internal",
                                 addr_space="Shared").ap()
    agi_T = [nc.dram_tensor("agi_T%d" % i, [128, 8 * 129], F32, kind="Internal").ap()
             for i in range(2)]
    ago_T = [nc.dram_tensor("ago_T%d" % i, [8, 128, 8 * 129], F32, kind="Internal",
                            addr_space="Shared").ap() for i in range(2)]
    agi_rt = nc.dram_tensor("agi_rt", [128, 256], F32, kind="Internal").ap()
    ago_rt = nc.dram_tensor("ago_rt", [8, 128, 256], F32, kind="Internal",
                            addr_space="Shared").ap()
    x2d = nc.dram_tensor("x2d", [E, TPC], F32, kind="Internal").ap()
    eidxd = nc.dram_tensor("eidxd", [8, 1024], I16, kind="Internal").ap()

    ctx = ExitStack()

    def sb(name, shape, dt):
        return ctx.enter_context(nc.sbuf_tensor(name, shape, dt))

    AR1 = sb("AR1", [128, 32768], BF16)
    AR2 = sb("AR2", [128, 16384], BF16)
    AR4 = sb("AR4", [128, 8192], BF16)
    AR5 = sb("AR5", [128, 16384], BF16)
    WS = sb("WS", [128, 12288], BF16)
    CSB = sb("CSB", [128, 2048], F32)
    CF = sb("CF", [128, 1024], F32)
    CB = sb("CB", [128, 132], BF16)
    STT = sb("STT", [128, 258], F32)
    SST = sb("SST", [128, 2064], F32)
    SBF = sb("SBF", [128, 258], BF16)
    STAT = sb("STAT", [128, 64], F32)
    RBC = sb("RBC", [128, 1024], F32)
    QRAW = sb("QRAW", [128, 1024], F32)
    RTMP = sb("RTMP", [128, 1280], F32)
    KTS = sb("KTS", [128, 1024], BF16)
    ATT = sb("ATT", [128, 512], BF16)
    RBS = sb("RBS", [128, 260], F32)
    RTI = sb("RTI", [128, 8], U32)
    IDX = sb("IDX", [128, 128], I16)
    PSEL = sb("PSEL", [128, 24], F32)
    OST = CSB[:, 0:1024]
    XST = CSB[:, 1024:2048]
    RTB = QRAW[:, 0:512]
    RT2 = QRAW[:, 512:832]
    EWS = QRAW[:, 832:928]
    PRJ = None

    PB = [ctx.enter_context(nc.psum_tensor("pb%d" % i, [128, 512], F32))
          for i in range(8)]

    semnames = ["vself", "sself", "gx", "gld", "gst", "idxz", "cc1",
                "amm", "aev", "ascl", "arow", "abc", "actr",
                "bw0", "bw1", "bmm", "bev",
                "cw0", "cw1", "cmm", "crot", "cse", "csx", "cvm0", "cvf",
                "ckt", "ckte", "cTg", "cTe", "cTd0", "cTd1",
                "misc"]
    S = {nm: Sem(ctx.enter_context(nc.semaphore(nm))) for nm in semnames}
    if cfg.get("upto") != "full":
        S["dbgs"] = Sem(ctx.enter_context(nc.semaphore("dbgs")))
    VS, SS = S["vself"], S["sself"]

    class Chain:
        def __init__(self):
            self.last = None

        def pre(self, e):
            if self.last:
                e.wait_ge(self.last[0], self.last[1])

        def tag(self, sem, inst, k=1):
            sem.inc(inst, k)
            self.last = (sem.h, sem.n)
            return inst

    VC = Chain()
    SC = Chain()

    IDENT_F = CF[:, 0:128]
    PMAT = CF[:, 128:256]
    TRIU = [CF[:, 256:512], CF[:, 512:768]]
    ONES_F = CF[:, 768:896]
    C8 = CF[:, 896:904]
    C1 = CF[:, 904:912]
    IDENT_B = CB[:, 0:128]
    ONESB = CB[:, 128:129]

    qTs = AR1[:, 0:16384].rearrange("p (h t) -> p h t", h=NH)
    ktok = AR1[:, 16384:32768].rearrange("p (m h d) -> p m h d", m=8, h=NH)
    xc = AR2[:, :].rearrange("p (tt c t) -> p tt c t", tt=2, c=16)
    vtok = AR5[:, :].rearrange("p (m e) -> p m e", m=8)
    outh = AR4[:, :].rearrange("p (h l) -> p h l", h=NH)

    with nc.Block() as block:
        def p0_sync(sync):
            S["misc"].inc(sync.dma_start(out=CSB[:, :], in_=cs), 16)
            S["misc"].inc(sync.dma_start(out=CF[:, :], in_=cf), 16)
            S["misc"].inc(sync.dma_start(out=CB[:, :], in_=cbi), 16)
            S["misc"].inc(sync.dma_start(out=PSEL[:, 0:8], in_=pself), 16)
            S["misc"].inc(sync.dma_start(out=PSEL[:, 8:24], in_=rtself), 16)
        with nc.named_scope("p0"):
            block.sync(p0_sync)

        def p0_gpsimd(g):
            S["idxz"].inc(g.memset(IDX[:, :], 0), 1)
            for tt in range(2):
                xv = xT.rearrange("(c p) t -> p c t", p=128)[:, :, tt * 512:(tt + 1) * 512]
                S["gx"].inc(g.dma_start(out=xc[:, tt], in_=xv), 16)
            par = 0
            CONV = [("wv", "wqkvT", 2 * E, E, E), ("wqk", "wqkvT", 0, E, 2 * E),
                    ("woutT", "woutT", 0, E, E),
                    ("gate_a", "gate", 0, E, INTER // 2),
                    ("gate_b", "gate", INTER // 2, E, INTER // 2),
                    ("up_a", "up", 0, E, INTER // 2),
                    ("up_b", "up", INTER // 2, E, INTER // 2),
                    ("down", "down", 0, INTER, E),
                    ("de", "de", 0, NEXP, E), ("ue", "ue", 0, NEXP, E)]
            for nm, snm, c0, rows, cols in CONV:
                nrp = rows // 8 // 128
                src_v = sh[snm].rearrange("(a p) c -> a p c", p=128)
                dst_v = agi[nm].rearrange("(a p) c -> a p c", p=128)
                for a in range(nrp):
                    pos = 0
                    while pos < cols:
                        take = min(4096, cols - pos)
                        if par >= 2:
                            g.wait_ge(S["gst"].h, par * 16)
                        S["gld"].inc(g.dma_start(
                            out=AR4[:, (par % 2) * 4096:(par % 2) * 4096 + take],
                            in_=src_v[a, :, c0 + pos:c0 + pos + take]), 16)
                        g.wait_ge(S["gld"].h, (par + 1) * 16)
                        S["gst"].inc(g.dma_start(
                            out=dst_v[a, :, pos:pos + take],
                            in_=AR4[:, (par % 2) * 4096:(par % 2) * 4096 + take]), 16)
                        par += 1
                        pos += take
                if nm in ("wv", "wqk", "woutT"):
                    g.wait_ge(S["gst"].h, S["gst"].n)
                    S["cc1"].inc(g.collective_compute(
                        "AllGather", ALU.bypass, replica_groups=RG,
                        ins=[agi[nm].opt()], outs=[ago[nm].opt()]), 1)
            g.wait_ge(S["gst"].h, S["gst"].n)
            for nm in ["gate_a", "up_a"]:
                S["cc1"].inc(g.collective_compute(
                    "AllGather", ALU.bypass, replica_groups=RG,
                    ins=[agi[nm].opt()], outs=[ago[nm].opt()]), 1)
        with nc.named_scope("p0conv"):
            block.gpsimd(p0_gpsimd)

        def p1a_tensor(t):
            t.wait_ge(S["gx"].h, 32)
            t.wait_ge(S["misc"].h, 80)
            for blk in range(8):
                tt, tb = blk // 4, blk % 4
                if blk >= 1:
                    t.wait_ge(S["aev"].h, blk)
                for c in range(16):
                    lhs = xc[:, tt, c, tb * 128:tb * 128 + 128]
                    t.matmul(PB[6][:, 0:128], lhs, lhs,
                             start=(c == 0), stop=(c == 15))
                for c in range(16):
                    mm = t.matmul(PB[6][:, 128:129],
                                  xc[:, tt, c, tb * 128:tb * 128 + 128],
                                  ONESB, start=False, stop=(c == 15),
                                  skip_group_check=True)
                S["amm"].inc(mm, 1)
        with nc.named_scope("p1a"):
            block.tensor(p1a_tensor)

        def p1a_vector(v):
            for blk in range(8):
                v.wait_ge(S["amm"].h, blk + 1)
                VC.pre(v)
                VC.tag(VS, v.tensor_mul(QRAW[:, 0:128], PB[6][:, 0:128], IDENT_F))
                VC.pre(v)
                VC.tag(VS, v.reduce_sum(STAT[:, blk:blk + 1], QRAW[:, 0:128], axis=AXX))
                VC.pre(v)
                VC.tag(S["aev"], v.tensor_copy(STAT[:, 8 + blk:9 + blk],
                                               PB[6][:, 128:129]))
            VC.pre(v)
            VC.tag(VS, v.tensor_scalar_mul(STAT[:, 16:24], STAT[:, 8:16], 1.0 / E))
            VC.pre(v)
            VC.tag(VS, v.tensor_mul(STAT[:, 40:48], STAT[:, 16:24], STAT[:, 16:24]))
            VC.pre(v)
            VC.tag(VS, v.scalar_tensor_tensor(STAT[:, 48:56], STAT[:, 0:8], 1.0 / E,
                                              STAT[:, 40:48], op0=ALU.mult,
                                              op1=ALU.subtract))
            VC.pre(v)
            VC.tag(S["ascl"], v.tensor_scalar_add(STAT[:, 48:56], STAT[:, 48:56], EPS))
        block.vector(p1a_vector)

        def p1a_scalar(s):
            s.wait_ge(S["ascl"].h, 1)
            SC.pre(s)
            SC.tag(S["ascl"], s.activation(STAT[:, 56:64], STAT[:, 48:56], AF.Sqrt))
        block.scalar(p1a_scalar)

        def p1a_vector2(v):
            v.wait_ge(S["ascl"].h, 2)
            VC.pre(v)
            VC.tag(VS, v.reciprocal(STAT[:, 40:48], STAT[:, 56:64]))
            VC.pre(v)
            VC.tag(VS, v.tensor_mul(STAT[:, 40:48], STAT[:, 40:48], STAT[:, 48:56]))
            VC.pre(v)
            VC.tag(VS, v.tensor_add(STAT[:, 40:48], STAT[:, 40:48], STAT[:, 56:64]))
            VC.pre(v)
            VC.tag(VS, v.tensor_scalar_mul(STAT[:, 40:48], STAT[:, 40:48], 0.5))
            VC.pre(v)
            VC.tag(VS, v.reciprocal(STAT[:, 24:32], STAT[:, 40:48]))
            VC.pre(v)
            VC.tag(S["ascl"], v.tensor_scalar_mul(STAT[:, 32:40], STAT[:, 16:24],
                                                  -1.0))
        block.vector(p1a_vector2)

        def p1a_sync2(sync):
            sync.wait_ge(S["ascl"].h, 3)
            for b in range(8):
                S["arow"].inc(sync.dma_start(
                    out=AP(QRAW, b * 128, [[1024, 1], [1, 128]]),
                    in_=STAT[:, 32 + b:33 + b]), 16)
                S["arow"].inc(sync.dma_start(
                    out=AP(RTMP, b * 128, [[1280, 1], [1, 128]]),
                    in_=STAT[:, 24 + b:25 + b]), 16)
        block.sync(p1a_sync2)

        def p1a_tensor2(t):
            t.wait_ge(S["arow"].h, 256)
            t.wait_ge(S["aev"].h, 8)
            for tt in range(2):
                t.matmul(PB[7 - tt][:, 0:512], ONES_F[0:1, 0:128],
                         AP(QRAW, tt * 512, [[1024, 1], [1, 512]]),
                         start=True, stop=True)
                S["abc"].inc(t.matmul(PB[5 - tt][:, 0:512], ONES_F[0:1, 0:128],
                                      AP(RTMP, tt * 512, [[1280, 1], [1, 512]]),
                                      start=True, stop=True), 1)
        block.tensor(p1a_tensor2)

        def p1a_scalar2(s):
            for tt in range(2):
                s.wait_ge(S["abc"].h, tt + 1)
                SC.pre(s)
                SC.tag(SS, s.activation(KTS[:, tt * 512:tt * 512 + 512],
                                        PB[7 - tt][:, 0:512], AF.Copy))
                SC.pre(s)
                SC.tag(S["ascl"], s.activation(RBC[:, tt * 512:tt * 512 + 512],
                                               PB[5 - tt][:, 0:512], AF.Copy))
        block.scalar(p1a_scalar2)

        def p1a_vector3(v):
            v.wait_ge(S["ascl"].h, 5)
            for tt in range(2):
                for c in range(16):
                    VC.pre(v)
                    last = (tt == 1 and c == 15)
                    VC.tag(S["actr"] if last else VS,
                           v.tensor_add(xc[:, tt, c], xc[:, tt, c],
                                        KTS[:, tt * 512:tt * 512 + 512]))
        block.vector(p1a_vector3)

        wv16v = ago["wv"].rearrange("(c p) j -> p c j", p=128)
        wqk16v = ago["wqk"].rearrange("(c p) j -> p c j", p=128)

        def p1b_sync(sync):
            sync.wait_ge(S["cc1"].h, 1)
            for eg in range(8):
                if eg >= 2:
                    sync.wait_ge(S["bev"].h, (eg - 1) * 8)
                wv = wv16v[:, :, eg * 256:(eg + 1) * 256]
                S["bw%d" % (eg % 2)].inc(sync.dma_start(
                    out=WS[:, (eg % 2) * 4096:(eg % 2) * 4096 + 4096], in_=wv), 16)
        with nc.named_scope("p1b"):
            block.sync(p1b_sync)

        def p1b_tensor(t):
            t.wait_ge(S["actr"].h, 1)
            for eg in range(8):
                t.wait_ge(S["bw%d" % (eg % 2)].h, (eg // 2 + 1) * 16)
                for mb in range(8):
                    it = eg * 8 + mb
                    if it >= 2:
                        t.wait_ge(S["bev"].h, it - 1)
                    tt, tb = mb // 4, mb % 4
                    for c in range(16):
                        mm = t.matmul(PB[it % 2][:, 0:256],
                                      xc[:, tt, c, tb * 128:tb * 128 + 128],
                                      WS[:, (eg % 2) * 4096 + c * 256:
                                         (eg % 2) * 4096 + (c + 1) * 256],
                                      start=(c == 0), stop=(c == 15))
                    S["bmm"].inc(mm, 1)
        with nc.named_scope("p1b"):
            block.tensor(p1b_tensor)

        def p1b_scalar(s):
            for eg in range(8):
                for mb in range(8):
                    it = eg * 8 + mb
                    s.wait_ge(S["bmm"].h, it + 1)
                    SC.pre(s)
                    SC.tag(S["bev"], s.activation(
                        vtok[:, mb, eg * 256:(eg + 1) * 256], PB[it % 2][:, 0:256],
                        AF.Copy, scale=STAT[:, 24 + mb:25 + mb]))
        with nc.named_scope("p1b"):
            block.scalar(p1b_scalar)

        def dec(u):
            return u // 4, (u // 2) % 2, u % 2

        def p1c_sync(sync):
            sync.wait_ge(S["cc1"].h, 2)
            for h in range(NH):
                if h < 2:
                    sync.wait_ge(S["bmm"].h, 64)
                else:
                    sync.wait_ge(S["cmm"].h, (h - 1) * 4)
                for kk in range(2):
                    wv = wqk16v[:, :, kk * E + h * 128:kk * E + (h + 1) * 128]
                    S["cw%d" % (h % 2)].inc(sync.dma_start(
                        out=WS[:, (h % 2) * 4096 + kk * 2048:
                               (h % 2) * 4096 + (kk + 1) * 2048], in_=wv), 16)
        with nc.named_scope("p1c"):
            block.sync(p1c_sync)

        def rslot(u):
            tn = QRAW if u % 4 < 2 else RBC
            return tn[:, (u % 2) * 512:(u % 2) * 512 + 512]
        CSRB = S["arow"].n

        def p1c_tensor(t):
            t.wait_ge(S["actr"].h, 1)
            t.wait_ge(S["bev"].h, 64)
            for u in range(64 + 4):
                if 0 <= u - 3 < 64:
                    uT = u - 3
                    hT, kkT, ttT = dec(uT)
                    if kkT == 1:
                        ksT = 2 * hT + ttT
                        t.wait_ge(S["ckte"].h, ksT + 1)
                        if hT == 0 and ttT == 0:
                            t.wait_ge(S["bev"].h, 64)
                        if ttT == 0 and hT >= 1:
                            t.wait_ge(S["cTe"].h, hT)
                        for i in range(4):
                            mb = ttT * 4 + i
                            first = (ttT == 0 and i == 0)
                            t.matmul(PB[6][:, 0:128], ktok[:, mb, hT],
                                     vtok[:, mb, hT * 128:(hT + 1) * 128],
                                     start=first, stop=False,
                                     skip_group_check=True)
                            mm = t.matmul(PB[6][:, 128:129], ktok[:, mb, hT],
                                          ONESB, start=False,
                                          stop=(ttT == 1 and i == 3),
                                          skip_group_check=True)
                        S["cTg"].inc(mm, 1)
                if 0 <= u - 2 < 64:
                    uk = u - 2
                    h2_, kk2, tt2 = dec(uk)
                    if kk2 == 1:
                        ks = 2 * h2_ + tt2
                        t.wait_ge(S["cvf"].h, uk + 1)
                        if ks >= 1:
                            t.wait_ge(S["ckte"].h, ks)
                        for i in range(4):
                            mm = t.matmul(PB[7][:, i * 128:(i + 1) * 128],
                                          KTS[:, (ks % 2) * 512 + i * 128:
                                              (ks % 2) * 512 + (i + 1) * 128],
                                          IDENT_B, start=(i == 0), stop=(i == 3))
                        S["ckt"].inc(mm, 1)
                if u < 64:
                    h, kk, tt = dec(u)
                    if kk == 0 and tt == 0:
                        t.wait_ge(S["cw%d" % (h % 2)].h, (h // 2 + 1) * 32)
                    if u >= 4:
                        t.wait_ge(S["cse"].h, u - 3)
                    base = (h % 2) * 4096 + kk * 2048
                    for c in range(16):
                        mm = t.matmul(PB[u % 4][:, 0:512],
                                      WS[:, base + c * 128:base + (c + 1) * 128],
                                      xc[:, tt, c],
                                      start=(c == 0), stop=(c == 15))
                    S["cmm"].inc(mm, 1)
                if 0 <= u - 1 < 64:
                    up = u - 1
                    t.wait_ge(S["cse"].h, up + 1)
                    if up >= 2:
                        t.wait_ge(S["cvm0"].h, up - 1)
                    S["crot"].inc(t.matmul(
                        PB[4 + up % 2][:, 0:512], PMAT,
                        rslot(up),
                        start=True, stop=True), 1)
        with nc.named_scope("p1c"):
            block.tensor(p1c_tensor)

        def p1c_scalar(s):
            for u in range(64 + 1):
                if u < 64:
                    h, kk, tt = dec(u)
                    s.wait_ge(S["cmm"].h, u + 1)
                    if u in (2, 3):
                        s.wait_ge(S["arow"].h, CSRB + 1)
                    if u >= 4:
                        s.wait_ge(S["cvm0"].h, u - 3)
                    SC.pre(s)
                    SC.tag(S["cse"], s.activation(
                        rslot(u), PB[u % 4][:, 0:512], AF.Copy))
                if u >= 1:
                    up = u - 1
                    s.wait_ge(S["cvm0"].h, up + 1)
                    SC.pre(s)
                    SC.tag(S["csx"], s.activation(
                        RTMP[:, 1024:1280].bitcast(BF16),
                        rslot(up), AF.Exp))
                    hp, kkp, ttp = dec(up)
                    if kkp == 1:
                        ks = 2 * hp + ttp
                        s.wait_ge(S["ckt"].h, ks + 1)
                        for i in range(4):
                            SC.pre(s)
                            SC.tag(S["ckte"] if i == 3 else SS,
                                   s.activation(ktok[:, ttp * 4 + i, hp],
                                                PB[7][:, i * 128:(i + 1) * 128],
                                                AF.Copy))
                    if kkp == 1 and ttp == 1:
                        s.wait_ge(S["cTg"].h, 2 * hp + 2)
                        if hp >= 2:
                            s.wait_ge(S["cTd%d" % (hp % 2)].h, (hp // 2) * 16)
                        SC.pre(s)
                        SC.tag(S["cTe"], s.activation(
                            STT[:, (hp % 2) * 129:(hp % 2) * 129 + 129],
                            PB[6][:, 0:129], AF.Copy))
                        s.wait_ge(S["cTe"].h, S["cTe"].n)
                        S["cTd%d" % (hp % 2)].inc(s.dma_start(
                            out=agi_T[hp // 8][:, (hp % 8) * 129:(hp % 8 + 1) * 129],
                            in_=STT[:, (hp % 2) * 129:(hp % 2) * 129 + 129]), 16)
        with nc.named_scope("p1c"):
            block.scalar(p1c_scalar)

        def p1c_vector(v):
            v.wait_ge(S["misc"].h, 80)
            v.wait_ge(S["ascl"].h, 5)
            for i in range(4):
                VC.pre(v)
                VC.tag(S["arow"] if i == 3 else VS,
                       v.tensor_mul(CSB[:, i * 512:(i + 1) * 512],
                                    CSB[:, i * 512:(i + 1) * 512],
                                    RBC[:, (i % 2) * 512:(i % 2) * 512 + 512]))
            for u in range(64):
                h, kk, tt = dec(u)
                v.wait_ge(S["crot"].h, u + 1)
                A = RTMP[:, 0:512]
                B = RTMP[:, 512:1024]
                VC.pre(v)
                VC.tag(VS, v.tensor_mul(A, rslot(u),
                                        CSB[:, tt * 512:(tt + 1) * 512]))
                VC.pre(v)
                VC.tag(VS, v.tensor_mul(B, PB[4 + u % 2][:, 0:512],
                                        CSB[:, 1024 + tt * 512:1024 + (tt + 1) * 512]))
                VC.pre(v)
                VC.tag(VS, v.tensor_add(B, A, B))
                VC.pre(v)
                VC.tag(S["cvm0"], v.tensor_scalar_min(rslot(u), B, 0.0))
                VC.pre(v)
                VC.tag(VS, v.tensor_scalar_max(A, B, 0.0))
                v.wait_ge(S["csx"].h, u + 1)
                VC.pre(v)
                if kk == 0:
                    dst = qTs[:, h, tt * 512:(tt + 1) * 512]
                else:
                    ks = 2 * h + tt
                    dst = KTS[:, (ks % 2) * 512:(ks % 2 + 1) * 512]
                VC.tag(S["cvf"], v.tensor_add(dst, A, RTMP[:, 1024:1280].bitcast(BF16)))
        with nc.named_scope("p1c"):
            block.vector(p1c_vector)

        def p1d_gpsimd(g):
            for i in range(2):
                g.wait_ge(S["cTd0"].h, (i + 1) * 64)
                g.wait_ge(S["cTd1"].h, (i + 1) * 64)
                S["cc1"].inc(g.collective_compute(
                    "AllGather", ALU.bypass, replica_groups=RG,
                    ins=[agi_T[i].opt()], outs=[ago_T[i].opt()]), 1)
            for nm in ["down", "de", "ue", "gate_b", "up_b"]:
                S["cc1"].inc(g.collective_compute(
                    "AllGather", ALU.bypass, replica_groups=RG,
                    ins=[agi[nm].opt()], outs=[ago[nm].opt()]), 1)
        with nc.named_scope("p1d"):
            block.gpsimd(p1d_gpsimd)

        if upto == "p1":
            def dbg_sync(sync):
                sync.wait_ge(S["cc1"].h, 12)
                sync.wait_ge(S["cvf"].h, 64)
                S["dbgs"].inc(sync.dma_start(out=dbg["qT"], in_=AR1[:, 0:16384]), 16)
                S["dbgs"].inc(sync.dma_start(out=dbg["ktok"], in_=AR1[:, 16384:32768]), 16)
                S["dbgs"].inc(sync.dma_start(out=dbg["v"], in_=AR5[:, :]), 16)
                for i in range(2):
                    S["dbgs"].inc(sync.dma_start(
                        out=dbg["T"].rearrange("r p (i c) -> i r p c", i=2)[i],
                        in_=ago_T[i]), 16)
                S["dbgs"].inc(sync.dma_start(out=dbg["st"], in_=STAT[:, :]), 16)
                S["dbgs"].inc(sync.dma_start(
                    out=out[0:128, 0:1024], in_=QRAW[:, 0:1024]), 16)
                sync.wait_ge(S["dbgs"].h, 96)
            block.sync(dbg_sync)
            ctx.close()
            return nc

        for nm in ["e2w", "e2s", "wo0", "wo1", "gau0", "gau1",
                   "d2kt", "d2at", "d2dn", "d2rb", "d2nm", "d2tu", "d2sb",
                   "x2s0", "x2s1",
                   "d2ke", "d2am", "d2dr", "d2re", "d2oh", "d2su",
                   "womm", "woev",
                   "l2mm", "l2ev", "l2sc", "l2row", "l2bc", "l2h2",
                   "rw", "rmm", "rev", "rpk", "rcl", "rsh",
                   "gat0", "gat1", "emm",
                   "mg0", "mg1", "mu0", "mu1", "md0", "md1",
                   "mmm", "msil", "mev", "x2l0", "x2l1"]:
            S[nm] = Sem(ctx.enter_context(nc.semaphore(nm)))

        pselsb = PSEL[:, 0:8]
        rtselsb = PSEL[:, 8:24]
        SGS = SST[:, 0:2048]
        SGZ = SST[:, 2048:2064]
        x2bf = AR5[:, 0:8192]
        exp_tok = AR5[:, 8192:16384]
        proj = AR5[:, 0:2048].bitcast(F32)
        deT = [AR4[:, 0:2048], AR4[:, 2048:4096]]
        ueg = [AR4[:, 4096:6144], AR4[:, 6144:8192]]
        h2v = AR2[:, :].rearrange("p (c t) -> p c t", c=16)

        ar4f = AR4[:, :].bitcast(F32)
        wsf = WS[:, 8192:12288].bitcast(F32)

        def p2i_bufs(half, sl):
            if half == 0:
                return (ar4f[:, sl * 1024:sl * 1024 + 1024],
                        ar4f[:, 2048 + sl * 8:2048 + sl * 8 + 8])
            return (wsf[:, sl * 1024:sl * 1024 + 1024],
                    STT[:, sl * 8:sl * 8 + 8])

        S["e2w0"], S["e2w1"] = S["gx"], S["gld"]
        E2WB = [S["e2w0"].n, S["e2w1"].n]

        def p2i_gpsimd(g):
            for half in range(2):
                g.wait_ge(S["cc1"].h, 6 + half)
                for r in range(8):
                    idx = half * 8 + r
                    p = idx % 2
                    if idx >= 2:
                        g.wait_ge(S["e2s"].h, idx - 1)
                        g.wait_ge(S["e2w%d" % p].h, E2WB[p] + (idx // 2) * 32)
                    sbuf, zbuf = p2i_bufs(half, p)
                    S["e2w%d" % p].inc(g.dma_start(
                        out=sbuf,
                        in_=AP(ago_T[half].tensor, r * 128 * 1032,
                               [[1032, 128], [129, 8], [1, 128]])), 16)
                    S["e2w%d" % p].inc(g.dma_start(
                        out=zbuf,
                        in_=AP(ago_T[half].tensor, r * 128 * 1032 + 128,
                               [[1032, 128], [129, 8]]),
                        allow_slow_non_contiguous=True), 16)
        with nc.named_scope("p2i"):
            block.gpsimd(p2i_gpsimd)

        def p2i_vec_half(v, half):
            if True:
                for r in range(8):
                    idx = half * 8 + r
                    p = idx % 2
                    sbuf, zbuf = p2i_bufs(half, p)
                    v.wait_ge(S["e2w%d" % p].h, E2WB[p] + (idx // 2 + 1) * 32)
                    sgs_h = SGS[:, half * 1024:half * 1024 + 1024]
                    sgz_h = SGZ[:, half * 8:half * 8 + 8]
                    VC.pre(v)
                    if r == 0:
                        VC.tag(VS, v.tensor_scalar_mul(sgs_h, sbuf,
                                                       pselsb[:, 0:1]))
                        VC.pre(v)
                        VC.tag(S["e2s"], v.tensor_scalar_mul(sgz_h, zbuf,
                                                             pselsb[:, 0:1]))
                    else:
                        VC.tag(VS, v.scalar_tensor_tensor(
                            sgs_h, sbuf, pselsb[:, r:r + 1], sgs_h,
                            op0=ALU.mult, op1=ALU.add))
                        VC.pre(v)
                        VC.tag(S["e2s"], v.scalar_tensor_tensor(
                            sgz_h, zbuf, pselsb[:, r:r + 1], sgz_h,
                            op0=ALU.mult, op1=ALU.add))
        def p2i_vector(v):
            p2i_vec_half(v, 0)
        with nc.named_scope("p2i"):
            block.vector(p2i_vector)

        TRIUB = TRIU

        def emit_p2():
            def sstep(c, h):
                return 32 * (c // 2) + (h // 8) * 16 + (c % 2) * 8 + (h % 8)

            def sinv(u2):
                tt_, sp = u2 // 32, u2 % 32
                return (2 * tt_ + (sp // 8) % 2, (sp // 16) * 8 + sp % 8)

            def ntu(u2):
                return sum(1 for q in range(u2) if sinv(q)[0] < 3)

            def scan_tensor(t, c, h):
                u2 = sstep(c, h)
                if u2 == 0:
                    t.wait_ge(S["e2s"].h, 8)
                    t.wait_ge(S["ckte"].h, 32)
                    t.wait_ge(S["cTe"].h, 16)
                    t.wait_ge(S["cvf"].h, 64)
                if u2 == 16:
                    t.wait_ge(S["e2s"].h, 16)
                def emit_kts(u2k):
                    ck, hk = sinv(u2k)
                    if u2k >= 1:
                        t.wait_ge(S["d2ke"].h, u2k)
                    if u2k == 32:
                        t.wait_ge(S["l2ev"].h, 4)
                    for i in range(2):
                        mm = t.matmul(PB[6][:, i * 128:(i + 1) * 128],
                                      ktok[:, 2 * ck + i, hk], IDENT_B,
                                      start=(i == 0), stop=(i == 1))
                    S["d2kt"].inc(mm, 1)
                if u2 % 32 == 0:
                    emit_kts(u2)
                t.wait_ge(S["d2ke"].h, u2 + 1)
                if u2 >= 1:
                    t.wait_ge(S["d2am"].h, 2 * u2)
                for mb in range(2):
                    mm = t.matmul(PB[mb][:, 0:256], KTS[:, mb * 128:(mb + 1) * 128],
                                  qTs[:, h, c * 256:(c + 1) * 256],
                                  start=True, stop=True)
                S["d2at"].inc(mm, 1)
                if u2 + 1 < 64 and (u2 + 1) % 32 != 0:
                    emit_kts(u2 + 1)
                t.wait_ge(S["d2am"].h, 2 * u2 + 2)
                if u2 >= 1:
                    t.wait_ge(S["d2dr"].h, u2)
                t.wait_ge(S["d2sb"].h, u2 + 1)
                first = True
                for j in range(2):
                    for mb in range(2):
                        mm = t.matmul(PB[2][:, j:j + 1],
                                      ATT[:, mb * 256 + j * 128:mb * 256 + (j + 1) * 128],
                                      ONESB, start=first, stop=False,
                                      skip_group_check=True)
                        first = False
                    mm = t.matmul(PB[2][:, j:j + 1],
                                  qTs[:, h, c * 256 + j * 128:c * 256 + (j + 1) * 128],
                                  SBF[:, 256 + (u2 % 2):257 + (u2 % 2)],
                                  start=False, stop=(j == 1), skip_group_check=True)
                S["d2dn"].inc(mm, 1)
                if u2 >= 1:
                    t.wait_ge(S["d2oh"].h, u2)
                for mb in range(2):
                    t.matmul(PB[4][:, 0:256], vtok[:, 2 * c + mb, h * 128:(h + 1) * 128],
                             ATT[:, mb * 256:(mb + 1) * 256],
                             start=(mb == 0), stop=False, skip_group_check=True)
                mm = t.matmul(PB[4][:, 0:256], SBF[:, (u2 % 2) * 128:(u2 % 2 + 1) * 128],
                              qTs[:, h, c * 256:(c + 1) * 256],
                              start=False, stop=True, skip_group_check=True)
                S["d2nm"].inc(mm, 1)
                t.wait_ge(S["d2dr"].h, u2 + 1)
                if u2 >= 1:
                    t.wait_ge(S["d2re"].h, u2)
                for j in range(2):
                    mm = t.matmul(PB[3][:, j * 128:(j + 1) * 128],
                                  RBS[:, 258 + j:259 + j].to_broadcast([128, 128]),
                                  IDENT_F, start=(j == 0), stop=(j == 1))
                S["d2rb"].inc(mm, 1)
                if c < 3:
                    if u2 == 32:
                        t.wait_ge(S["woev"].h, 16)
                    if ntu(u2) >= 1:
                        t.wait_ge(S["d2su"].h, ntu(u2))
                    first = True
                    for mb in range(2):
                        t.matmul(PB[5][:, 0:128], ktok[:, 2 * c + mb, h],
                                 vtok[:, 2 * c + mb, h * 128:(h + 1) * 128],
                                 start=first, stop=False, skip_group_check=True)
                        first = False
                        mm = t.matmul(PB[5][:, 128:129], ktok[:, 2 * c + mb, h],
                                      ONESB, start=False, stop=(mb == 1),
                                      skip_group_check=True)
                    S["d2tu"].inc(mm, 1)

            def scan_scalar(s, c, h):
                u2 = sstep(c, h)
                s.wait_ge(S["d2kt"].h, u2 + 1)
                SC.pre(s)
                SC.tag(S["d2ke"], s.activation(KTS[:, 0:256], PB[6][:, 0:256], AF.Copy))

            def emit_sbf(v, u2s):
                hs = sinv(u2s)[1]
                VC.pre(v)
                VC.tag(VS, v.tensor_copy(SBF[:, (u2s % 2) * 128:(u2s % 2 + 1) * 128],
                                         SGS[:, hs * 128:(hs + 1) * 128]))
                VC.pre(v)
                VC.tag(S["d2sb"], v.tensor_copy(
                    SBF[:, 256 + (u2s % 2):257 + (u2s % 2)], SGZ[:, hs:hs + 1]))

            def scan_vector(v, c, h):
                u2 = sstep(c, h)
                if u2 == 0:
                    emit_sbf(v, 0)
                v.wait_ge(S["d2at"].h, u2 + 1)
                for mb in range(2):
                    VC.pre(v)
                    VC.tag(S["d2am"] if True else VS,
                           v.tensor_mul(ATT[:, mb * 256:(mb + 1) * 256],
                                        PB[mb][:, 0:256], TRIUB[mb]))
                v.wait_ge(S["d2dn"].h, u2 + 1)
                VC.pre(v)
                VC.tag(VS, v.tensor_scalar_max(RBS[:, 256:258], PB[2][:, 0:2], 1e-6))
                VC.pre(v)
                VC.tag(S["d2dr"], v.reciprocal(RBS[:, 258:260], RBS[:, 256:258]))
                if u2 + 1 < 64:
                    emit_sbf(v, u2 + 1)
                v.wait_ge(S["d2rb"].h, u2 + 1)
                VC.pre(v)
                VC.tag(S["d2re"], v.tensor_copy(RBS[:, 0:256], PB[3][:, 0:256]))
                v.wait_ge(S["d2nm"].h, u2 + 1)
                VC.pre(v)
                VC.tag(S["d2oh"], v.tensor_mul(
                    outh[:, h, (c % 2) * 256:(c % 2) * 256 + 256],
                    PB[4][:, 0:256], RBS[:, 0:256]))
                if c < 3:
                    v.wait_ge(S["d2tu"].h, ntu(u2) + 1)
                    VC.pre(v)
                    VC.tag(VS, v.tensor_add(SGS[:, h * 128:(h + 1) * 128],
                                            SGS[:, h * 128:(h + 1) * 128],
                                            PB[5][:, 0:128]))
                    VC.pre(v)
                    VC.tag(S["d2su"], v.tensor_add(SGZ[:, h:h + 1], SGZ[:, h:h + 1],
                                                   PB[5][:, 128:129]))

            S["xo0"], S["xo1"] = S["bw0"], S["bw1"]
            S["xo2"], S["xo3"] = S["misc"], S["arow"]
            XOB = [S["xo0"].n, S["xo1"].n, S["xo2"].n, S["xo3"].n]
            S["wo2"], S["wo3"] = S["cw0"], S["cw1"]
            WOB = [S["wo0"].n, S["wo1"].n, S["wo2"].n, S["wo3"].n]

            def xst4(g):
                if g % 4 < 2:
                    return XST[:, (g % 4) * 512:(g % 4) * 512 + 512]
                return RTMP[:, (g % 4 - 2) * 512:(g % 4 - 2) * 512 + 512]

            def wout_sync(sync, tt):
                if tt == 0:
                    sync.wait_ge(S["cc1"].h, 3)
                    sync.wait_ge(S["cvf"].h, 64)
                    sync.wait_ge(S["cmm"].h, 64)
                for f in range(16):
                    g = tt * 16 + f
                    if g >= 4:
                        sync.wait_ge(S["womm"].h, g - 3)
                        sync.wait_ge(S["woev"].h, g - 3)
                        sync.wait_ge(S["wo%d" % (g % 4)].h,
                                     WOB[g % 4] + (g // 4) * 16)
                    wv = ago["woutT"].rearrange("(c p) j -> p c j", p=128)[
                        :, :, f * 128:(f + 1) * 128]
                    S["wo%d" % (g % 4)].inc(sync.dma_start(
                        out=WS[:, (g % 4) * 2048:(g % 4) * 2048 + 2048], in_=wv), 16)
                    xv = xT.rearrange("(c p) t -> c p t", p=128)[
                        f, :, tt * 512:(tt + 1) * 512]
                    if g >= 4:
                        sync.wait_ge(S["xo%d" % (g % 4)].h,
                                     XOB[g % 4] + (g // 4) * 16)
                    S["xo%d" % (g % 4)].inc(sync.dma_start(
                        out=xst4(g), in_=xv), 16)

            def wout_tensor(t, tt):
                t.wait_ge(S["d2oh"].h, (tt + 1) * 32)
                for f in range(16):
                    g = tt * 16 + f
                    t.wait_ge(S["wo%d" % (g % 4)].h,
                              WOB[g % 4] + (g // 4 + 1) * 16)
                    if g % 2 == 1:
                        t.wait_ge(S["d2su"].h, 32 + tt * 16)
                    if g >= 2:
                        t.wait_ge(S["woev"].h, g - 1)
                    for ec in range(16):
                        mm = t.matmul(PB[7 - 2 * (g % 2)][:, 0:512],
                                      WS[:, (g % 4) * 2048 + ec * 128:
                                         (g % 4) * 2048 + (ec + 1) * 128],
                                      outh[:, ec, :],
                                      start=(ec == 0), stop=(ec == 15))
                    S["womm"].inc(mm, 1)

            def wout_vector(v, tt):
                for f in range(16):
                    g = tt * 16 + f
                    v.wait_ge(S["womm"].h, g + 1)
                    v.wait_ge(S["xo%d" % (g % 4)].h, XOB[g % 4] + (g // 4 + 1) * 16)
                    if g >= 2:
                        v.wait_ge(S["x2s%d" % (g % 2)].h, (g // 2) * 16)
                    VC.pre(v)
                    VC.tag(VS, v.tensor_add(OST[:, (g % 2) * 512:(g % 2) * 512 + 512],
                                            PB[7 - 2 * (g % 2)][:, 0:512],
                                            xst4(g)))
                    VC.pre(v)
                    VC.tag(S["woev"], v.tensor_copy(
                        x2bf[:, f * 512:(f + 1) * 512],
                        OST[:, (g % 2) * 512:(g % 2) * 512 + 512]))

            def wout_scalar(s, tt):
                for f in range(16):
                    g = tt * 16 + f
                    s.wait_ge(S["woev"].h, g + 1)
                    S["x2s%d" % (g % 2)].inc(s.dma_start(
                        out=AP(x2d.tensor, f * 128 * TPC + tt * 512,
                               [[TPC, 128], [1, 512]]),
                        in_=OST[:, (g % 2) * 512:(g % 2) * 512 + 512]), 16)

            for tt in range(2):
                for g4 in range(4):
                    c = 2 * tt + (g4 % 2)
                    hb = (g4 // 2) * 8

                    def mk_t(c=c, hb=hb):
                        def f(t):
                            for h in range(hb, hb + 8):
                                scan_tensor(t, c, h)
                        return f

                    def mk_s(c=c, hb=hb):
                        def f(s):
                            for h in range(hb, hb + 8):
                                scan_scalar(s, c, h)
                        return f

                    def mk_v(c=c, hb=hb):
                        def f(v):
                            for h in range(hb, hb + 8):
                                if c == 1 and h == 7:
                                    p2i_vec_half(v, 1)
                                scan_vector(v, c, h)
                        return f
                    with nc.named_scope("scan"):
                        block.tensor(mk_t())
                        block.scalar(mk_s())
                        block.vector(mk_v())
                with nc.named_scope("wout"):
                    block.sync(lambda sync, tt=tt: wout_sync(sync, tt))
                    block.tensor(lambda t, tt=tt: wout_tensor(t, tt))
                    block.vector(lambda v, tt=tt: wout_vector(v, tt))
                    block.scalar(lambda s, tt=tt: wout_scalar(s, tt))
                with nc.named_scope("ln2"):
                    emit_ln2(tt)

        def emit_ln2(tt):
            def l2_tensor(t):
                base_ev = S["l2ev"].n
                t.wait_ge(S["woev"].h, (tt + 1) * 16)
                for blk in range(4):
                    if S["l2mm"].n >= 1:
                        t.wait_ge(S["l2ev"].h, S["l2mm"].n)
                    for c in range(16):
                        lhs = x2bf[:, c * 512 + blk * 128:c * 512 + blk * 128 + 128]
                        t.matmul(PB[6][:, 0:128], lhs, lhs,
                                 start=(c == 0), stop=(c == 15))
                    for c in range(16):
                        mm = t.matmul(PB[6][:, 128:129],
                                      x2bf[:, c * 512 + blk * 128:c * 512 + blk * 128 + 128],
                                      ONESB, start=False, stop=(c == 15),
                                      skip_group_check=True)
                    S["l2mm"].inc(mm, 1)
            block.tensor(l2_tensor)

            def l2_vector(v):
                base = S["l2mm"].n - 4
                for blk in range(4):
                    v.wait_ge(S["l2mm"].h, base + blk + 1)
                    VC.pre(v)
                    VC.tag(VS, v.tensor_mul(RT2[:, 192:320], PB[6][:, 0:128], IDENT_F))
                    VC.pre(v)
                    VC.tag(VS, v.reduce_sum(STAT[:, blk:blk + 1], RT2[:, 192:320],
                                            axis=AXX))
                    VC.pre(v)
                    VC.tag(S["l2ev"], v.tensor_copy(STAT[:, 8 + blk:9 + blk],
                                                    PB[6][:, 128:129]))
                VC.pre(v)
                VC.tag(VS, v.tensor_scalar_mul(STAT[:, 16:20], STAT[:, 8:12], 1.0 / E))
                VC.pre(v)
                VC.tag(VS, v.tensor_mul(STAT[:, 40:44], STAT[:, 16:20], STAT[:, 16:20]))
                VC.pre(v)
                VC.tag(VS, v.scalar_tensor_tensor(STAT[:, 48:52], STAT[:, 0:4], 1.0 / E,
                                                  STAT[:, 40:44], op0=ALU.mult,
                                                  op1=ALU.subtract))
                VC.pre(v)
                VC.tag(S["l2sc"], v.tensor_scalar_add(STAT[:, 48:52], STAT[:, 48:52], EPS))
            block.vector(l2_vector)

            def l2_scalar(s):
                s.wait_ge(S["l2sc"].h, 3 * tt + 1)
                SC.pre(s)
                SC.tag(S["l2sc"], s.activation(STAT[:, 56:60], STAT[:, 48:52], AF.Sqrt))
            block.scalar(l2_scalar)

            def l2_vector2(v):
                v.wait_ge(S["l2sc"].h, 3 * tt + 2)
                VC.pre(v)
                VC.tag(VS, v.reciprocal(STAT[:, 40:44], STAT[:, 56:60]))
                VC.pre(v)
                VC.tag(VS, v.tensor_mul(STAT[:, 40:44], STAT[:, 40:44], STAT[:, 48:52]))
                VC.pre(v)
                VC.tag(VS, v.tensor_add(STAT[:, 40:44], STAT[:, 40:44], STAT[:, 56:60]))
                VC.pre(v)
                VC.tag(VS, v.tensor_scalar_mul(STAT[:, 40:44], STAT[:, 40:44], 0.5))
                VC.pre(v)
                VC.tag(VS, v.reciprocal(STAT[:, 24:28], STAT[:, 40:44]))
                VC.pre(v)
                VC.tag(S["l2sc"], v.tensor_scalar_mul(STAT[:, 32:36], STAT[:, 16:20],
                                                      -1.0))
            block.vector(l2_vector2)

            def l2_sync(sync):
                sync.wait_ge(S["l2sc"].h, 3 * tt + 3)
                for b in range(4):
                    S["l2row"].inc(sync.dma_start(
                        out=AP(QRAW, b * 128, [[1024, 1], [1, 128]]),
                        in_=STAT[:, 32 + b:33 + b]), 16)
                    S["l2row"].inc(sync.dma_start(
                        out=AP(QRAW, 32 * 1024 + b * 128, [[1024, 1], [1, 128]]),
                        in_=STAT[:, 24 + b:25 + b]), 16)
            block.sync(l2_sync)

            def l2_tensor2(t):
                t.wait_ge(S["l2row"].h, (tt + 1) * 128)
                t.matmul(PB[4][:, 0:512], ONES_F[0:1, 0:128],
                         AP(QRAW, 0, [[1024, 1], [1, 512]]), start=True, stop=True)
                S["l2bc"].inc(t.matmul(PB[5][:, 0:512], ONES_F[32:33, 0:128],
                                       AP(QRAW, 32 * 1024, [[1024, 1], [1, 512]]),
                                       start=True, stop=True), 1)
            block.tensor(l2_tensor2)

            def l2_vector3(v):
                v.wait_ge(S["l2bc"].h, tt + 1)
                for c in range(16):
                    VC.pre(v)
                    VC.tag(VS, v.tensor_add(KTS[:, 512:1024],
                                            x2bf[:, c * 512:(c + 1) * 512],
                                            PB[4][:, 0:512]))
                    VC.pre(v)
                    last = (c == 15)
                    VC.tag(S["l2h2"] if last else VS,
                           v.tensor_mul(h2v[:, c, tt * 512:(tt + 1) * 512],
                                        KTS[:, 512:1024], PB[5][:, 0:512]))
            block.vector(l2_vector3)

        emit_p2()

        if upto == "p2":
            def dbg2_sync(sync):
                sync.wait_ge(S["l2h2"].h, 2)
                sync.wait_ge(S["x2s0"].h, 256)
                sync.wait_ge(S["x2s1"].h, 256)
                S["dbgs"].inc(sync.dma_start(out=dbg["x2"], in_=x2d), 16)
                S["dbgs"].inc(sync.dma_start(out=dbg["h2"], in_=AR2[:, :]), 16)
                S["dbgs"].inc(sync.dma_start(out=dbg["oh"], in_=AR4[:, :]), 16)
                S["dbgs"].inc(sync.dma_start(out=dbg["qT"], in_=AR1[:, 0:16384]), 16)
                S["dbgs"].inc(sync.dma_start(out=dbg["ktok"], in_=AR1[:, 16384:32768]), 16)
                S["dbgs"].inc(sync.dma_start(out=dbg["v"], in_=AR5[:, :]), 16)
                for i in range(2):
                    S["dbgs"].inc(sync.dma_start(
                        out=dbg["T"].rearrange("r p (i c) -> i r p c", i=2)[i],
                        in_=ago_T[i]), 16)
                S["dbgs"].inc(sync.dma_start(out=dbg["st"], in_=STAT[:, :]), 16)
                S["dbgs"].inc(sync.dma_start(
                    out=out[0:128, 0:1024], in_=QRAW[:, 0:1024]), 16)
                sync.wait_ge(S["dbgs"].h, 144)
            block.sync(dbg2_sync)
            ctx.close()
            return nc

        for nm in ["rtp", "rsd", "eix", "eexp", "enm", "gid", "gbk", "edg",
                   "esl", "eac", "mdq", "msum", "so0", "so1",
                   "mdn", "mtr", "mte"]:
            S[nm] = Sem(ctx.enter_context(nc.semaphore(nm)))

        def r2(a, b):
            return QRAW[:, 512 + a:512 + b]

        smw_all = QRAW[:, 832:896]
        ewd = QRAW[:, 896:904]
        ews = QRAW[:, 904:912]
        ewp = QRAW[:, 912:920]

        def p3r_gpsimd(g):
            g.wait_ge(S["womm"].h, 32)
            wv = wqf.rearrange("(c p) j -> p c j", p=128)
            S["rw"].inc(g.dma_start(out=WS[:, 0:2048], in_=wv), 16)
            keys32a = ATT[0:64, 0:128].bitcast(F32)
            keys32b = ATT[64:128, 0:128].bitcast(F32)
            S["rw"].inc(g.dma_start(out=keys32a, in_=keysf), 16)
            S["rw"].inc(g.dma_start(out=keys32b, in_=keysf), 16)
        with nc.named_scope("p3r"):
            block.gpsimd(p3r_gpsimd)

        def p3r_tensor(t):
            t.wait_ge(S["l2h2"].h, 2)
            t.wait_ge(S["rw"].h, 48)
            for tt in range(2):
                if tt == 1:
                    t.wait_ge(S["rev"].h, 1)
                for c in range(16):
                    mm = t.matmul(PB[0][:, 0:512], WS[:, c * 128:(c + 1) * 128],
                                  h2v[:, c, tt * 512:(tt + 1) * 512],
                                  start=(c == 0), stop=(c == 15))
                S["rmm"].inc(mm, 1)
            t.wait_ge(S["rev"].h, 2)
            for sb in range(8):
                for j in range(2):
                    gidx = sb * 2 + j
                    if gidx >= 1:
                        t.wait_ge(S["rtp"].h, gidx)
                    keys32 = (ATT[0:64, 0:128] if j == 0
                              else ATT[64:128, 0:128]).bitcast(F32)
                    mm = t.matmul(PB[1][:, 0:64],
                                  proj[j * 64:(j + 1) * 64, sb * 128:(sb + 1) * 128],
                                  keys32,
                                  start=True, stop=True)
                    S["rmm"].inc(mm, 1)
        with nc.named_scope("p3r"):
            block.tensor(p3r_tensor)

        def p3r_scalar(s):
            for tt in range(2):
                s.wait_ge(S["rmm"].h, tt + 1)
                SC.pre(s)
                SC.tag(S["rev"], s.activation(proj[:, tt * 512:(tt + 1) * 512],
                                              PB[0][:, 0:512], AF.Copy))
        with nc.named_scope("p3r"):
            block.scalar(p3r_scalar)

        def p3r_vector(v):
            for sb in range(8):
                for j in range(2):
                    gidx = sb * 2 + j
                    v.wait_ge(S["rmm"].h, 2 + gidx + 1)
                    VC.pre(v)
                    VC.tag(VS, v.tensor_copy(r2(0, 64), PB[1][:, 0:64]))
                    VC.pre(v)
                    VC.tag(VS, v.max(r2(64, 72), r2(0, 64)))
                    VC.pre(v)
                    VC.tag(VS, v.max_index(RTI[:, 0:8], r2(64, 72), r2(0, 64)))
                    VC.pre(v)
                    VC.tag(VS, v.tensor_copy(
                        QRAW[:, j * 128 + sb * 16:j * 128 + sb * 16 + 8],
                        r2(64, 72)))
                    VC.pre(v)
                    VC.tag(S["rtp"], v.tensor_copy(
                        QRAW[:, j * 128 + sb * 16 + 8:j * 128 + sb * 16 + 16],
                        RTI[:, 0:8]))
        with nc.named_scope("p3r"):
            block.vector(p3r_vector)

        def p3r_gpsimd2(g):
            g.wait_ge(S["rtp"].h, 16)
            S["rpk"].inc(g.dma_start(out=agi_rt, in_=QRAW[:, 0:256]), 16)
            g.wait_ge(S["rpk"].h, 16)
            S["cc1"].inc(g.collective_compute(
                "AllGather", ALU.bypass, replica_groups=RG,
                ins=[agi_rt.opt()], outs=[ago_rt.opt()]), 1)
            g.wait_ge(S["cc1"].h, 13)
            g.wait_ge(S["x2s0"].h, 256)
            g.wait_ge(S["x2s1"].h, 256)
            for i in range(16):
                r = (i % 8) // 2 + (4 if i >= 8 else 0)
                hh = i % 2
                src = AP(ago_rt.tensor, r * 128 * 256 + hh * 64,
                         [[256, 128], [128, 2], [1, 64]])
                S["rcl"].inc(g.dma_start(out=CSB[:, i * 128:(i + 1) * 128],
                                         in_=src), 16)
        with nc.named_scope("p3r"):
            block.gpsimd(p3r_gpsimd2)

        def p3c_vec_half(v, half):
            v.wait_ge(S["rcl"].h, 256)
            for i in range(half * 8, half * 8 + 8):
                dst = QRAW[:, 0:128] if i < 8 else QRAW[:, 128:256]
                sel = rtselsb[:, i:i + 1]
                VC.pre(v)
                if i % 8 == 0:
                    VC.tag(VS, v.tensor_scalar_mul(dst, CSB[:, i * 128:(i + 1) * 128],
                                                   sel))
                else:
                    tag = S["rsh"] if (i % 8 == 7) else VS
                    VC.tag(tag, v.scalar_tensor_tensor(
                        dst, CSB[:, i * 128:(i + 1) * 128], sel, dst,
                        op0=ALU.mult, op1=ALU.add))

        def p3c_gpsimd2(g):
            g.wait_ge(S["rsh"].h, 2)
            for qy in range(2):
                for j in range(2):
                    for tbp in range(2):
                        src = AP(QRAW, (64 * tbp) * 1024 + qy * 128 + j * 64,
                                 [[1024, 64], [16, 4], [1, 16]])
                        dst = AP(QRAW, j * 1024 + 256 + qy * 128 + 16 * tbp,
                                 [[2048, 64], [32, 4], [1, 16]])
                        S["rsd"].inc(g.dma_start(out=dst, in_=src), 16)
        with nc.named_scope("p3c"):
            block.gpsimd(p3c_gpsimd2)

        def p3a_vec_tb(v, tb):
            if tb == 0:
                v.wait_ge(S["rsd"].h, 128)
            m2 = AP(QRAW, 512 + 192, [[1024, 128], [8, 8], [1, 8]])
            if True:
                sxs = AP(QRAW, 256 + tb * 16, [[1024, 128], [1, 8], [0, 8]])
                sys_ = AP(QRAW, 384 + tb * 16, [[1024, 128], [0, 8], [1, 8]])
                VC.pre(v)
                VC.tag(VS, v.tensor_tensor(r2(64, 128), sxs, sys_, op=ALU.add))
                VC.pre(v)
                VC.tag(VS, v.max(r2(128, 136), r2(64, 128)))
                VC.pre(v)
                VC.tag(VS, v.max_index(RTI[:, 0:8], r2(128, 136), r2(64, 128)))
                VC.pre(v)
                VC.tag(VS, v.tensor_copy(r2(136, 144), RTI[:, 0:8]))
                VC.pre(v)
                VC.tag(VS, v.reduce_max(r2(144, 145), r2(128, 136), axis=AXX))
                VC.pre(v)
                VC.tag(S["enm"], v.tensor_scalar_mul(r2(145, 146), r2(144, 145),
                                                     -1.0))
                v.wait_ge(S["eexp"].h, tb + 1)
                VC.pre(v)
                VC.tag(VS, v.reciprocal(r2(161, 162), r2(160, 161)))
                VC.pre(v)
                VC.tag(VS, v.tensor_scalar_mul(smw_all[:, tb * 8:(tb + 1) * 8],
                                               r2(152, 160), r2(161, 162)))
                posv = AP(QRAW, 512 + 136, [[1024, 128], [1, 8], [0, 8]])
                c8pv = AP(CF, 912, [[1024, 128], [0, 8], [1, 8]])
                VC.pre(v)
                VC.tag(VS, v.tensor_tensor(r2(192, 256), posv, c8pv, op=ALU.is_ge))
                VC.pre(v)
                VC.tag(VS, v.reduce_sum(r2(170, 178), m2, axis=AXX))
                VC.pre(v)
                VC.tag(VS, v.scalar_tensor_tensor(r2(162, 170), r2(170, 178),
                                                  -8.0, r2(136, 144),
                                                  op0=ALU.mult, op1=ALU.add))
                pdv = AP(QRAW, 512 + 170, [[1024, 128], [1, 8], [0, 8]])
                c1v = AP(CF, 904, [[1024, 128], [0, 8], [1, 8]])
                ixv = AP(QRAW, 256 + tb * 16 + 8, [[1024, 128], [0, 8], [1, 8]])
                VC.pre(v)
                VC.tag(VS, v.tensor_tensor(r2(192, 256), pdv, c1v, op=ALU.is_equal))
                VC.pre(v)
                VC.tag(VS, v.tensor_tensor(r2(192, 256), m2, ixv, op=ALU.mult))
                VC.pre(v)
                VC.tag(VS, v.reduce_sum(r2(178, 186), m2, axis=AXX))
                pmv = AP(QRAW, 512 + 162, [[1024, 128], [1, 8], [0, 8]])
                iyv = AP(QRAW, 384 + tb * 16 + 8, [[1024, 128], [0, 8], [1, 8]])
                VC.pre(v)
                VC.tag(VS, v.tensor_tensor(r2(192, 256), pmv, c1v, op=ALU.is_equal))
                VC.pre(v)
                VC.tag(VS, v.tensor_tensor(r2(192, 256), m2, iyv, op=ALU.mult))
                VC.pre(v)
                VC.tag(VS, v.reduce_sum(r2(186, 194), m2, axis=AXX))
                VC.pre(v)
                VC.tag(S["eix"], v.scalar_tensor_tensor(
                    r2(208, 216), r2(178, 186), 64.0, r2(186, 194),
                    op0=ALU.mult, op1=ALU.add))
                v.wait_ge(S["gid"].h, (tb + 1) * 16)

        def p3a_scl_tb(s, tb):
            s.wait_ge(S["enm"].h, tb + 1)
            SC.pre(s)
            SC.tag(S["eexp"], s.activation(r2(152, 160), r2(128, 136), AF.Exp,
                                           bias=r2(145, 146),
                                           accum_out=r2(160, 161)))

        def p3a_gpsimd(g):
            for tb in range(8):
                g.wait_ge(S["eix"].h, tb + 1)
                S["gid"].inc(g.dma_start(
                    out=AP(eidxd.tensor, tb * 1024, [[8, 128], [1, 8]]),
                    in_=r2(208, 216)), 16)
                g.wait_ge(S["gid"].h, (tb + 1) * 16)
        with nc.named_scope("p3a"):
            block.gpsimd(p3a_gpsimd)

        gate16v = [ago["gate_a"].rearrange("(a p) j -> a p j", p=128),
                   ago["gate_b"].rearrange("(a p) j -> a p j", p=128)]
        up16v = [ago["up_a"].rearrange("(a p) j -> a p j", p=128),
                 ago["up_b"].rearrange("(a p) j -> a p j", p=128)]
        down16v = ago["down"].rearrange("(a p) j -> a p j", p=128)

        S["mgt"] = S["womm"]
        MGTB = S["mgt"].n
        GAUB = [S["gau0"].n, S["gau1"].n]
        S["md2"], S["md3"] = S["wo0"], S["x2s0"]
        MDB = [S["md0"].n, S["md1"].n, S["md2"].n, S["md3"].n]

        def exp_gpsimd(g, tb):
            if tb >= 2:
                g.wait_ge(S["gat0"].h, (tb - 1) * 4 * 16)
                g.wait_ge(S["gat1"].h, (tb - 1) * 4 * 16)
                g.wait_ge(S["gau0"].h, GAUB[0] + (tb - 1) * 4 * 16)
                g.wait_ge(S["gau1"].h, GAUB[1] + (tb - 1) * 4 * 16)
            if tb == 0:
                g.wait_ge(S["idxz"].h, 1)
            src = AP(eidxd.tensor, tb * 1024, [[8, 16], [1, 8], [128, 8]])
            for pr in range(4):
                S["gbk"].inc(g.dma_start(
                    out=IDX[pr * 16:(pr + 1) * 16,
                            (tb % 2) * 64:(tb % 2) * 64 + 64], in_=src,
                    allow_slow_non_contiguous=True), 16)
            g.wait_ge(S["gbk"].h, S["gbk"].n)
            for k in range(8):
                gs = tb * 8 + k
                if gs >= 2:
                    g.wait_ge(S["emm"].h, gs - 1)
                    g.wait_ge(S["gat%d" % (gs % 2)].h, (gs // 2) * 16)
                S["gat%d" % (gs % 2)].inc(g.dma_gather(
                    deT[gs % 2].rearrange("p (a c) -> p a c", a=16),
                    ago["de"],
                    IDX[:, (tb % 2) * 64 + k * 8:(tb % 2) * 64 + k * 8 + 8],
                    128, 128, E, elem_step=E, transpose=True, queue_num=0), 16)
                if gs >= 2:
                    g.wait_ge(S["eac"].h, gs - 1)
                    g.wait_ge(S["gau%d" % (gs % 2)].h, GAUB[gs % 2] + (gs // 2) * 16)
                S["gau%d" % (gs % 2)].inc(g.dma_gather(
                    ueg[gs % 2].rearrange("p (a c) -> p a c", a=1),
                    ago["ue"],
                    IDX[:, (tb % 2) * 64 + k * 8:(tb % 2) * 64 + k * 8 + 8],
                    128, 128, E, elem_step=E, queue_num=1), 16)

        def mlp_sync(sync, tt):
            for ib in range(4):
                for f2 in range(16):
                    g2 = tt * 64 + ib * 16 + f2
                    if g2 == 0:
                        sync.wait_ge(S["rmm"].h, 2)
                        sync.wait_ge(S["cc1"].h, 5)
                    if g2 >= 2:
                        sync.wait_ge(S["mmm"].h, g2 - 1)
                    io = ib * 16 + f2
                    if io == 32 and tt == 0:
                        sync.wait_ge(S["cc1"].h, 12)
                    S["mg%d" % (g2 % 2)].inc(sync.dma_start(
                        out=WS[:, (g2 % 2) * 2048:(g2 % 2) * 2048 + 2048],
                        in_=gate16v[io // 32][:, :, (io % 32) * 128:
                                              (io % 32 + 1) * 128].rearrange(
                            "a p j -> p a j")), 16)
                    if g2 in (64, 65):
                        sync.wait_ge(S["mdq"].h, 64)
                    S["mu%d" % (g2 % 2)].inc(sync.dma_start(
                        out=WS[:, 4096 + (g2 % 2) * 2048:4096 + (g2 % 2) * 2048 + 2048],
                        in_=up16v[io // 32][:, :, (io % 32) * 128:
                                            (io % 32 + 1) * 128].rearrange(
                            "a p j -> p a j")), 16)
            for f in range(16):
                fd = tt * 16 + f
                if f >= 4 and (fd - 4) % 2 == 1:
                    sync.wait_ge(S["mte"].h, fd - 3)
                    S["so1"].inc(sync.dma_start(
                        out=AP(out.tensor, (tt * 512) * E + (f - 4) * 128,
                               [[E, 128], [128 * E, 4], [1, 128]]),
                        in_=RBC[:, ((fd - 4) % 2) * 512:((fd - 4) % 2) * 512 + 512]),
                        16)
                for qq in (0, 2):
                    qg = fd * 4 + qq
                    sl = qg % 4
                    if qg == 0:
                        sync.wait_ge(S["cc1"].h, 8)
                    if qg < tt * 64 + 4:
                        sync.wait_ge(S["mmm"].h, (tt + 1) * 64)
                    if qg >= 4:
                        sync.wait_ge(S["mdq"].h, qg - 3)
                        sync.wait_ge(S["md%d" % sl].h, MDB[sl] + (qg // 4) * 16)
                    S["md%d" % sl].inc(sync.dma_start(
                        out=WS[:, 4096 + sl * 2048:4096 + sl * 2048 + 2048],
                        in_=down16v[qq * 16:(qq + 1) * 16, :,
                                    f * 128:(f + 1) * 128].rearrange(
                            "a p j -> p a j")), 16)
            for f in (13, 15):
                fd = tt * 16 + f
                sync.wait_ge(S["mte"].h, fd + 1)
                S["so1"].inc(sync.dma_start(
                    out=AP(out.tensor, (tt * 512) * E + f * 128,
                           [[E, 128], [128 * E, 4], [1, 128]]),
                    in_=RBC[:, 512:1024]), 16)

        EXPB = [25, 2]

        def mlp_tensor(t, tt):
            if tt == 0:
                t.wait_ge(S["rmm"].h, 18)
                t.wait_ge(S["rtp"].h, 16)
            for ib in range(4):
                for f2 in range(16):
                    g2 = tt * 64 + ib * 16 + f2
                    io = ib * 16 + f2
                    t.wait_ge(S["mg%d" % (g2 % 2)].h, (g2 // 2 + 1) * 16)
                    if g2 >= 1:
                        t.wait_ge(S["msil"].h, g2)
                    for c in range(16):
                        mmg = t.matmul(PB[0][:, 0:512],
                                       WS[:, (g2 % 2) * 2048 + c * 128:
                                          (g2 % 2) * 2048 + (c + 1) * 128],
                                       h2v[:, c, tt * 512:(tt + 1) * 512],
                                       start=(c == 0), stop=(c == 15))
                    S["mgt"].inc(mmg, 1)
                    if EXPB[tt] <= io < EXPB[tt] + 32:
                        e = io - EXPB[tt]
                        tb = 4 * tt + e // 8
                        k = e % 8
                        gs = tb * 8 + k
                        t.wait_ge(S["gat%d" % (gs % 2)].h, (gs // 2 + 1) * 16)
                        if gs >= 2:
                            t.wait_ge(S["edg"].h, gs - 1)
                        for f in range(16):
                            mm = t.matmul(PB[2 + gs % 2][:, 0:128],
                                          h2v[:, f, tb * 128:(tb + 1) * 128],
                                          deT[gs % 2][:, f * 128:(f + 1) * 128],
                                          start=(f == 0), stop=(f == 15))
                        S["emm"].inc(mm, 1)
                    t.wait_ge(S["mu%d" % (g2 % 2)].h, (g2 // 2 + 1) * 16)
                    if g2 >= 1:
                        t.wait_ge(S["mev"].h, g2)
                    for c in range(16):
                        mm = t.matmul(PB[1][:, 0:512],
                                      WS[:, 4096 + (g2 % 2) * 2048 + c * 128:
                                         4096 + (g2 % 2) * 2048 + (c + 1) * 128],
                                      h2v[:, c, tt * 512:(tt + 1) * 512],
                                      start=(c == 0), stop=(c == 15))
                    S["mmm"].inc(mm, 1)
            t.wait_ge(S["eac"].h, (tt * 4 + 4) * 8)
            for f in range(16):
                fd = tt * 16 + f
                t.wait_ge(S["mev"].h, tt * 64 + 64)
                if fd >= 1:
                    t.wait_ge(S["msum"].h, fd)
                first = True
                for qq in range(4):
                    qg = fd * 4 + qq
                    t.wait_ge(S["md%d" % (qg % 4)].h,
                              MDB[qg % 4] + (qg // 4 + 1) * 16)
                    for c in range(16):
                        io = qq * 16 + c
                        mm = t.matmul(PB[4 + fd % 2][:, 0:512],
                                      WS[:, 4096 + (qg % 4) * 2048 + c * 128:
                                         4096 + (qg % 4) * 2048 + (c + 1) * 128],
                                      AR1[:, io * 512:(io + 1) * 512],
                                      start=first, stop=False,
                                      skip_group_check=True)
                        first = False
                    S["mdq"].inc(mm, 1)
                for mt in range(4):
                    mm = t.matmul(PB[4 + fd % 2][:, mt * 128:(mt + 1) * 128],
                                  exp_tok[:, mt * 2048 + f * 128:mt * 2048 + (f + 1) * 128],
                                  IDENT_B, start=False, stop=(mt == 3),
                                  skip_group_check=True)
                S["mdn"].inc(mm, 1)
                t.wait_ge(S["msum"].h, fd + 1)
                if fd >= 1:
                    t.wait_ge(S["mte"].h, fd)
                for mt in range(4):
                    mm = t.matmul(PB[6][:, mt * 128:(mt + 1) * 128],
                                  OST[:, (fd % 2) * 512 + mt * 128:
                                      (fd % 2) * 512 + (mt + 1) * 128],
                                  IDENT_F, start=(mt == 0), stop=(mt == 3))
                S["mtr"].inc(mm, 1)

        def mlp_scalar(s, tt):
            for ib in range(4):
                for f2 in range(16):
                    g2 = tt * 64 + ib * 16 + f2
                    io = ib * 16 + f2
                    s.wait_ge(S["mgt"].h, MGTB + g2 + 1)
                    if g2 >= 1:
                        s.wait_ge(S["mev"].h, g2)
                    SC.pre(s)
                    SC.tag(S["msil"], s.activation(KTS[:, 512:1024], PB[0][:, 0:512],
                                                   AF.Silu))
                    if tt == 0 and 14 <= io < 22:
                        p3a_scl_tb(s, io - 14)
                    if EXPB[tt] <= io < EXPB[tt] + 32:
                        e = io - EXPB[tt]
                        gs = 32 * tt + e
                        k = e % 8
                        s.wait_ge(S["edg"].h, gs + 1)
                        SC.pre(s)
                        SC.tag(S["esl"], s.activation(ews[:, k:k + 1],
                                                      ewd[:, k:k + 1], AF.Silu))
            for f in range(16):
                fd = tt * 16 + f
                for qq in (1, 3):
                    qg = fd * 4 + qq
                    sl = qg % 4
                    if qg == 1:
                        s.wait_ge(S["cc1"].h, 8)
                    if qg < tt * 64 + 4:
                        s.wait_ge(S["mmm"].h, (tt + 1) * 64)
                    if qg >= 4:
                        s.wait_ge(S["mdq"].h, qg - 3)
                        s.wait_ge(S["md%d" % sl].h, MDB[sl] + (qg // 4) * 16)
                    S["md%d" % sl].inc(s.dma_start(
                        out=WS[:, 4096 + sl * 2048:4096 + sl * 2048 + 2048],
                        in_=down16v[qq * 16:(qq + 1) * 16, :,
                                    f * 128:(f + 1) * 128].rearrange(
                            "a p j -> p a j")), 16)
                if fd >= 2:
                    s.wait_ge(S["msum"].h, fd - 1)
                S["x2l%d" % (fd % 2)].inc(s.dma_start(
                    out=XST[:, (fd % 2) * 512:(fd % 2) * 512 + 512],
                    in_=AP(x2d.tensor, f * 128 * TPC + tt * 512,
                           [[TPC, 128], [1, 512]])), 16)
                if fd % 2 == 0:
                    s.wait_ge(S["mte"].h, fd + 1)
                    S["so0"].inc(s.dma_start(
                        out=AP(out.tensor, (tt * 512) * E + f * 128,
                               [[E, 128], [128 * E, 4], [1, 128]]),
                        in_=RBC[:, 0:512]), 16)

        def _exp_acc(v, tt, d):
            tb = 4 * tt + d // 8
            kd = d % 8
            gd = tb * 8 + kd
            v.wait_ge(S["esl"].h, gd + 1)
            VC.pre(v)
            VC.tag(VS, v.tensor_mul(ewp[:, kd:kd + 1], ews[:, kd:kd + 1],
                                    smw_all[:, tb * 8 + kd:tb * 8 + kd + 1]))
            v.wait_ge(S["gau%d" % (gd % 2)].h, GAUB[gd % 2] + (gd // 2 + 1) * 16)
            slot = exp_tok[:, (tb % 4) * 2048:(tb % 4 + 1) * 2048]
            VC.pre(v)
            if kd == 0:
                VC.tag(S["eac"], v.tensor_scalar_mul(
                    slot, ueg[gd % 2][:, 0:2048], ewp[:, 0:1]))
            else:
                VC.tag(S["eac"], v.scalar_tensor_tensor(
                    slot, ueg[gd % 2][:, 0:2048], ewp[:, kd:kd + 1], slot,
                    op0=ALU.mult, op1=ALU.add))

        def mlp_vector(v, tt):
            for ib in range(4):
                for f2 in range(16):
                    g2 = tt * 64 + ib * 16 + f2
                    io = ib * 16 + f2
                    v.wait_ge(S["mmm"].h, g2 + 1)
                    v.wait_ge(S["msil"].h, g2 + 1)
                    VC.pre(v)
                    VC.tag(S["mev"], v.tensor_mul(AR1[:, io * 512:(io + 1) * 512],
                                                  PB[1][:, 0:512], KTS[:, 512:1024]))
                    if tt == 0 and io in (12, 13):
                        p3c_vec_half(v, io - 12)
                    if tt == 0 and 14 <= io < 22:
                        p3a_vec_tb(v, io - 14)
                    if EXPB[tt] <= io < EXPB[tt] + 32:
                        e = io - EXPB[tt]
                        tb = 4 * tt + e // 8
                        k = e % 8
                        gs = tb * 8 + k
                        v.wait_ge(S["emm"].h, gs + 1)
                        VC.pre(v)
                        VC.tag(VS, v.tensor_mul(r2(192, 320),
                                                PB[2 + gs % 2][:, 0:128], IDENT_F))
                        VC.pre(v)
                        VC.tag(S["edg"], v.reduce_sum(ewd[:, k:k + 1],
                                                      r2(192, 320), axis=AXX))
                        if e >= 1:
                            _exp_acc(v, tt, e - 1)
                        if e == 31:
                            _exp_acc(v, tt, 31)
            for f in range(16):
                fd = tt * 16 + f
                v.wait_ge(S["mdn"].h, fd + 1)
                v.wait_ge(S["x2l%d" % (fd % 2)].h, (fd // 2 + 1) * 16)
                VC.pre(v)
                VC.tag(S["msum"], v.tensor_add(
                    OST[:, (fd % 2) * 512:(fd % 2) * 512 + 512],
                    PB[4 + fd % 2][:, 0:512],
                    XST[:, (fd % 2) * 512:(fd % 2) * 512 + 512]))
                v.wait_ge(S["mtr"].h, fd + 1)
                if fd >= 2:
                    v.wait_ge(S["so%d" % (fd % 2)].h, (fd // 2) * 16)
                VC.pre(v)
                VC.tag(S["mte"], v.tensor_copy(
                    RBC[:, (fd % 2) * 512:(fd % 2) * 512 + 512], PB[6][:, 0:512]))

        for tt in range(2):
            with nc.named_scope("experts"):
                for tb in range(4 * tt, 4 * tt + 4):
                    block.gpsimd(lambda g, tb=tb: exp_gpsimd(g, tb))
            with nc.named_scope("mlp"):
                block.sync(lambda sync, tt=tt: mlp_sync(sync, tt))
                block.tensor(lambda t, tt=tt: mlp_tensor(t, tt))
                block.scalar(lambda s, tt=tt: mlp_scalar(s, tt))
                block.vector(lambda v, tt=tt: mlp_vector(v, tt))

        def p3_final(s):
            s.wait_ge(S["so0"].h, 256)
            s.wait_ge(S["so1"].h, 256)
        block.scalar(p3_final)

    ctx.close()
    return nc


import ml_dtypes
BF = ml_dtypes.bfloat16


def make_consts():
    cf = np.zeros((128, 1024), np.float32)
    cf[:, 0:128] = np.eye(128, dtype=np.float32)
    P = np.zeros((128, 128), np.float32)
    for i in range(64):
        P[2 * i, 2 * i + 1] = -1.0
        P[2 * i + 1, 2 * i] = 1.0
    cf[:, 128:256] = P.T
    p = np.arange(128)[:, None]
    l = np.arange(256)[None, :]
    cf[:, 256:512] = (p <= l).astype(np.float32)
    cf[:, 512:768] = (p + 128 <= l).astype(np.float32)
    cf[:, 768:896] = 1.0
    cf[:, 896:904] = (np.arange(8) * 8.0)[None, :]
    cf[:, 904:912] = np.arange(8.0)[None, :]
    cf[:, 912:920] = (np.arange(8.0)[None, :] + 1) * 8.0
    cb = np.zeros((128, 132), BF)
    cb[:, 0:128] = np.eye(128, dtype=np.float32).astype(BF)
    cb[:, 128:129] = np.float32(1.0)
    return cf, cb


def rope_tables(s0):
    inv = 1.0 / 10000.0 ** (np.arange(0, D, 2, dtype=np.float32) / D)
    invf = np.concatenate([inv, inv])
    pos = np.arange(s0, s0 + TPC, dtype=np.float32)
    ang = invf[:, None] * pos[None, :]
    cs = np.zeros((128, 2048), np.float32)
    cs[:, 0:1024] = np.cos(ang)
    cs[:, 1024:2048] = np.sin(ang)
    return cs


def stage_inputs(inp):
    x = np.asarray(inp["L0_attn_x"], np.float32)
    Wqkv = np.asarray(inp["L0_attn_W_qkv"], np.float32)
    Wout = np.asarray(inp["L0_attn_W_out"], np.float32)
    Wq = np.asarray(inp["L0_ffn_W_q"], np.float32)
    keys = np.asarray(inp["L0_ffn_keys"], np.float32)
    de = np.asarray(inp["L0_ffn_down_embed"], np.float32)
    ue = np.asarray(inp["L0_ffn_up_embed"], np.float32)
    gw = np.asarray(inp["L0_ffn_gate_w"], np.float32)
    uw = np.asarray(inp["L0_ffn_up_w"], np.float32)
    dw = np.asarray(inp["L0_ffn_down_w"], np.float32)

    wqkvT = np.ascontiguousarray(Wqkv.T)
    woutT = np.ascontiguousarray(Wout.T)
    cf, cb = make_consts()

    in_maps = []
    for c in range(8):
        b, q = c // 4, c % 4
        s0 = TPC * q
        xt = np.ascontiguousarray(x[b, s0:s0 + TPC].T)
        psel = np.zeros((128, 8), np.float32)
        for r in range(8):
            if r // 4 == b and r % 4 < q:
                psel[:, r] = 1.0
        rtsel = np.zeros((128, 16), np.float32)
        r0 = 2 * b + q // 2
        hsel = q % 2
        rtsel[:, 2 * r0 + hsel] = 1.0
        rtsel[:, 8 + 2 * r0 + hsel] = 1.0
        m = {
            "xT": xt,
            "cs": rope_tables(s0),
            "cf": cf,
            "cbi": cb,
            "wqf": Wq,
            "keysf": np.ascontiguousarray(keys[b]),
            "pself": psel,
            "rtself": rtsel,
            "sh_wqkvT": np.ascontiguousarray(wqkvT[c * 256:(c + 1) * 256]),
            "sh_woutT": np.ascontiguousarray(woutT[c * 256:(c + 1) * 256]),
            "sh_gate": np.ascontiguousarray(gw[c * 256:(c + 1) * 256]),
            "sh_up": np.ascontiguousarray(uw[c * 256:(c + 1) * 256]),
            "sh_down": np.ascontiguousarray(dw[c * 1024:(c + 1) * 1024]),
            "sh_de": np.ascontiguousarray(de[c * 512:(c + 1) * 512]),
            "sh_ue": np.ascontiguousarray(ue[c * 512:(c + 1) * 512]),
        }
        in_maps.append(m)
    return in_maps


from concourse.bass_utils import run_bass_kernel_spmd

_NC_CACHE = {}


def kernel(**inputs):
    in_maps = stage_inputs(inputs)
    if "nc" not in _NC_CACHE:
        nc = build({"upto": "full"})
        nc.finalize()
        _NC_CACHE["nc"] = nc
    nc = _NC_CACHE["nc"]
    res = run_bass_kernel_spmd(nc, in_maps, core_ids=list(range(8)))
    full = np.zeros((2, 4096, 2048), np.float32)
    for c in range(8):
        b, q = c // 4, c % 4
        full[b, q * 1024:(q + 1) * 1024] = np.asarray(res.results[c]["out"])
    return full
